# version 57
# baseline (speedup 1.0000x reference)
import math
import numpy as np
import ml_dtypes
from contextlib import ExitStack

import concourse.bass as bass
import concourse.mybir as mybir
from concourse.bass_utils import run_bass_kernel_spmd

F32 = mybir.dt.float32
BF16 = mybir.dt.bfloat16
I32 = mybir.dt.int32
U8 = mybir.dt.uint8
AF = mybir.ActivationFunctionType
ALU = mybir.AluOpType
AX = mybir.AxisListType

S = 2048
D = 2048
M = 16384
KBITS = 0x3F000000
C0 = 0.7978845608028654
SQC = math.sqrt(C0 * 0.044715)


class Sem:
    def __init__(self, nc, name):
        self.h = nc.alloc_semaphore(name)
        self.v = 0

    def inc(self, instr, n=1):
        instr.then_inc(self.h, n)
        self.v += n


def _consts():
    i = np.arange(128)
    ang = 2.0 * np.pi * (np.outer(i, i) % 128) / 128.0
    bf = ml_dtypes.bfloat16
    return {
        "ident": np.eye(128, dtype=np.float32).astype(bf),
        "cf": np.cos(ang).astype(np.float32).astype(bf),
        "sf": (-np.sin(ang)).astype(np.float32).astype(bf),
        "ci": (np.cos(ang) / 128.0).astype(np.float32).astype(bf),
        "si": (-np.sin(ang) / 128.0).astype(np.float32).astype(bf),
        "tri": np.triu(np.ones((128, 128), np.float32)).astype(bf),
        "ones": np.ones((128, 128), np.float32).astype(bf),
    }


def build_nc(debug=False, kv_raw=False):
    nc = bass.Bass("TRN2", target_bir_lowering=False)
    uid_ctr = [0]

    x_in = nc.dram_tensor("x", [S, D], F32, kind="ExternalInput").ap()
    wqT = nc.dram_tensor("wqT", [D, D], F32, kind="ExternalInput").ap()
    wkvT = nc.dram_tensor("wkvT", [D, D], F32, kind="ExternalInput").ap()
    woT = nc.dram_tensor("woT", [D, D], F32, kind="ExternalInput").ap()
    gpk = nc.dram_tensor("gpk", [1024, 2048], U8, kind="ExternalInput").ap()
    upk = nc.dram_tensor("upk", [1024, 2048], U8, kind="ExternalInput").ap()
    dpk = nc.dram_tensor("dpk", [1024, 2048], U8, kind="ExternalInput").ap()
    gsc = nc.dram_tensor("gsc", [64, 2048], F32, kind="ExternalInput").ap()
    usc = nc.dram_tensor("usc", [64, 2048], F32, kind="ExternalInput").ap()
    dsc = nc.dram_tensor("dsc", [64, 2048], F32, kind="ExternalInput").ap()
    y_out = nc.dram_tensor("y", [S, D], F32, kind="ExternalOutput").ap()

    def scratch(name, shape, dtype):
        kind = "ExternalOutput" if debug else "Internal"
        return nc.dram_tensor(name, shape, dtype, kind=kind).ap()

    wg_s = nc.dram_tensor("wg_s", [64, 128, 16, 256], BF16, kind="Internal",
                          addr_space="Shared").ap()
    wu_s = nc.dram_tensor("wu_s", [64, 128, 16, 256], BF16, kind="Internal",
                          addr_space="Shared").ap()
    wd_s = nc.dram_tensor("wd_s", [32, 128, 16, 512], BF16, kind="Internal",
                          addr_space="Shared").ap()
    wg_loc = nc.dram_tensor("wg_loc", [8, 128, 16, 256], BF16, kind="Internal").ap()
    wu_loc = nc.dram_tensor("wu_loc", [8, 128, 16, 256], BF16, kind="Internal").ap()
    wd_loc = nc.dram_tensor("wd_loc", [4, 128, 16, 512], BF16, kind="Internal").ap()
    wsac = nc.dram_tensor("wsac", [8, 128, 4, 256], BF16, kind="Internal").ap()
    inter_s = scratch("inter_s", [4, 8, 128, 16, 512], BF16)
    x1_s = scratch("x1_s", [S, D], F32)
    x2_s = scratch("x2_s", [S, D], F32)
    qnT_s = scratch("qnT_s", [D, S], BF16)
    xt_dbg = scratch("xt_dbg", [128, 32768], BF16) if debug else None
    kvn_s = scratch("kvn_s", [4, 128, 16, 512], BF16)
    q_raw_s = nc.dram_tensor("q_raw_s", [S, D], BF16, kind="Internal").ap()
    kv_raw_s = nc.dram_tensor("kv_raw_s", [S, D], BF16, kind="Internal").ap()
    kvcT_s = scratch("kvcT_s", [D, S], BF16)
    vals_s = scratch("vals_s", [4, 128, 16, 512], BF16)

    cns = _consts()
    c_aps = {k: nc.inline_tensor(v, f"c_{k}").ap() for k, v in cns.items()}

    with ExitStack() as top:
        sbt = lambda n, s, d: top.enter_context(nc.sbuf_tensor(n, s, d))
        kc = {k: sbt(f"k_{k}", [128, 128], BF16) for k in cns}
        k_eps = sbt("k_eps", [128, 1], F32)
        sa_psum = sbt("sa_psum", [128, 64], F32)
        sa_psq = sbt("sa_psq", [128, 64], F32)
        sa_scr = sbt("sa_scr", [128, 512], F32)
        nc.const_aps.register(F32, 1e-5, k_eps[:, 0:1])
        p = [top.enter_context(nc.psum_tensor(f"pb{i}", [128, 512], F32))
             for i in range(8)]

        s_ld = Sem(nc, "s_ld")
        s2_st = Sem(nc, "s2_st")
        g_ld = Sem(nc, "g_ld")
        q_ld = Sem(nc, "q_ld")
        q_st = Sem(nc, "q_st")
        v_a = Sem(nc, "v_a")
        v_b = Sem(nc, "v_b")
        s_a = Sem(nc, "s_a")
        mm = Sem(nc, "mm")
        t_tp = Sem(nc, "t_tp")
        st_g = Sem(nc, "st_g")
        st_u = Sem(nc, "st_u")
        i_ld = Sem(nc, "i_ld")

        all_sems = [s_ld, s2_st, g_ld, q_ld, q_st, v_a, v_b, s_a, mm, t_tp,
                    st_g, st_u, i_ld]
        with nc.Block() as block:
            @block.gpsimd
            def _(gp):
                for sm in all_sems:
                    gp.sem_clear(sm.h)

        with nc.Block() as block:
            @block.sync
            def _(sync):
                for k in cns:
                    ins = sync.dma_start(out=kc[k][:], in_=c_aps[k])
                    s_ld.inc(ins, 16)

            @block.vector
            def _(v):
                v.memset(k_eps[:], 1e-5)
                v.wait_ge(s_ld.h, s_ld.v)

        def emit_dequant():
            with ExitStack() as es:
                uid_ctr[0] += 1
                _u = uid_ctr[0]
                sb = lambda n, s, d: es.enter_context(
                    nc.sbuf_tensor(f"{n}_{_u}", s, d))
                wp = [sb(f"dq_wp{i}", [128, 2048], I32) for i in range(2)]
                sexp = [sb(f"dq_sc{i}", [128, 2048], F32) for i in range(2)]
                mag = sb("dq_mag", [128, 4096], F32)
                corr = sb("dq_corr", [128, 4096], F32)
                sbits = sb("dq_sb", [128, 4096], I32)
                ssgn = sb("dq_ssgn", [128, 4096], F32)
                val = [sb(f"dq_val{i}", [128, 4096], BF16) for i in range(2)]
                tv = sb("dq_tv", [1, 8], F32)
                tsc = sb("dq_ts", [1, 8], F32)

                jobs = []
                for jc in (0, 1):
                    src = gpk[jc * 128:(jc + 1) * 128, :]
                    scs = gsc[jc * 8:(jc + 1) * 8, :]
                    dlo = wsac[0:8, :, 2 * jc, :].transpose([1, 0, 2])
                    dhi = wsac[0:8, :, 2 * jc + 1, :].transpose([1, 0, 2])
                    jobs.append((src, scs, dlo, dhi))
                for w_pk, w_sc, w_dst in ((gpk, gsc, wg_loc),
                                          (upk, usc, wu_loc)):
                    for jc in range(8):
                        src = w_pk[jc * 128:(jc + 1) * 128, :]
                        scs = w_sc[jc * 8:(jc + 1) * 8, :]
                        dlo = w_dst[0:8, :, 2 * jc, :].transpose([1, 0, 2])
                        dhi = w_dst[0:8, :, 2 * jc + 1, :].transpose([1, 0, 2])
                        jobs.append((src, scs, dlo, dhi))
                for jl in range(8):
                    src = dpk[jl * 128:(jl + 1) * 128, :]
                    scs = dsc[jl * 8:(jl + 1) * 8, :]
                    dlo = wd_loc[:, :, 2 * jl, :].transpose([1, 0, 2])
                    dhi = wd_loc[:, :, 2 * jl + 1, :].transpose([1, 0, 2])
                    jobs.append((src, scs, dlo, dhi))
                NJ = len(jobs)
                b_g, b_s, b_va, b_vb, b_sa, b_qst = (g_ld.v, s_ld.v, v_a.v,
                                                     v_b.v, s_a.v, q_st.v)
                b_stg, b_stu = st_g.v, st_u.v

                def st_sem(j):
                    if 2 <= j <= 9:
                        return st_g
                    if 10 <= j <= 17:
                        return st_u
                    return q_st

                def st_waits(eng, j):
                    nq = 32 * (min(j + 1, 2) + max(0, j - 17))
                    ng = 32 * max(0, min(j, 9) - 1)
                    nu = 32 * max(0, min(j, 17) - 9)
                    if nq:
                        eng.wait_ge(q_st.h, b_qst + nq)
                    if ng:
                        eng.wait_ge(st_g.h, b_stg + ng)
                    if nu:
                        eng.wait_ge(st_u.h, b_stu + nu)

                with nc.Block() as block:
                    @block.gpsimd
                    def _(gp):
                        gathers = {
                            11: (10, wg_loc.rearrange("a p c n -> a (p c n)"),
                                 wg_s.rearrange("a p c n -> a (p c n)")),
                            21: (18, wu_loc.rearrange("a p c n -> a (p c n)"),
                                 wu_s.rearrange("a p c n -> a (p c n)")),
                            25: (26, wd_loc.rearrange("h p c n -> h (p c n)"),
                                 wd_s.rearrange("a p c n -> a (p c n)")),
                        }
                        for i in range(NJ):
                            if i >= 2:
                                gp.wait_ge(v_b.h, b_vb + i - 1)
                            ins = gp.dma_start(out=wp[i % 2][:], in_=jobs[i][0])
                            g_ld.inc(ins, 16)
                        for i in (11, 21, 25):
                            nj, src_ap, dst_ap = gathers[i]
                            sm, tgt = {
                                11: (st_g, b_stg + 256),
                                21: (st_u, b_stu + 256),
                                25: (q_st, b_qst + 320)}[i]
                            gp.wait_ge(sm.h, tgt)
                            ins = gp.collective_compute(
                                "AllGather", ALU.bypass,
                                replica_groups=[list(range(8))],
                                ins=[src_ap], outs=[dst_ap])
                            g_ld.inc(ins, 1)
                        gp.wait_ge(g_ld.h, g_ld.v)

                    @block.sync
                    def _(sync):
                        for i in range(NJ):
                            if i >= 2:
                                sync.wait_ge(v_b.h, b_vb + i - 1)
                            scs = jobs[i][1]
                            bcast = scs.unsqueeze(1).broadcast_to(
                                [8, 16, scs.shape[1]])
                            ins = sync.dma_start(out=sexp[i % 2][:], in_=bcast)
                            s_ld.inc(ins, 16)

                    @block.vector
                    def _(v):
                        magi = mag[:].bitcast(I32)
                        ssgni = ssgn[:].bitcast(I32)
                        for i in range(NJ):
                            w = wp[i % 2][:]
                            sci = sexp[i % 2][:].bitcast(I32)
                            v.wait_ge(g_ld.h, b_g + 16 * (i + 1))
                            v.wait_ge(s_ld.h, b_s + 16 * (i + 1))
                            v.tensor_scalar(magi[:, 0:2048], w, 7, 22,
                                            ALU.bitwise_and,
                                            ALU.logical_shift_left)
                            v.tensor_scalar(magi[:, 2048:4096], w, 0x70, 18,
                                            ALU.bitwise_and,
                                            ALU.logical_shift_left)
                            v.tensor_scalar(magi[:, 0:2048], magi[:, 0:2048],
                                            KBITS, None, ALU.add)
                            v.tensor_scalar(magi[:, 2048:4096],
                                            magi[:, 2048:4096],
                                            KBITS, None, ALU.add)
                            ins = v.tensor_copy(tv[0:1, 0:2], mag[0:1, 0:2])
                            v_a.inc(ins, 1)
                            v.wait_ge(s_a.h, b_sa + i + 1)
                            v.tensor_tensor(mag[:], mag[:], corr[:],
                                            ALU.subtract)
                            v.tensor_scalar(sbits[:, 0:2048], w, 8, 28,
                                            ALU.bitwise_and,
                                            ALU.logical_shift_left)
                            v.tensor_scalar(sbits[:, 2048:4096], w, 0x80, 24,
                                            ALU.bitwise_and,
                                            ALU.logical_shift_left)
                            v.tensor_tensor(ssgni[:, 0:2048], sci,
                                            sbits[:, 0:2048], ALU.bitwise_or)
                            v.tensor_tensor(ssgni[:, 2048:4096], sci,
                                            sbits[:, 2048:4096], ALU.bitwise_or)
                            if i >= 2:
                                st_waits(v, i - 2)
                            v.tensor_tensor(val[i % 2][:], mag[:],
                                            ssgn[:], ALU.mult)
                            ins = v.tensor_copy(tv[0:1, 2:4], mag[0:1, 0:2])
                            v_b.inc(ins, 1)

                    @block.scalar
                    def _(s):
                        for i in range(NJ):
                            s.wait_ge(v_a.h, b_va + i + 1)
                            s.activation(corr[:], mag[:], AF.Relu,
                                         bias=1.0, scale=-1.0)
                            ins = s.copy(tsc[0:1, 0:2], corr[0:1, 0:2])
                            s_a.inc(ins, 1)
                            s.wait_ge(v_b.h, b_vb + i + 1)
                            _, _, dlo, dhi = jobs[i]
                            ncol = dlo.shape[2]
                            vv = val[i % 2][:]
                            lo = vv[:, 0:2048].rearrange("p (a n) -> p a n",
                                                         n=ncol)
                            hi = vv[:, 2048:4096].rearrange("p (a n) -> p a n",
                                                            n=ncol)
                            ins = s.dma_start(out=dlo, in_=lo)
                            st_sem(i).inc(ins, 16)
                            ins = s.dma_start(out=dhi, in_=hi)
                            st_sem(i).inc(ins, 16)
                        st_waits(s, NJ - 1)

        def emit_ln_T(src, xT, perm, prestats=False):
            with ExitStack() as es:
                uid_ctr[0] += 1
                _u = uid_ctr[0]
                sb = lambda n, s, d: es.enter_context(
                    nc.sbuf_tensor(f"{n}_{_u}", s, d))
                xr = [sb(f"ln_x{i}", [128, 2048], F32) for i in range(2)]
                lnr = [sb(f"ln_o{i}", [128, 2048], BF16) for i in range(2)]
                sums = sb("ln_sum", [128, 16], F32)
                sqs = sb("ln_sqs", [128, 16], F32)
                sq_scr = sb("ln_scr", [128, 2048], F32)
                nm = sb("ln_nm", [128, 16], F32)
                m2 = sb("ln_m2", [128, 16], F32)
                var = sb("ln_var", [128, 16], F32)
                sd = sb("ln_sd", [128, 16], F32)
                rstd = sb("ln_rstd", [128, 16], F32)
                xc = sb("ln_xc", [128, 2048], F32)
                tv = sb("ln_tv", [1, 8], F32)
                tsc = sb("ln_ts", [1, 8], F32)

                def cslice(ln_ap, dc):
                    if perm == "ffn":
                        jc, par = dc >> 1, dc & 1
                        base = 256 * jc + par
                        return ln_ap[:, base:base + 255:2]
                    return ln_ap[:, dc * 128:(dc + 1) * 128]

                b_s, b_v, b_sa = s_ld.v, v_b.v, s_a.v
                if not prestats:
                    with nc.Block() as block:
                        @block.sync
                        def _(sync):
                            for t in range(16):
                                if t >= 2:
                                    sync.wait_ge(s_a.h, b_sa + (t - 2) + 1)
                                    sync.wait_ge(v_b.h, b_v + (t - 2) + 1)
                                ins = sync.dma_start(
                                    out=xr[t % 2][:],
                                    in_=src[t * 128:(t + 1) * 128, :])
                                s_ld.inc(ins, 16)

                        @block.scalar
                        def _(s):
                            for t in range(16):
                                s.wait_ge(s_ld.h, b_s + 16 * (t + 1))
                                s.activation(sq_scr[:], xr[t % 2][:],
                                             AF.Square,
                                             accum_out=sqs[:, t:t + 1])
                                ins = s.copy(tsc[0:1, 0:2], sqs[0:1, 0:2])
                                s_a.inc(ins, 1)

                        @block.vector
                        def _(v):
                            for t in range(16):
                                v.wait_ge(s_ld.h, b_s + 16 * (t + 1))
                                ins = v.reduce_sum(sums[:, t:t + 1],
                                                   xr[t % 2][:],
                                                   axis=AX.X)
                                v_b.inc(ins, 1)

                if prestats:
                    with nc.Block() as block:
                        @block.vector
                        def _(v):
                            v.tensor_tensor(nm[:], sa_psum[:, 0::4],
                                            sa_psum[:, 1::4], ALU.add)
                            v.tensor_tensor(m2[:], sa_psum[:, 2::4],
                                            sa_psum[:, 3::4], ALU.add)
                            v.tensor_tensor(var[:], sa_psq[:, 0::4],
                                            sa_psq[:, 1::4], ALU.add)
                            v.tensor_tensor(sd[:], sa_psq[:, 2::4],
                                            sa_psq[:, 3::4], ALU.add)
                            for _pad in range(8):
                                v.tensor_copy(lnr[0][:][:, 0:512],
                                              xc[:][:, 0:512])
                            v.tensor_tensor(sums[:], nm[:], m2[:], ALU.add)
                            v.tensor_tensor(sqs[:], var[:], sd[:], ALU.add)
                b_v, b_sa = v_b.v, s_a.v
                with nc.Block() as block:
                    @block.vector
                    def _(v):
                        ins = v.tensor_scalar(nm[:], sums[:], -1.0 / D, None,
                                              ALU.mult)
                        v_b.inc(ins, 1)
                        v.wait_ge(s_a.h, b_sa + 1)
                        ins = v.scalar_tensor_tensor(
                            var[:], sqs[:], 1.0 / D, m2[:],
                            ALU.mult, ALU.subtract)
                        v_b.inc(ins, 1)
                        v.wait_ge(s_a.h, b_sa + 2)
                        ins = v.reciprocal(rstd[:], sd[:])
                        v_b.inc(ins, 1)

                    @block.scalar
                    def _(s):
                        s.wait_ge(v_b.h, b_v + 1)
                        ins = s.activation(m2[:], nm[:], AF.Square)
                        s_a.inc(ins, 1)
                        s.wait_ge(v_b.h, b_v + 2)
                        ins = s.activation(sd[:], var[:], AF.Sqrt, bias=1e-5)
                        s_a.inc(ins, 1)

                b_s, b_v, b_va, b_tp = s_ld.v, v_b.v, v_a.v, t_tp.v
                with nc.Block() as block:
                    @block.sync
                    def _(sync):
                        for t in range(16):
                            if t >= 2:
                                sync.wait_ge(v_b.h, b_v + (t - 2) + 1)
                            ins = sync.dma_start(
                                out=xr[t % 2][:],
                                in_=src[t * 128:(t + 1) * 128, :])
                            s_ld.inc(ins, 16)

                    @block.vector
                    def _(v):
                        xTv = xT[:].rearrange("p (c n) -> p c n", n=2048)
                        p6b = p[6][:].bitcast(BF16)
                        p7b = p[7][:].bitcast(BF16)
                        for t in range(16):
                            c = t % 2
                            v.wait_ge(s_ld.h, b_s + 16 * (t + 1))
                            v.tensor_tensor(
                                xc[:], xr[c][:],
                                nm[:, t:t + 1].broadcast_to([128, 2048]),
                                ALU.add)
                            if t >= 2:
                                v.wait_ge(t_tp.h, b_tp + 2 * (t - 2) + 2)
                            v.tensor_tensor(
                                lnr[c][:], xc[:],
                                rstd[:, t:t + 1].broadcast_to([128, 2048]),
                                ALU.mult)
                            ins = v.tensor_copy(tv[0:1, 4:6], tv[0:1, 0:2])
                            v_b.inc(ins, 1)
                            v.wait_ge(t_tp.h, b_tp + 2 * t + 1)
                            v.tensor_copy(
                                xTv[:, 0:8, t * 128:(t + 1) * 128],
                                p6b.rearrange("p (c n) -> p c n", n=128))
                            v.wait_ge(t_tp.h, b_tp + 2 * t + 2)
                            ins = v.tensor_copy(
                                xTv[:, 8:16, t * 128:(t + 1) * 128],
                                p7b.rearrange("p (c n) -> p c n", n=128))
                            v_a.inc(ins, 1)

                    @block.tensor
                    def _(te):
                        p6b = p[6][:].bitcast(BF16)
                        p7b = p[7][:].bitcast(BF16)
                        for t in range(16):
                            c = t % 2
                            te.wait_ge(v_b.h, b_v + t + 1)
                            if t >= 1:
                                te.wait_ge(v_a.h, b_va + t)
                            for dc in range(8):
                                ins = te.transpose(
                                    p6b[:, dc * 128:(dc + 1) * 128],
                                    cslice(lnr[c][:], dc), kc["ident"][:])
                            t_tp.inc(ins, 1)
                            for dc in range(8, 16):
                                ins = te.transpose(
                                    p7b[:, (dc - 8) * 128:(dc - 7) * 128],
                                    cslice(lnr[c][:], dc), kc["ident"][:])
                            t_tp.inc(ins, 1)

        def emit_ffn_a(xT):
            with ExitStack() as es:
                uid_ctr[0] += 1
                _u = uid_ctr[0]
                sb = lambda n, s, d: es.enter_context(
                    nc.sbuf_tensor(f"{n}_{_u}", s, d))
                wgb = [sb(f"fa_wg{i}", [128, 4096], BF16) for i in range(2)]
                wub = [sb(f"fa_wu{i}", [128, 4096], BF16) for i in range(2)]
                g2c = [sb(f"fa_g2c{i}", [128, 512], F32) for i in range(2)]
                arg = [sb(f"fa_arg{i}", [128, 512], F32) for i in range(2)]
                tnh = [sb(f"fa_tnh{i}", [128, 512], F32) for i in range(2)]
                gl = [[sb(f"fa_gl{s_}_{i}", [128, 512], F32)
                       for i in range(4)] for s_ in range(2)]
                stg = [sb(f"fa_st{i}", [128, 512], BF16) for i in range(8)]
                tv = sb("fa_tv", [1, 8], F32)
                tsc = sb("fa_ts", [1, 8], F32)
                b_s, b_mm, b_v, b_sa, b_qst = (s_ld.v, mm.v, v_a.v, s_a.v,
                                               q_st.v)

                def slot(g2, tq):
                    return (4 * g2 + tq) % 8

                with nc.Block() as block:
                    @block.sync
                    def _(sync):
                        for b in range(64):
                            if b >= 2:
                                sync.wait_ge(mm.h, b_mm + 16 * (b - 2) + 8)
                            ins = sync.dma_start(
                                out=wgb[b % 2][:].rearrange(
                                    "p (c n) -> p c n", n=256),
                                in_=wg_s[b])
                            s_ld.inc(ins, 16)
                            if b >= 2:
                                sync.wait_ge(mm.h, b_mm + 16 * (b - 2) + 16)
                            ins = sync.dma_start(
                                out=wub[b % 2][:].rearrange(
                                    "p (c n) -> p c n", n=256),
                                in_=wu_s[b])
                            s_ld.inc(ins, 16)

                    @block.tensor
                    def _(te):
                        for b in range(64):
                            for ph in range(4):
                                sub = ph % 2
                                off = 4 * sub
                                wblk = wgb[b % 2] if ph < 2 else wub[b % 2]
                                if ph == 0:
                                    te.wait_ge(s_ld.h, b_s + 32 * (b + 1))
                                    if b >= 1:
                                        te.wait_ge(v_a.h, b_v + 24 * b)
                                elif ph == 1:
                                    if b >= 1:
                                        te.wait_ge(v_a.h, b_v + 24 * b + 4)
                                elif ph == 2:
                                    te.wait_ge(v_a.h, b_v + 24 * b + 12)
                                    te.wait_ge(s_a.h, b_sa + 16 * b + 8)
                                else:
                                    te.wait_ge(v_a.h, b_v + 24 * b + 20)
                                    te.wait_ge(s_a.h, b_sa + 16 * b + 16)
                                for dc in range(16):
                                    for tq in range(4):
                                        ins = te.matmul(
                                            p[off + tq][:],
                                            wblk[:][:, dc * 256 + sub:
                                                    dc * 256 + 256:2],
                                            xT[:][:, dc * 2048 + tq * 512:
                                                  dc * 2048 + (tq + 1) * 512],
                                            start=(dc == 0), stop=(dc == 15))
                                        if dc == 15:
                                            mm.inc(ins, 1)

                    @block.scalar
                    def _(s):
                        def gl_scalar(b, sub):
                            for tq in range(2):
                                pass
                            for tq in range(4):
                                s.wait_ge(mm.h,
                                          b_mm + 16 * b + 4 * sub + tq + 1)
                                s.activation(g2c[tq % 2][:],
                                             p[4 * sub + tq][:], AF.Square,
                                             scale=SQC)
                                ins = s.copy(tsc[0:1, 0:2],
                                             g2c[tq % 2][0:1, 0:2])
                                s_a.inc(ins, 1)
                                s.wait_ge(v_a.h, b_v + 24 * b + 4 + 8 * sub +
                                          2 * tq + 1)
                                s.activation(tnh[tq % 2][:], arg[tq % 2][:],
                                             AF.Tanh)
                                ins = s.copy(tsc[0:1, 2:4],
                                             tnh[tq % 2][0:1, 0:2])
                                s_a.inc(ins, 1)

                        def st_inter(b, sub, vbase):
                            cc = 2 * b + sub
                            for tq in range(4):
                                s.wait_ge(v_a.h, vbase + tq + 1)
                                ins = s.dma_start(
                                    out=inter_s[tq, cc // 16, :, cc % 16, :],
                                    in_=stg[slot(cc, tq)][:])
                                q_st.inc(ins, 16)

                        for b in range(64):
                            if b >= 1:
                                st_inter(b - 1, 1, b_v + 24 * b)
                            gl_scalar(b, 0)
                            gl_scalar(b, 1)
                            st_inter(b, 0, b_v + 24 * b + 20)
                        st_inter(63, 1, b_v + 24 * 64)
                        s.wait_ge(q_st.h, b_qst + 16 * 512)

                    @block.vector
                    def _(v):
                        def inter_ops(b, sub, mmbase):
                            cc = 2 * b + sub
                            for tq in range(4):
                                v.wait_ge(mm.h, mmbase + tq + 1)
                                g2p = cc - 2
                                if g2p >= 0:
                                    v.wait_ge(q_st.h, b_qst +
                                              16 * (4 * g2p + tq + 1))
                                v.tensor_tensor(stg[slot(cc, tq)][:],
                                                gl[sub][tq][:],
                                                p[4 * sub + tq][:], ALU.mult)
                                ins = v.tensor_copy(
                                    tv[0:1, 0:2],
                                    stg[slot(cc, tq)][0:1, 0:2])
                                v_a.inc(ins, 1)

                        def gl_vector(b, sub):
                            for tq in range(4):
                                v.wait_ge(s_a.h, b_sa + 16 * b + 8 * sub +
                                          2 * tq + 1)
                                v.scalar_tensor_tensor(
                                    arg[tq % 2][:], g2c[tq % 2][:], C0,
                                    p[4 * sub + tq][:], ALU.add, ALU.mult)
                                ins = v.tensor_copy(tv[0:1, 2:4],
                                                    arg[tq % 2][0:1, 0:2])
                                v_a.inc(ins, 1)
                                v.wait_ge(s_a.h, b_sa + 16 * b + 8 * sub +
                                          2 * tq + 2)
                                v.scalar_tensor_tensor(
                                    gl[sub][tq][:], tnh[tq % 2][:], 1.0,
                                    p[4 * sub + tq][:], ALU.add, ALU.mult)
                                ins = v.tensor_copy(tv[0:1, 4:6],
                                                    gl[sub][tq][0:1, 0:2])
                                v_a.inc(ins, 1)

                        for b in range(64):
                            if b >= 1:
                                inter_ops(b - 1, 1,
                                          b_mm + 16 * (b - 1) + 12)
                            else:
                                for _d in range(4):
                                    ins = v.tensor_copy(tv[0:1, 6:8],
                                                        tv[0:1, 0:2])
                                    v_a.inc(ins, 1)
                            gl_vector(b, 0)
                            gl_vector(b, 1)
                            inter_ops(b, 0, b_mm + 16 * b + 8)
                        inter_ops(63, 1, b_mm + 16 * 63 + 12)

        def emit_ffn_b(x_src, dst, final=False, stats=False):
            with ExitStack() as es:
                uid_ctr[0] += 1
                _u = uid_ctr[0]
                sb = lambda n, s, d: es.enter_context(
                    nc.sbuf_tensor(f"{n}_{_u}", s, d))
                inter_a = sb("fb_inta", [128, 32768], BF16)
                inter_b = sb("fb_intb", [128, 32768], BF16)
                wdr = [sb(f"fb_wd{i}", [128, 8192], BF16) for i in range(2)]
                xres = [sb(f"fb_xr{i}", [128, 2048], F32) for i in range(2)]
                ystg = [sb(f"fb_ys{i}", [128, 2048], F32) for i in range(2)]
                tv = sb("fb_tv", [1, 8], F32)
                b_s, b_mm, b_va, b_qld, b_qst = (s_ld.v, mm.v, v_a.v, q_ld.v,
                                                 q_st.v)

                def xtile(t_ap, tb, hs):
                    return t_ap[tb * 512:(tb + 1) * 512,
                                hs * 512:(hs + 1) * 512].rearrange(
                                    "(a p) n -> p a n", p=128)

                ld_plan = []
                ipos = {}
                wpos = {}
                _np = [0, 0]

                def plan_eighth(tb, e):
                    gate = None
                    if tb >= 1:
                        gate = 8 * (4 * (tb - 1) + 3) + e + 1
                    ld_plan.append(("i", tb, e, gate))
                    _np[0] += 1
                    ipos[(tb, e)] = _np[0]

                def plan_wd(tb, hs, g):
                    gs = (tb * 4 + hs) * 8 + g
                    gate = gs - 1 if gs >= 2 else None
                    ld_plan.append(("w", tb, hs, g, gate))
                    _np[1] += 1
                    wpos[(tb, hs, g)] = _np[1]

                for k in range(8):
                    plan_eighth(0, k)
                    plan_wd(0, 0, k)
                for tb in range(4):
                    for hs in range(3):
                        if (tb, hs) != (0, 0):
                            for g in range(8):
                                plan_wd(tb, hs, g)
                    if tb < 3:
                        plan_wd(tb, 3, 0)
                        plan_wd(tb, 3, 1)
                        for k in range(6):
                            plan_eighth(tb + 1, k)
                            plan_wd(tb, 3, k + 2)
                        plan_eighth(tb + 1, 6)
                        plan_eighth(tb + 1, 7)
                    else:
                        for g in range(8):
                            plan_wd(tb, 3, g)

                b_il = i_ld.v
                with nc.Block() as block:
                    @block.gpsimd
                    def _(gp):
                        for item in ld_plan:
                            if item[0] != "i":
                                continue
                            _, tb, e, gate = item
                            if gate is not None:
                                gp.wait_ge(mm.h, b_mm + gate)
                            half = inter_a if e < 4 else inter_b
                            ee = e % 4
                            ins = gp.dma_start(
                                out=half[:][:, ee * 8192:(ee + 1) * 8192]
                                .rearrange("p (c n) -> p c n", n=512),
                                in_=inter_s[tb, e])
                            i_ld.inc(ins, 16)

                    @block.sync
                    def _(sync):
                        for item in ld_plan:
                            if item[0] != "w":
                                continue
                            _, tb, hs, g, gate = item
                            if gate is not None:
                                sync.wait_ge(mm.h, b_mm + gate)
                            gs = (tb * 4 + hs) * 8 + g
                            ins = sync.dma_start(
                                out=wdr[gs % 2][:].rearrange(
                                    "p (c n) -> p c n", n=512),
                                in_=wd_s[4 * g + hs])
                            s_ld.inc(ins, 16)

                    @block.tensor
                    def _(te):
                        for tb in range(4):
                            for hs in range(4):
                                idx = 4 * tb + hs
                                if idx >= 2:
                                    te.wait_ge(v_a.h, b_va + idx - 1)
                                pd = [p[(idx % 2) * 4 + tt] for tt in range(4)]
                                for c in range(128):
                                    g = c // 16
                                    gs = (tb * 4 + hs) * 8 + g
                                    if c % 16 == 0:
                                        te.wait_ge(i_ld.h,
                                                   b_il + 16 * ipos[(tb, g)])
                                        te.wait_ge(s_ld.h,
                                                   b_s + 16 * wpos[(tb, hs,
                                                                    g)])
                                    ibuf = inter_a if c < 64 else inter_b
                                    cc = c % 64
                                    for tt in range(4):
                                        ins = te.matmul(
                                            pd[tt][:],
                                            ibuf[:][:,
                                                    cc * 512 + tt * 128:
                                                    cc * 512 + (tt + 1) * 128],
                                            wdr[gs % 2][:][:,
                                                           (c % 16) * 512:
                                                           (c % 16 + 1) * 512],
                                            start=(c == 0), stop=(c == 127))
                                    if c % 16 == 15:
                                        mm.inc(ins, 1)

                    @block.scalar
                    def _(s):
                        for idx in range(16):
                            tb, hs = divmod(idx, 4)
                            if idx == 0:
                                ins = s.dma_start(
                                    out=xres[0][:].rearrange(
                                        "p (a n) -> p a n", n=512),
                                    in_=xtile(x_src, 0, 0))
                                q_ld.inc(ins, 16)
                            if idx + 1 < 16:
                                tb2, hs2 = divmod(idx + 1, 4)
                                if idx >= 1:
                                    s.wait_ge(v_a.h, b_va + idx)
                                ins = s.dma_start(
                                    out=xres[(idx + 1) % 2][:].rearrange(
                                        "p (a n) -> p a n", n=512),
                                    in_=xtile(x_src, tb2, hs2))
                                q_ld.inc(ins, 16)
                            s.wait_ge(v_a.h, b_va + idx + 1)
                            ins = s.dma_start(
                                out=xtile(dst, tb, hs),
                                in_=ystg[idx % 2][:].rearrange(
                                    "p (a n) -> p a n", n=512))
                            q_st.inc(ins, 16)
                        s.wait_ge(q_st.h, b_qst + 16 * 16)

                    @block.vector
                    def _(v):
                        for idx in range(16):
                            pd = [p[(idx % 2) * 4 + tt] for tt in range(4)]
                            v.wait_ge(mm.h, b_mm + 8 * (idx + 1))
                            v.wait_ge(q_ld.h, b_qld + 16 * (idx + 1))
                            if idx >= 2:
                                v.wait_ge(q_st.h, b_qst + 16 * (idx - 1))
                            for tt in range(4):
                                v.scalar_tensor_tensor(
                                    ystg[idx % 2][:][:,
                                                     tt * 512:(tt + 1) * 512],
                                    pd[tt][:], 0.25,
                                    xres[idx % 2][:][:,
                                                     tt * 512:(tt + 1) * 512],
                                    ALU.mult, ALU.add)
                            ins = v.tensor_copy(tv[0:1, 0:2],
                                                ystg[idx % 2][0:1, 0:2])
                            v_a.inc(ins, 1)
                            if stats:
                                tbs, hss = divmod(idx, 4)
                                for a in range(4):
                                    ysl = ystg[idx % 2][:][:,
                                                          a * 512:
                                                          (a + 1) * 512]
                                    col = (4 * tbs + a) * 4 + hss
                                    v.reduce_sum(
                                        sa_psum[:, col:col + 1], ysl,
                                        axis=AX.X)
                                    v.tensor_tensor(sa_scr[:], ysl, ysl,
                                                    ALU.mult)
                                    v.reduce_sum(
                                        sa_psq[:, col:col + 1], sa_scr[:],
                                        axis=AX.X)

        def emit_attn_proj(xT):
            with ExitStack() as es:
                uid_ctr[0] += 1
                _u = uid_ctr[0]
                sb = lambda n, s, d: es.enter_context(
                    nc.sbuf_tensor(f"{n}_{_u}", s, d))
                wqs = [sb(f"ap_wq{i}", [128, 8192], BF16) for i in range(2)]
                wks = [sb(f"ap_wk{i}", [128, 8192], BF16) for i in range(2)]
                sqq = sb("ap_sqq", [128, 512], F32)
                sqk = sb("ap_sqk", [128, 512], F32)
                qss = sb("ap_qss", [128, 4], F32)
                kss = sb("ap_kss", [128, 4], F32)
                sdq = sb("ap_sdq", [128, 4], F32)
                sdk = sb("ap_sdk", [128, 4], F32)
                rq_all = sb("ap_rqa", [128, 256], F32)
                rk_all = sb("ap_rka", [128, 256], F32)
                qraw_st = [sb(f"ap_qr{i}", [128, 512], BF16) for i in range(2)]
                kvraw_st = [sb(f"ap_kr{i}", [128, 512], BF16) for i in range(2)]
                qn_st = [sb(f"ap_qn{i}", [128, 512], BF16)
                         for i in range(2)]
                kvn_st = [sb(f"ap_kv{i}", [128, 512], BF16) for i in range(2)]
                qnT_sp = [sb(f"ap_qT{i}", [128, 8192], BF16) for i in range(2)]
                tv = sb("ap_tv", [1, 8], F32)
                tsc = sb("ap_ts", [1, 8], F32)
                rqv = rq_all[:].rearrange("p (u h) -> p u h", h=4)
                rkv = rk_all[:].rearrange("p (u h) -> p u h", h=4)

                b_g, b_mm, b_sa, b_vb, b_qst = (g_ld.v, mm.v, s_a.v, v_b.v,
                                                q_st.v)
                with nc.Block() as block:
                    @block.gpsimd
                    def _(gp):
                        for sp in range(4):
                            if sp >= 2:
                                gp.wait_ge(mm.h, b_mm + 32 * (sp - 1))
                            srcw = wqT[:, sp * 512:(sp + 1) * 512].rearrange(
                                "(c p) n -> p c n", p=128)
                            ins = gp.dma_start(
                                out=wqs[sp % 2][:].rearrange(
                                    "p (c n) -> p c n", n=512), in_=srcw)
                            g_ld.inc(ins, 16)
                            srcw = wkvT[:, sp * 512:(sp + 1) * 512].rearrange(
                                "(c p) n -> p c n", p=128)
                            ins = gp.dma_start(
                                out=wks[sp % 2][:].rearrange(
                                    "p (c n) -> p c n", n=512), in_=srcw)
                            g_ld.inc(ins, 16)

                    @block.tensor
                    def _(te):
                        for u in range(64):
                            sp, t = divmod(u, 16)
                            if t == 0:
                                te.wait_ge(g_ld.h, b_g + 32 * (sp + 1))
                            if u >= 2:
                                te.wait_ge(v_b.h, b_vb + 3 * (u - 2) + 3)
                                te.wait_ge(s_a.h, b_sa + 4 * (u - 2) + 4)
                            pq = p[2 * (u % 2)]
                            pkv = p[2 * (u % 2) + 1]
                            for ps, wsp in ((pq, wqs[sp % 2]),
                                            (pkv, wks[sp % 2])):
                                for dc in range(16):
                                    ins = te.matmul(
                                        ps[:],
                                        xT[:][:, dc * 2048 + t * 128:
                                              dc * 2048 + (t + 1) * 128],
                                        wsp[:][:, dc * 512:(dc + 1) * 512],
                                        start=(dc == 0), stop=(dc == 15))
                                mm.inc(ins, 1)

                    @block.scalar
                    def _(s):
                        for u in range(64):
                            s.wait_ge(mm.h, b_mm + 2 * u + 1)
                            if u >= 1:
                                s.wait_ge(v_b.h, b_vb + 3 * (u - 1) + 1)
                            pq = p[2 * (u % 2)]
                            pkv = p[2 * (u % 2) + 1]
                            s.activation(sqq[:], pq[:], AF.Square)
                            ins = s.copy(tsc[0:1, 0:2], sqq[0:1, 0:2])
                            s_a.inc(ins, 1)
                            s.wait_ge(v_b.h, b_vb + 3 * u + 1)
                            s.activation(sdq[:], qss[:], AF.Sqrt, scale=128.0)
                            ins = s.copy(tsc[0:1, 2:4], sdq[0:1, 0:2])
                            s_a.inc(ins, 1)
                            s.wait_ge(mm.h, b_mm + 2 * u + 2)
                            s.activation(sqk[:], pkv[:], AF.Square)
                            ins = s.copy(tsc[0:1, 4:6], sqk[0:1, 0:2])
                            s_a.inc(ins, 1)
                            s.wait_ge(v_b.h, b_vb + 3 * u + 2)
                            s.activation(sdk[:], kss[:], AF.Sqrt, scale=128.0)
                            ins = s.copy(tsc[0:1, 6:8], sdk[0:1, 0:2])
                            s_a.inc(ins, 1)
                            sp, t = divmod(u, 16)
                            s.wait_ge(v_b.h, b_vb + 3 * u + 3)
                            ins = s.dma_start(
                                out=q_raw_s[t * 128:(t + 1) * 128,
                                            sp * 512:(sp + 1) * 512],
                                in_=qraw_st[u % 2][:])
                            q_st.inc(ins, 16)
                            ins = s.dma_start(
                                out=kv_raw_s[t * 128:(t + 1) * 128,
                                             sp * 512:(sp + 1) * 512],
                                in_=kvraw_st[u % 2][:])
                            q_st.inc(ins, 16)
                        s.wait_ge(q_st.h, b_qst + 32 * 64)

                    @block.vector
                    def _(v):
                        for u in range(64):
                            pq = p[2 * (u % 2)]
                            pkv = p[2 * (u % 2) + 1]
                            v.wait_ge(s_a.h, b_sa + 4 * u + 1)
                            v.reduce_sum(qss[:].unsqueeze(2),
                                         sqq[:].rearrange(
                                             "p (a b) -> p a b", b=128),
                                         axis=AX.X)
                            ins = v.tensor_copy(tv[0:1, 0:2], qss[0:1, 0:2])
                            v_b.inc(ins, 1)
                            v.wait_ge(s_a.h, b_sa + 4 * u + 2)
                            v.reciprocal(rqv[:, u, :], sdq[:])
                            if u >= 2:
                                v.wait_ge(q_st.h, b_qst + 32 * (u - 1))
                            v.tensor_copy(qraw_st[u % 2][:], pq[:])
                            v.wait_ge(s_a.h, b_sa + 4 * u + 3)
                            v.reduce_sum(kss[:].unsqueeze(2),
                                         sqk[:].rearrange(
                                             "p (a b) -> p a b", b=128),
                                         axis=AX.X)
                            ins = v.tensor_copy(tv[0:1, 2:4], kss[0:1, 0:2])
                            v_b.inc(ins, 1)
                            v.wait_ge(s_a.h, b_sa + 4 * u + 4)
                            v.reciprocal(rkv[:, u, :], sdk[:])
                            v.tensor_copy(kvraw_st[u % 2][:], pkv[:])
                            ins = v.tensor_copy(tv[0:1, 4:6], tv[0:1, 0:2])
                            v_b.inc(ins, 1)

                b_s, b_mm, b_vb, b_tp, b_qst, b_s2 = (
                    s_ld.v, mm.v, v_b.v, t_tp.v, q_st.v, s2_st.v)
                with nc.Block() as block:
                    @block.sync
                    def _(sync):
                        for u in range(64):
                            sp, t = divmod(u, 16)
                            if u >= 2:
                                sync.wait_ge(v_b.h, b_vb + 3 * (u - 2) + 3)
                            ins = sync.dma_start(
                                out=qraw_st[u % 2][:],
                                in_=q_raw_s[t * 128:(t + 1) * 128,
                                            sp * 512:(sp + 1) * 512])
                            s_ld.inc(ins, 16)
                            ins = sync.dma_start(
                                out=kvraw_st[u % 2][:],
                                in_=kv_raw_s[t * 128:(t + 1) * 128,
                                             sp * 512:(sp + 1) * 512])
                            s_ld.inc(ins, 16)

                    @block.gpsimd
                    def _(gp):
                        for sp in range(4):
                            gp.wait_ge(v_b.h,
                                       b_vb + 3 * (16 * sp + 16) + 1)
                            qTv = qnT_sp[sp % 2][:].rearrange(
                                "p (h n) -> p h n", n=2048)
                            for h in range(4):
                                ins = gp.dma_start(
                                    out=qnT_s[(4 * sp + h) * 128:
                                              (4 * sp + h + 1) * 128, :],
                                    in_=qTv[:, h, :])
                                s2_st.inc(ins, 16)
                        gp.wait_ge(s2_st.h, b_s2 + 16 * 16)

                    @block.vector
                    def _(v):
                        def qcopy(uc):
                            spc, tc = divmod(uc, 16)
                            if tc == 0 and spc >= 2:
                                v.wait_ge(s2_st.h, b_s2 + 64 * (spc - 1))
                            v.wait_ge(t_tp.h, b_tp + uc + 1)
                            pub = p[6 + uc % 2][:].bitcast(BF16)
                            qTv = qnT_sp[spc % 2][:].rearrange(
                                "p (h n) -> p h n", n=2048)
                            ins = v.tensor_copy(
                                qTv[:, :, tc * 128:(tc + 1) * 128],
                                pub[:, 0:512].rearrange(
                                    "p (h n) -> p h n", n=128))
                            v_b.inc(ins, 1)

                        for u in range(64):
                            sp, t = divmod(u, 16)
                            if u >= 1:
                                qcopy(u - 1)
                            else:
                                ins = v.tensor_copy(tv[0:1, 0:2],
                                                    tv[0:1, 2:4])
                                v_b.inc(ins, 1)
                            v.wait_ge(s_ld.h, b_s + 32 * (u + 1))
                            if u >= 1:
                                v.wait_ge(t_tp.h, b_tp + u - 1)
                            v.tensor_tensor(
                                qn_st[u % 2][:].rearrange(
                                    "p (a b) -> p a b", b=128),
                                qraw_st[u % 2][:].rearrange(
                                    "p (a b) -> p a b", b=128),
                                rqv[:, u, :].unsqueeze(2).broadcast_to(
                                    [128, 4, 128]),
                                ALU.mult)
                            ins = v.tensor_copy(tv[0:1, 0:2], tv[0:1, 2:4])
                            v_b.inc(ins, 1)
                            if u >= 2:
                                v.wait_ge(q_st.h, b_qst + 16 * (u - 1))
                            v.tensor_tensor(
                                kvn_st[u % 2][:].rearrange(
                                    "p (a b) -> p a b", b=128),
                                kvraw_st[u % 2][:].rearrange(
                                    "p (a b) -> p a b", b=128),
                                rkv[:, u, :].unsqueeze(2).broadcast_to(
                                    [128, 4, 128]),
                                ALU.mult)
                            ins = v.tensor_copy(tv[0:1, 2:4], tv[0:1, 4:6])
                            v_b.inc(ins, 1)
                        qcopy(63)

                    @block.tensor
                    def _(te):
                        for u in range(64):
                            pub = p[6 + u % 2][:].bitcast(BF16)
                            te.wait_ge(v_b.h, b_vb + 3 * u + 2)
                            for h in range(4):
                                ins = te.transpose(
                                    pub[:, h * 128:(h + 1) * 128],
                                    qn_st[u % 2][:][:, h * 128:(h + 1) * 128],
                                    kc["ident"][:])
                            t_tp.inc(ins, 1)

                    @block.scalar
                    def _(s):
                        for u in range(64):
                            sp, t = divmod(u, 16)
                            s.wait_ge(v_b.h, b_vb + 3 * u + 3)
                            ins = s.dma_start(
                                out=kvn_s[sp, :, t, :],
                                in_=kvn_st[u % 2][:])
                            q_st.inc(ins, 16)
                        s.wait_ge(q_st.h, b_qst + 16 * 64)

        def emit_cumsum():
            with ExitStack() as es:
                uid_ctr[0] += 1
                _u = uid_ctr[0]
                sb = lambda n, s, d: es.enter_context(
                    nc.sbuf_tensor(f"{n}_{_u}", s, d))
                kvsp = [sb(f"cs_kv{i}", [128, 8192], BF16) for i in range(2)]
                kvc_st = [sb(f"cs_st{i}", [128, 512], BF16)
                          for i in range(2)]
                kvcT_sp = [sb(f"cs_T{i}", [128, 8192], BF16) for i in range(2)]
                b_s, b_mm, b_va, b_vb, b_tp, b_s2 = (s_ld.v, mm.v, v_a.v,
                                                     v_b.v, t_tp.v, s2_st.v)

                with nc.Block() as block:
                    @block.sync
                    def _(sync):
                        for sp in range(4):
                            if sp >= 2:
                                sync.wait_ge(mm.h, b_mm + 16 * (sp - 1))
                            src = kvn_s[sp]
                            ins = sync.dma_start(
                                out=kvsp[sp % 2][:].rearrange(
                                    "p (a n) -> p a n", n=512), in_=src)
                            s_ld.inc(ins, 16)
                            sync.wait_ge(v_b.h, b_vb + 16 * (sp + 1))
                            Tv = kvcT_sp[sp % 2][:].rearrange(
                                "p (h n) -> p h n", n=2048)
                            for h in range(4):
                                ins = sync.dma_start(
                                    out=kvcT_s[(4 * sp + h) * 128:
                                               (4 * sp + h + 1) * 128, :],
                                    in_=Tv[:, h, :])
                                s2_st.inc(ins, 16)
                        sync.wait_ge(s2_st.h, b_s2 + 16 * 16)

                    @block.tensor
                    def _(te):
                        p6b = p[6][:].bitcast(BF16)
                        for u in range(64):
                            sp, tout = divmod(u, 16)
                            if tout == 0:
                                te.wait_ge(s_ld.h, b_s + 16 * (sp + 1))
                            pc = p[u % 2]
                            if u >= 2:
                                te.wait_ge(v_a.h, b_va + u - 1)
                            kvv = kvsp[sp % 2][:].rearrange(
                                "p (a n) -> p a n", n=512)
                            for tp in range(tout + 1):
                                ins = te.matmul(
                                    pc[:],
                                    kc["tri"][:] if tp == tout
                                    else kc["ones"][:],
                                    kvv[:, tp, :],
                                    start=(tp == 0), stop=(tp == tout))
                            mm.inc(ins, 1)
                            te.wait_ge(v_a.h, b_va + u + 1)
                            if u >= 2:
                                te.wait_ge(v_b.h, b_vb + u - 1)
                            pub = p[6 + u % 2][:].bitcast(BF16)
                            for h in range(4):
                                ins = te.transpose(
                                    pub[:, h * 128:(h + 1) * 128],
                                    kvc_st[u % 2][:][:, h * 128:(h + 1) * 128],
                                    kc["ident"][:])
                            t_tp.inc(ins, 1)

                    @block.vector
                    def _(v):
                        for u in range(64):
                            sp, tout = divmod(u, 16)
                            pc = p[u % 2]
                            v.wait_ge(mm.h, b_mm + u + 1)
                            ins = v.tensor_copy(kvc_st[u % 2][:], pc[:])
                            v_a.inc(ins, 1)
                            if tout == 0 and sp >= 2:
                                v.wait_ge(s2_st.h, b_s2 + 64 * (sp - 1))
                            v.wait_ge(t_tp.h, b_tp + u + 1)
                            pub = p[6 + u % 2][:].bitcast(BF16)
                            Tv = kvcT_sp[sp % 2][:].rearrange(
                                "p (h n) -> p h n", n=2048)
                            ins = v.tensor_copy(
                                Tv[:, :, tout * 128:(tout + 1) * 128],
                                pub[:, 0:512].rearrange(
                                    "p (h n) -> p h n", n=128))
                            v_b.inc(ins, 1)

        def emit_heads():
            with ExitStack() as es:
                uid_ctr[0] += 1
                _u = uid_ctr[0]
                sb = lambda n, s, d: es.enter_context(
                    nc.sbuf_tensor(f"{n}_{_u}", s, d))
                qh = [sb(f"hd_q{i}", [128, 2048], BF16) for i in range(2)]
                kvh = [sb(f"hd_k{i}", [128, 2048], BF16) for i in range(2)]
                qfr = sb("hd_qfr", [128, 2048], F32)
                qfi = sb("hd_qfi", [128, 2048], F32)
                kr = sb("hd_kr", [128, 2048], F32)
                ki = sb("hd_ki", [128, 2048], F32)
                t1 = sb("hd_t1", [128, 2048], F32)
                u1 = sb("hd_u1", [128, 2048], F32)
                qvr = sb("hd_qvr", [128, 2048], BF16)
                qvi = sb("hd_qvi", [128, 2048], BF16)
                vals_st = [sb(f"hd_v{i}", [128, 2048], BF16) for i in range(2)]
                b_s, b_mm, b_sa, b_vb, b_qst = (s_ld.v, mm.v, s_a.v, v_b.v,
                                                q_st.v)

                with nc.Block() as block:
                    @block.sync
                    def _(sync):
                        for h in range(16):
                            if h >= 2:
                                sync.wait_ge(mm.h, b_mm + 12 * (h - 1))
                            ins = sync.dma_start(
                                out=qh[h % 2][:],
                                in_=qnT_s[h * 128:(h + 1) * 128, :])
                            s_ld.inc(ins, 16)
                            ins = sync.dma_start(
                                out=kvh[h % 2][:],
                                in_=kvcT_s[h * 128:(h + 1) * 128, :])
                            s_ld.inc(ins, 16)

                    @block.tensor
                    def _(te):
                        for h in range(16):
                            te.wait_ge(s_ld.h, b_s + 32 * (h + 1))
                            if h >= 1:
                                te.wait_ge(v_b.h, b_vb + 4 * h)
                                te.wait_ge(s_a.h, b_sa + 12 * (h - 1) + 8)
                            g = 0
                            for srcb in (qh[h % 2], kvh[h % 2]):
                                for sp4 in range(4):
                                    pa = p[2 * (g % 2)]
                                    pb = p[2 * (g % 2) + 1]
                                    if g >= 2:
                                        te.wait_ge(s_a.h,
                                                   b_sa + 12 * h + g - 1)
                                    sl = srcb[:][:, sp4 * 512:(sp4 + 1) * 512]
                                    te.matmul(pa[:], kc["cf"][:], sl,
                                              start=True, stop=True)
                                    ins = te.matmul(pb[:], kc["sf"][:], sl,
                                                    start=True, stop=True)
                                    mm.inc(ins, 1)
                                    g += 1
                            if h >= 1:
                                te.wait_ge(s_a.h, b_sa + 12 * h)
                            for sp4 in range(4):
                                pv = p[4 + sp4 % 2]
                                te.wait_ge(v_b.h, b_vb + 4 * h + sp4 + 1)
                                if sp4 >= 2:
                                    te.wait_ge(s_a.h,
                                               b_sa + 12 * h + 8 + sp4 - 1)
                                te.matmul(pv[:], kc["ci"][:],
                                          qvr[:][:, sp4 * 512:(sp4 + 1) * 512],
                                          start=True, stop=False)
                                ins = te.matmul(
                                    pv[:], kc["si"][:],
                                    qvi[:][:, sp4 * 512:(sp4 + 1) * 512],
                                    start=False, stop=True)
                                mm.inc(ins, 1)

                    @block.scalar
                    def _(s):
                        for h in range(16):
                            dsts = [(qfr, qfi), (qfr, qfi), (qfr, qfi),
                                    (qfr, qfi), (kr, ki), (kr, ki), (kr, ki),
                                    (kr, ki)]
                            for g in range(8):
                                sp4 = g % 4
                                pa = p[2 * (g % 2)]
                                pb = p[2 * (g % 2) + 1]
                                s.wait_ge(mm.h, b_mm + 12 * h + g + 1)
                                if h >= 1 and g == 0:
                                    s.wait_ge(v_b.h, b_vb + 4 * h)
                                da, db = dsts[g]
                                s.activation(
                                    da[:][:, sp4 * 512:(sp4 + 1) * 512],
                                    pa[:], AF.Copy)
                                ins = s.activation(
                                    db[:][:, sp4 * 512:(sp4 + 1) * 512],
                                    pb[:], AF.Copy)
                                s_a.inc(ins, 1)
                            for sp4 in range(4):
                                pv = p[4 + sp4 % 2]
                                s.wait_ge(mm.h, b_mm + 12 * h + 8 + sp4 + 1)
                                if sp4 == 0 and h >= 1:
                                    s.wait_ge(q_st.h, b_qst + 16 * (h - 1))
                                ins = s.activation(
                                    vals_st[h % 2][:][:,
                                                      sp4 * 512:(sp4 + 1) * 512],
                                    pv[:], AF.Copy)
                                s_a.inc(ins, 1)
                            s.wait_ge(s_a.h, b_sa + 12 * h + 12)
                            ins = s.dma_start(
                                out=vals_s[:, :, h, :]
                                .transpose([1, 0, 2]),
                                in_=vals_st[h % 2][:].rearrange(
                                    "p (a n) -> p a n", n=512))
                            q_st.inc(ins, 16)
                        s.wait_ge(q_st.h, b_qst + 16 * 16)

                    @block.vector
                    def _(v):
                        for h in range(16):
                            for sp4 in range(4):
                                sl = slice(sp4 * 512, (sp4 + 1) * 512)
                                v.wait_ge(s_a.h, b_sa + 12 * h + 5 + sp4)
                                v.tensor_tensor(t1[:][:, sl], qfr[:][:, sl],
                                                kr[:][:, sl], ALU.mult)
                                v.tensor_tensor(u1[:][:, sl], qfi[:][:, sl],
                                                ki[:][:, sl], ALU.mult)
                                v.tensor_tensor(qvr[:][:, sl], t1[:][:, sl],
                                                u1[:][:, sl], ALU.subtract)
                                v.tensor_tensor(t1[:][:, sl], qfr[:][:, sl],
                                                ki[:][:, sl], ALU.mult)
                                v.tensor_tensor(u1[:][:, sl], qfi[:][:, sl],
                                                kr[:][:, sl], ALU.mult)
                                ins = v.tensor_tensor(qvi[:][:, sl],
                                                      t1[:][:, sl],
                                                      u1[:][:, sl], ALU.add)
                                v_b.inc(ins, 1)

        def emit_wo():
            with ExitStack() as es:
                uid_ctr[0] += 1
                _u = uid_ctr[0]
                sb = lambda n, s, d: es.enter_context(
                    nc.sbuf_tensor(f"{n}_{_u}", s, d))
                wo_sb = sb("wo_w", [128, 32768], BF16)
                vals_sb = [sb(f"wo_v{i}", [128, 8192], BF16) for i in range(2)]
                xres = [sb(f"wo_xr{i}", [128, 2048], F32) for i in range(2)]
                ystg = [sb(f"wo_ys{i}", [128, 2048], F32) for i in range(2)]
                b_g, b_s, b_mm, b_va, b_qld, b_qst = (g_ld.v, s_ld.v, mm.v,
                                                      v_a.v, q_ld.v, q_st.v)

                def xtile(t_ap, tb, dsp):
                    return t_ap[tb * 512:(tb + 1) * 512,
                                dsp * 512:(dsp + 1) * 512].rearrange(
                                    "(a p) n -> p a n", p=128)

                with nc.Block() as block:
                    @block.gpsimd
                    def _(gp):
                        src = woT.rearrange("(c p) n -> p c n", p=128)
                        ins = gp.dma_start(
                            out=wo_sb[:].rearrange("p (c n) -> p c n", n=2048),
                            in_=src)
                        g_ld.inc(ins, 16)

                    @block.sync
                    def _(sync):
                        for tb in range(4):
                            if tb >= 2:
                                sync.wait_ge(mm.h, b_mm + 4 * (tb - 1))
                            src = vals_s[tb]
                            ins = sync.dma_start(
                                out=vals_sb[tb % 2][:].rearrange(
                                    "p (c n) -> p c n", n=512), in_=src)
                            s_ld.inc(ins, 16)

                    @block.tensor
                    def _(te):
                        te.wait_ge(g_ld.h, b_g + 16)
                        for idx in range(16):
                            tb, dsp = divmod(idx, 4)
                            if dsp == 0:
                                te.wait_ge(s_ld.h, b_s + 16 * (tb + 1))
                            if idx >= 2:
                                te.wait_ge(v_a.h, b_va + idx - 1)
                            pd = [p[(idx % 2) * 4 + tt] for tt in range(4)]
                            vv = vals_sb[tb % 2][:].rearrange(
                                "p (c n) -> p c n", n=512)
                            for oc in range(16):
                                for tt in range(4):
                                    ins = te.matmul(
                                        pd[tt][:],
                                        vv[:, oc, tt * 128:(tt + 1) * 128],
                                        wo_sb[:][:, oc * 2048 + dsp * 512:
                                                 oc * 2048 + (dsp + 1) * 512],
                                        start=(oc == 0), stop=(oc == 15))
                            mm.inc(ins, 1)

                    @block.scalar
                    def _(s):
                        for idx in range(16):
                            tb, dsp = divmod(idx, 4)
                            if idx == 0:
                                ins = s.dma_start(
                                    out=xres[0][:].rearrange(
                                        "p (a n) -> p a n", n=512),
                                    in_=xtile(x1_s, 0, 0))
                                q_ld.inc(ins, 16)
                            if idx + 1 < 16:
                                tb2, dsp2 = divmod(idx + 1, 4)
                                if idx >= 1:
                                    s.wait_ge(v_a.h, b_va + idx)
                                ins = s.dma_start(
                                    out=xres[(idx + 1) % 2][:].rearrange(
                                        "p (a n) -> p a n", n=512),
                                    in_=xtile(x1_s, tb2, dsp2))
                                q_ld.inc(ins, 16)
                            s.wait_ge(v_a.h, b_va + idx + 1)
                            ins = s.dma_start(
                                out=xtile(x2_s, tb, dsp),
                                in_=ystg[idx % 2][:].rearrange(
                                    "p (a n) -> p a n", n=512))
                            q_st.inc(ins, 16)
                        s.wait_ge(q_st.h, b_qst + 16 * 16)

                    @block.vector
                    def _(v):
                        for idx in range(16):
                            pd = [p[(idx % 2) * 4 + tt] for tt in range(4)]
                            v.wait_ge(mm.h, b_mm + idx + 1)
                            v.wait_ge(q_ld.h, b_qld + 16 * (idx + 1))
                            if idx >= 2:
                                v.wait_ge(q_st.h, b_qst + 16 * (idx - 1))
                            for tt in range(4):
                                ins = v.scalar_tensor_tensor(
                                    ystg[idx % 2][:][:,
                                                     tt * 512:(tt + 1) * 512],
                                    pd[tt][:], 1.0,
                                    xres[idx % 2][:][:,
                                                     tt * 512:(tt + 1) * 512],
                                    ALU.mult, ALU.add)
                            v_a.inc(ins, 1)
                            tbs, dss = divmod(idx, 4)
                            for a in range(4):
                                ysl = ystg[idx % 2][:][:,
                                                      a * 512:(a + 1) * 512]
                                col = (4 * tbs + a) * 4 + dss
                                v.reduce_sum(sa_psum[:, col:col + 1], ysl,
                                             axis=AX.X)
                                v.tensor_tensor(sa_scr[:], ysl, ysl,
                                                ALU.mult)
                                v.reduce_sum(sa_psq[:, col:col + 1],
                                             sa_scr[:], axis=AX.X)

        def emit_gather():
            b_cc = g_ld.v
            with nc.Block() as block:
                @block.gpsimd
                def _(gp):
                    pairs = [
                        (wg_loc.rearrange("a p c n -> a (p c n)"),
                         wg_s.rearrange("a p c n -> a (p c n)")),
                        (wu_loc.rearrange("a p c n -> a (p c n)"),
                         wu_s.rearrange("a p c n -> a (p c n)")),
                        (wd_loc.rearrange("a h p n -> a (h p n)"),
                         wd_s.rearrange("a h p n -> a (h p n)")),
                    ]
                    for i, (src_ap, dst_ap) in enumerate(pairs):
                        ins = gp.collective_compute(
                            "AllGather", ALU.bypass,
                            replica_groups=[list(range(8))],
                            ins=[src_ap], outs=[dst_ap])
                        g_ld.inc(ins, 1)
                    gp.wait_ge(g_ld.h, b_cc + 3)

        emit_dequant()
        with ExitStack() as g:
            xT1 = g.enter_context(nc.sbuf_tensor("xT1", [128, 32768], BF16))
            emit_ln_T(x_in, xT1, "ffn")
            if debug:
                with nc.Block() as block:
                    @block.sync
                    def _(sync):
                        ins = sync.dma_start(out=xt_dbg, in_=xT1[:])
                        s_ld.inc(ins, 16)
                        sync.wait_ge(s_ld.h, s_ld.v)
            emit_ffn_a(xT1)
        emit_ffn_b(x_in, x1_s, final=False, stats=True)
        with ExitStack() as g:
            xTa = g.enter_context(nc.sbuf_tensor("xTa", [128, 32768], BF16))
            emit_ln_T(x1_s, xTa, "nat", prestats=True)
            emit_attn_proj(xTa)
        emit_cumsum()
        emit_heads()
        emit_wo()
        with ExitStack() as g:
            xT2 = g.enter_context(nc.sbuf_tensor("xT2", [128, 32768], BF16))
            emit_ln_T(x2_s, xT2, "ffn", prestats=True)
            emit_ffn_a(xT2)
        emit_ffn_b(x2_s, y_out, final=True)

    return nc


def kernel(L0_attn_x, L0_attn_W_q, L0_attn_W_kv, L0_attn_W_o,
           L0_ffn_gate_W_packed, L0_ffn_gate_scales,
           L0_ffn_up_W_packed, L0_ffn_up_scales,
           L0_ffn_down_W_packed, L0_ffn_down_scales):
    x = np.ascontiguousarray(np.asarray(L0_attn_x, dtype=np.float32))

    def pk_T(a):
        a = np.ascontiguousarray(np.asarray(a, dtype=np.int32))
        return np.ascontiguousarray(a.view(np.uint8)[:, ::4].T)

    def wT(a):
        return np.ascontiguousarray(np.asarray(a, dtype=np.float32).T)

    common = {
        "wqT": wT(L0_attn_W_q),
        "wkvT": wT(L0_attn_W_kv),
        "woT": wT(L0_attn_W_o),
    }
    gpkT = pk_T(L0_ffn_gate_W_packed)
    upkT = pk_T(L0_ffn_up_W_packed)
    dpkT = pk_T(L0_ffn_down_W_packed)
    gscT = wT(L0_ffn_gate_scales)
    uscT = wT(L0_ffn_up_scales)
    dscT = wT(L0_ffn_down_scales)
    nc = build_nc()
    in_maps = [dict(
        common, x=np.ascontiguousarray(x[b]),
        gpk=np.ascontiguousarray(gpkT[:, b * 2048:(b + 1) * 2048]),
        upk=np.ascontiguousarray(upkT[:, b * 2048:(b + 1) * 2048]),
        dpk=np.ascontiguousarray(dpkT[b * 1024:(b + 1) * 1024, :]),
        gsc=np.ascontiguousarray(gscT[:, b * 2048:(b + 1) * 2048]),
        usc=np.ascontiguousarray(uscT[:, b * 2048:(b + 1) * 2048]),
        dsc=np.ascontiguousarray(dscT[b * 64:(b + 1) * 64, :]),
    ) for b in range(8)]
    res = run_bass_kernel_spmd(nc, in_maps, core_ids=list(range(8)))
    return np.stack([np.asarray(res.results[b]["y"]) for b in range(8)],
                    axis=0)


# version 58
# speedup vs baseline: 1.0016x; 1.0016x over previous
import math
import numpy as np
import ml_dtypes
from contextlib import ExitStack

import concourse.bass as bass
import concourse.mybir as mybir
from concourse.bass_utils import run_bass_kernel_spmd

F32 = mybir.dt.float32
BF16 = mybir.dt.bfloat16
I32 = mybir.dt.int32
U8 = mybir.dt.uint8
AF = mybir.ActivationFunctionType
ALU = mybir.AluOpType
AX = mybir.AxisListType

S = 2048
D = 2048
M = 16384
KBITS = 0x3F000000
C0 = 0.7978845608028654
SQC = math.sqrt(C0 * 0.044715)


class Sem:
    def __init__(self, nc, name):
        self.h = nc.alloc_semaphore(name)
        self.v = 0

    def inc(self, instr, n=1):
        instr.then_inc(self.h, n)
        self.v += n


def _consts():
    i = np.arange(128)
    ang = 2.0 * np.pi * (np.outer(i, i) % 128) / 128.0
    bf = ml_dtypes.bfloat16
    return {
        "ident": np.eye(128, dtype=np.float32).astype(bf),
        "cf": np.cos(ang).astype(np.float32).astype(bf),
        "sf": (-np.sin(ang)).astype(np.float32).astype(bf),
        "ci": (np.cos(ang) / 128.0).astype(np.float32).astype(bf),
        "si": (-np.sin(ang) / 128.0).astype(np.float32).astype(bf),
        "tri": np.triu(np.ones((128, 128), np.float32)).astype(bf),
        "ones": np.ones((128, 128), np.float32).astype(bf),
    }


def build_nc(debug=False, kv_raw=False):
    nc = bass.Bass("TRN2", target_bir_lowering=False)
    uid_ctr = [0]

    x_in = nc.dram_tensor("x", [S, D], F32, kind="ExternalInput").ap()
    wqT = nc.dram_tensor("wqT", [D, D], F32, kind="ExternalInput").ap()
    wkvT = nc.dram_tensor("wkvT", [D, D], F32, kind="ExternalInput").ap()
    woT = nc.dram_tensor("woT", [D, D], F32, kind="ExternalInput").ap()
    gpk = nc.dram_tensor("gpk", [1024, 2048], U8, kind="ExternalInput").ap()
    upk = nc.dram_tensor("upk", [1024, 2048], U8, kind="ExternalInput").ap()
    dpk = nc.dram_tensor("dpk", [1024, 2048], U8, kind="ExternalInput").ap()
    gsc = nc.dram_tensor("gsc", [64, 2048], F32, kind="ExternalInput").ap()
    usc = nc.dram_tensor("usc", [64, 2048], F32, kind="ExternalInput").ap()
    dsc = nc.dram_tensor("dsc", [64, 2048], F32, kind="ExternalInput").ap()
    y_out = nc.dram_tensor("y", [S, D], F32, kind="ExternalOutput").ap()

    def scratch(name, shape, dtype):
        kind = "ExternalOutput" if debug else "Internal"
        return nc.dram_tensor(name, shape, dtype, kind=kind).ap()

    wg_s = nc.dram_tensor("wg_s", [64, 128, 16, 256], BF16, kind="Internal",
                          addr_space="Shared").ap()
    wu_s = nc.dram_tensor("wu_s", [64, 128, 16, 256], BF16, kind="Internal",
                          addr_space="Shared").ap()
    wd_s = nc.dram_tensor("wd_s", [32, 128, 16, 512], BF16, kind="Internal",
                          addr_space="Shared").ap()
    wg_loc = nc.dram_tensor("wg_loc", [8, 128, 16, 256], BF16, kind="Internal").ap()
    wu_loc = nc.dram_tensor("wu_loc", [8, 128, 16, 256], BF16, kind="Internal").ap()
    wd_loc = nc.dram_tensor("wd_loc", [4, 128, 16, 512], BF16, kind="Internal").ap()
    wsac = nc.dram_tensor("wsac", [8, 128, 4, 256], BF16, kind="Internal").ap()
    inter_s = scratch("inter_s", [4, 8, 128, 16, 512], BF16)
    x1_s = scratch("x1_s", [S, D], F32)
    x2_s = scratch("x2_s", [S, D], F32)
    qnT_s = scratch("qnT_s", [D, S], BF16)
    xt_dbg = scratch("xt_dbg", [128, 32768], BF16) if debug else None
    kvn_s = scratch("kvn_s", [4, 128, 16, 512], BF16)
    q_raw_s = nc.dram_tensor("q_raw_s", [S, D], BF16, kind="Internal").ap()
    kv_raw_s = nc.dram_tensor("kv_raw_s", [S, D], BF16, kind="Internal").ap()
    kvcT_s = scratch("kvcT_s", [D, S], BF16)
    vals_s = scratch("vals_s", [4, 128, 16, 512], BF16)

    cns = _consts()
    c_aps = {k: nc.inline_tensor(v, f"c_{k}").ap() for k, v in cns.items()}

    with ExitStack() as top:
        sbt = lambda n, s, d: top.enter_context(nc.sbuf_tensor(n, s, d))
        kc = {k: sbt(f"k_{k}", [128, 128], BF16) for k in cns}
        k_eps = sbt("k_eps", [128, 1], F32)
        sa_psum = sbt("sa_psum", [128, 64], F32)
        sa_psq = sbt("sa_psq", [128, 64], F32)
        sa_scr = sbt("sa_scr", [128, 512], F32)
        nc.const_aps.register(F32, 1e-5, k_eps[:, 0:1])
        p = [top.enter_context(nc.psum_tensor(f"pb{i}", [128, 512], F32))
             for i in range(8)]

        s_ld = Sem(nc, "s_ld")
        s2_st = Sem(nc, "s2_st")
        g_ld = Sem(nc, "g_ld")
        q_ld = Sem(nc, "q_ld")
        q_st = Sem(nc, "q_st")
        v_a = Sem(nc, "v_a")
        v_b = Sem(nc, "v_b")
        s_a = Sem(nc, "s_a")
        mm = Sem(nc, "mm")
        t_tp = Sem(nc, "t_tp")
        st_g = Sem(nc, "st_g")
        st_u = Sem(nc, "st_u")
        i_ld = Sem(nc, "i_ld")

        all_sems = [s_ld, s2_st, g_ld, q_ld, q_st, v_a, v_b, s_a, mm, t_tp,
                    st_g, st_u, i_ld]
        with nc.Block() as block:
            @block.gpsimd
            def _(gp):
                for sm in all_sems:
                    gp.sem_clear(sm.h)

        with nc.Block() as block:
            @block.sync
            def _(sync):
                for k in cns:
                    ins = sync.dma_start(out=kc[k][:], in_=c_aps[k])
                    s_ld.inc(ins, 16)

            @block.vector
            def _(v):
                v.memset(k_eps[:], 1e-5)
                v.wait_ge(s_ld.h, s_ld.v)

        def emit_dequant():
            with ExitStack() as es:
                uid_ctr[0] += 1
                _u = uid_ctr[0]
                sb = lambda n, s, d: es.enter_context(
                    nc.sbuf_tensor(f"{n}_{_u}", s, d))
                wp = [sb(f"dq_wp{i}", [128, 2048], I32) for i in range(2)]
                sexp = [sb(f"dq_sc{i}", [128, 2048], F32) for i in range(2)]
                mag = sb("dq_mag", [128, 4096], F32)
                corr = sb("dq_corr", [128, 4096], F32)
                sbits = sb("dq_sb", [128, 4096], I32)
                ssgn = sb("dq_ssgn", [128, 4096], F32)
                val = [sb(f"dq_val{i}", [128, 4096], BF16) for i in range(2)]
                tv = sb("dq_tv", [1, 8], F32)
                tsc = sb("dq_ts", [1, 8], F32)

                jobs = []
                for jc in (0, 1):
                    src = gpk[jc * 128:(jc + 1) * 128, :]
                    scs = gsc[jc * 8:(jc + 1) * 8, :]
                    dlo = wsac[0:8, :, 2 * jc, :].transpose([1, 0, 2])
                    dhi = wsac[0:8, :, 2 * jc + 1, :].transpose([1, 0, 2])
                    jobs.append((src, scs, dlo, dhi))
                for w_pk, w_sc, w_dst in ((gpk, gsc, wg_loc),
                                          (upk, usc, wu_loc)):
                    for jc in range(8):
                        src = w_pk[jc * 128:(jc + 1) * 128, :]
                        scs = w_sc[jc * 8:(jc + 1) * 8, :]
                        dlo = w_dst[0:8, :, 2 * jc, :].transpose([1, 0, 2])
                        dhi = w_dst[0:8, :, 2 * jc + 1, :].transpose([1, 0, 2])
                        jobs.append((src, scs, dlo, dhi))
                for jl in range(8):
                    src = dpk[jl * 128:(jl + 1) * 128, :]
                    scs = dsc[jl * 8:(jl + 1) * 8, :]
                    dlo = wd_loc[:, :, 2 * jl, :].transpose([1, 0, 2])
                    dhi = wd_loc[:, :, 2 * jl + 1, :].transpose([1, 0, 2])
                    jobs.append((src, scs, dlo, dhi))
                NJ = len(jobs)
                b_g, b_s, b_va, b_vb, b_sa, b_qst = (g_ld.v, s_ld.v, v_a.v,
                                                     v_b.v, s_a.v, q_st.v)
                b_stg, b_stu = st_g.v, st_u.v

                def st_sem(j):
                    if 2 <= j <= 9:
                        return st_g
                    if 10 <= j <= 17:
                        return st_u
                    return q_st

                def st_waits(eng, j):
                    nq = 32 * (min(j + 1, 2) + max(0, j - 17))
                    ng = 32 * max(0, min(j, 9) - 1)
                    nu = 32 * max(0, min(j, 17) - 9)
                    if nq:
                        eng.wait_ge(q_st.h, b_qst + nq)
                    if ng:
                        eng.wait_ge(st_g.h, b_stg + ng)
                    if nu:
                        eng.wait_ge(st_u.h, b_stu + nu)

                with nc.Block() as block:
                    @block.gpsimd
                    def _(gp):
                        gathers = {
                            11: (10, wg_loc.rearrange("a p c n -> a (p c n)"),
                                 wg_s.rearrange("a p c n -> a (p c n)")),
                            21: (18, wu_loc.rearrange("a p c n -> a (p c n)"),
                                 wu_s.rearrange("a p c n -> a (p c n)")),
                            25: (26, wd_loc.rearrange("h p c n -> h (p c n)"),
                                 wd_s.rearrange("a p c n -> a (p c n)")),
                        }
                        for i in range(NJ):
                            if i >= 2:
                                gp.wait_ge(v_b.h, b_vb + i - 1)
                            ins = gp.dma_start(out=wp[i % 2][:], in_=jobs[i][0])
                            g_ld.inc(ins, 16)
                        for i in (11, 21, 25):
                            nj, src_ap, dst_ap = gathers[i]
                            sm, tgt = {
                                11: (st_g, b_stg + 256),
                                21: (st_u, b_stu + 256),
                                25: (q_st, b_qst + 320)}[i]
                            gp.wait_ge(sm.h, tgt)
                            ins = gp.collective_compute(
                                "AllGather", ALU.bypass,
                                replica_groups=[list(range(8))],
                                ins=[src_ap], outs=[dst_ap])
                            g_ld.inc(ins, 1)
                        gp.wait_ge(g_ld.h, g_ld.v)

                    @block.sync
                    def _(sync):
                        for i in range(NJ):
                            if i >= 2:
                                sync.wait_ge(v_b.h, b_vb + i - 1)
                            scs = jobs[i][1]
                            bcast = scs.unsqueeze(1).broadcast_to(
                                [8, 16, scs.shape[1]])
                            ins = sync.dma_start(out=sexp[i % 2][:], in_=bcast)
                            s_ld.inc(ins, 16)

                    @block.vector
                    def _(v):
                        magi = mag[:].bitcast(I32)
                        ssgni = ssgn[:].bitcast(I32)
                        for i in range(NJ):
                            w = wp[i % 2][:]
                            sci = sexp[i % 2][:].bitcast(I32)
                            v.wait_ge(g_ld.h, b_g + 16 * (i + 1))
                            v.wait_ge(s_ld.h, b_s + 16 * (i + 1))
                            v.tensor_scalar(magi[:, 0:2048], w, 7, 22,
                                            ALU.bitwise_and,
                                            ALU.logical_shift_left)
                            v.tensor_scalar(magi[:, 2048:4096], w, 0x70, 18,
                                            ALU.bitwise_and,
                                            ALU.logical_shift_left)
                            v.tensor_scalar(magi[:, 0:2048], magi[:, 0:2048],
                                            KBITS, None, ALU.add)
                            v.tensor_scalar(magi[:, 2048:4096],
                                            magi[:, 2048:4096],
                                            KBITS, None, ALU.add)
                            ins = v.tensor_copy(tv[0:1, 0:2], mag[0:1, 0:2])
                            v_a.inc(ins, 1)
                            v.wait_ge(s_a.h, b_sa + i + 1)
                            v.tensor_tensor(mag[:], mag[:], corr[:],
                                            ALU.subtract)
                            v.tensor_scalar(sbits[:, 0:2048], w, 8, 28,
                                            ALU.bitwise_and,
                                            ALU.logical_shift_left)
                            v.tensor_scalar(sbits[:, 2048:4096], w, 0x80, 24,
                                            ALU.bitwise_and,
                                            ALU.logical_shift_left)
                            v.tensor_tensor(ssgni[:, 0:2048], sci,
                                            sbits[:, 0:2048], ALU.bitwise_or)
                            v.tensor_tensor(ssgni[:, 2048:4096], sci,
                                            sbits[:, 2048:4096], ALU.bitwise_or)
                            if i >= 2:
                                st_waits(v, i - 2)
                            v.tensor_tensor(val[i % 2][:], mag[:],
                                            ssgn[:], ALU.mult)
                            ins = v.tensor_copy(tv[0:1, 2:4], mag[0:1, 0:2])
                            v_b.inc(ins, 1)

                    @block.scalar
                    def _(s):
                        for i in range(NJ):
                            s.wait_ge(v_a.h, b_va + i + 1)
                            s.activation(corr[:], mag[:], AF.Relu,
                                         bias=1.0, scale=-1.0)
                            ins = s.copy(tsc[0:1, 0:2], corr[0:1, 0:2])
                            s_a.inc(ins, 1)
                            s.wait_ge(v_b.h, b_vb + i + 1)
                            _, _, dlo, dhi = jobs[i]
                            ncol = dlo.shape[2]
                            vv = val[i % 2][:]
                            lo = vv[:, 0:2048].rearrange("p (a n) -> p a n",
                                                         n=ncol)
                            hi = vv[:, 2048:4096].rearrange("p (a n) -> p a n",
                                                            n=ncol)
                            ins = s.dma_start(out=dlo, in_=lo)
                            st_sem(i).inc(ins, 16)
                            ins = s.dma_start(out=dhi, in_=hi)
                            st_sem(i).inc(ins, 16)
                        st_waits(s, NJ - 1)

        def emit_ln_T(src, xT, perm, prestats=False):
            with ExitStack() as es:
                uid_ctr[0] += 1
                _u = uid_ctr[0]
                sb = lambda n, s, d: es.enter_context(
                    nc.sbuf_tensor(f"{n}_{_u}", s, d))
                xr = [sb(f"ln_x{i}", [128, 2048], F32) for i in range(2)]
                lnr = [sb(f"ln_o{i}", [128, 2048], BF16) for i in range(2)]
                sums = sb("ln_sum", [128, 16], F32)
                sqs = sb("ln_sqs", [128, 16], F32)
                sq_scr = sb("ln_scr", [128, 2048], F32)
                nm = sb("ln_nm", [128, 16], F32)
                m2 = sb("ln_m2", [128, 16], F32)
                var = sb("ln_var", [128, 16], F32)
                sd = sb("ln_sd", [128, 16], F32)
                rstd = sb("ln_rstd", [128, 16], F32)
                xc = sb("ln_xc", [128, 2048], F32)
                tv = sb("ln_tv", [1, 8], F32)
                tsc = sb("ln_ts", [1, 8], F32)

                def cslice(ln_ap, dc):
                    if perm == "ffn":
                        jc, par = dc >> 1, dc & 1
                        base = 256 * jc + par
                        return ln_ap[:, base:base + 255:2]
                    return ln_ap[:, dc * 128:(dc + 1) * 128]

                b_s, b_v, b_sa = s_ld.v, v_b.v, s_a.v
                if not prestats:
                    with nc.Block() as block:
                        @block.sync
                        def _(sync):
                            for t in range(16):
                                if t >= 2:
                                    sync.wait_ge(s_a.h, b_sa + (t - 2) + 1)
                                    sync.wait_ge(v_b.h, b_v + (t - 2) + 1)
                                ins = sync.dma_start(
                                    out=xr[t % 2][:],
                                    in_=src[t * 128:(t + 1) * 128, :])
                                s_ld.inc(ins, 16)

                        @block.scalar
                        def _(s):
                            for t in range(16):
                                s.wait_ge(s_ld.h, b_s + 16 * (t + 1))
                                s.activation(sq_scr[:], xr[t % 2][:],
                                             AF.Square,
                                             accum_out=sqs[:, t:t + 1])
                                ins = s.copy(tsc[0:1, 0:2], sqs[0:1, 0:2])
                                s_a.inc(ins, 1)

                        @block.vector
                        def _(v):
                            for t in range(16):
                                v.wait_ge(s_ld.h, b_s + 16 * (t + 1))
                                ins = v.reduce_sum(sums[:, t:t + 1],
                                                   xr[t % 2][:],
                                                   axis=AX.X)
                                v_b.inc(ins, 1)

                if prestats:
                    with nc.Block() as block:
                        @block.vector
                        def _(v):
                            v.tensor_tensor(nm[:], sa_psum[:, 0::4],
                                            sa_psum[:, 1::4], ALU.add)
                            v.tensor_tensor(m2[:], sa_psum[:, 2::4],
                                            sa_psum[:, 3::4], ALU.add)
                            v.tensor_tensor(var[:], sa_psq[:, 0::4],
                                            sa_psq[:, 1::4], ALU.add)
                            v.tensor_tensor(sd[:], sa_psq[:, 2::4],
                                            sa_psq[:, 3::4], ALU.add)
                            for _pad in range(8):
                                v.tensor_copy(lnr[0][:][:, 0:512],
                                              xc[:][:, 0:512])
                            v.tensor_tensor(sums[:], nm[:], m2[:], ALU.add)
                            v.tensor_tensor(sqs[:], var[:], sd[:], ALU.add)
                b_v, b_sa = v_b.v, s_a.v
                with nc.Block() as block:
                    @block.vector
                    def _(v):
                        ins = v.tensor_scalar(nm[:], sums[:], -1.0 / D, None,
                                              ALU.mult)
                        v_b.inc(ins, 1)
                        v.wait_ge(s_a.h, b_sa + 1)
                        ins = v.scalar_tensor_tensor(
                            var[:], sqs[:], 1.0 / D, m2[:],
                            ALU.mult, ALU.subtract)
                        v_b.inc(ins, 1)
                        v.wait_ge(s_a.h, b_sa + 2)
                        ins = v.reciprocal(rstd[:], sd[:])
                        v_b.inc(ins, 1)

                    @block.scalar
                    def _(s):
                        s.wait_ge(v_b.h, b_v + 1)
                        ins = s.activation(m2[:], nm[:], AF.Square)
                        s_a.inc(ins, 1)
                        s.wait_ge(v_b.h, b_v + 2)
                        ins = s.activation(sd[:], var[:], AF.Sqrt, bias=1e-5)
                        s_a.inc(ins, 1)

                b_s, b_v, b_va, b_tp = s_ld.v, v_b.v, v_a.v, t_tp.v
                with nc.Block() as block:
                    @block.sync
                    def _(sync):
                        for t in range(16):
                            if t >= 2:
                                sync.wait_ge(v_b.h, b_v + (t - 2) + 1)
                            ins = sync.dma_start(
                                out=xr[t % 2][:],
                                in_=src[t * 128:(t + 1) * 128, :])
                            s_ld.inc(ins, 16)

                    @block.vector
                    def _(v):
                        xTv = xT[:].rearrange("p (c n) -> p c n", n=2048)
                        p6b = p[6][:].bitcast(BF16)
                        p7b = p[7][:].bitcast(BF16)
                        for t in range(16):
                            c = t % 2
                            v.wait_ge(s_ld.h, b_s + 16 * (t + 1))
                            v.tensor_tensor(
                                xc[:], xr[c][:],
                                nm[:, t:t + 1].broadcast_to([128, 2048]),
                                ALU.add)
                            if t >= 2:
                                v.wait_ge(t_tp.h, b_tp + 2 * (t - 2) + 2)
                            v.tensor_tensor(
                                lnr[c][:], xc[:],
                                rstd[:, t:t + 1].broadcast_to([128, 2048]),
                                ALU.mult)
                            ins = v.tensor_copy(tv[0:1, 4:6], tv[0:1, 0:2])
                            v_b.inc(ins, 1)
                            v.wait_ge(t_tp.h, b_tp + 2 * t + 1)
                            v.tensor_copy(
                                xTv[:, 0:8, t * 128:(t + 1) * 128],
                                p6b.rearrange("p (c n) -> p c n", n=128))
                            v.wait_ge(t_tp.h, b_tp + 2 * t + 2)
                            ins = v.tensor_copy(
                                xTv[:, 8:16, t * 128:(t + 1) * 128],
                                p7b.rearrange("p (c n) -> p c n", n=128))
                            v_a.inc(ins, 1)

                    @block.tensor
                    def _(te):
                        p6b = p[6][:].bitcast(BF16)
                        p7b = p[7][:].bitcast(BF16)
                        for t in range(16):
                            c = t % 2
                            te.wait_ge(v_b.h, b_v + t + 1)
                            if t >= 1:
                                te.wait_ge(v_a.h, b_va + t)
                            for dc in range(8):
                                ins = te.transpose(
                                    p6b[:, dc * 128:(dc + 1) * 128],
                                    cslice(lnr[c][:], dc), kc["ident"][:])
                            t_tp.inc(ins, 1)
                            for dc in range(8, 16):
                                ins = te.transpose(
                                    p7b[:, (dc - 8) * 128:(dc - 7) * 128],
                                    cslice(lnr[c][:], dc), kc["ident"][:])
                            t_tp.inc(ins, 1)

        def emit_ffn_a(xT):
            with ExitStack() as es:
                uid_ctr[0] += 1
                _u = uid_ctr[0]
                sb = lambda n, s, d: es.enter_context(
                    nc.sbuf_tensor(f"{n}_{_u}", s, d))
                wgb = [sb(f"fa_wg{i}", [128, 4096], BF16) for i in range(2)]
                wub = [sb(f"fa_wu{i}", [128, 4096], BF16) for i in range(2)]
                g2c = [sb(f"fa_g2c{i}", [128, 512], F32) for i in range(2)]
                arg = [sb(f"fa_arg{i}", [128, 512], F32) for i in range(2)]
                tnh = [sb(f"fa_tnh{i}", [128, 512], F32) for i in range(2)]
                gl = [[sb(f"fa_gl{s_}_{i}", [128, 512], F32)
                       for i in range(4)] for s_ in range(2)]
                stg = [sb(f"fa_st{i}", [128, 512], BF16) for i in range(8)]
                tv = sb("fa_tv", [1, 8], F32)
                tsc = sb("fa_ts", [1, 8], F32)
                b_s, b_mm, b_v, b_sa, b_qst = (s_ld.v, mm.v, v_a.v, s_a.v,
                                               q_st.v)

                def slot(g2, tq):
                    return (4 * g2 + tq) % 8

                with nc.Block() as block:
                    @block.sync
                    def _(sync):
                        for b in range(64):
                            if b >= 2:
                                sync.wait_ge(mm.h, b_mm + 16 * (b - 2) + 8)
                            ins = sync.dma_start(
                                out=wgb[b % 2][:].rearrange(
                                    "p (c n) -> p c n", n=256),
                                in_=wg_s[b])
                            s_ld.inc(ins, 16)
                            if b >= 2:
                                sync.wait_ge(mm.h, b_mm + 16 * (b - 2) + 16)
                            ins = sync.dma_start(
                                out=wub[b % 2][:].rearrange(
                                    "p (c n) -> p c n", n=256),
                                in_=wu_s[b])
                            s_ld.inc(ins, 16)

                    @block.tensor
                    def _(te):
                        for b in range(64):
                            for ph in range(4):
                                sub = ph % 2
                                off = 4 * sub
                                wblk = wgb[b % 2] if ph < 2 else wub[b % 2]
                                if ph == 0:
                                    te.wait_ge(s_ld.h, b_s + 32 * (b + 1))
                                    if b >= 1:
                                        te.wait_ge(v_a.h, b_v + 24 * b)
                                elif ph == 1:
                                    if b >= 1:
                                        te.wait_ge(v_a.h, b_v + 24 * b + 4)
                                elif ph == 2:
                                    te.wait_ge(v_a.h, b_v + 24 * b + 12)
                                    te.wait_ge(s_a.h, b_sa + 16 * b + 8)
                                else:
                                    te.wait_ge(v_a.h, b_v + 24 * b + 20)
                                    te.wait_ge(s_a.h, b_sa + 16 * b + 16)
                                for dc in range(16):
                                    for tq in range(4):
                                        ins = te.matmul(
                                            p[off + tq][:],
                                            wblk[:][:, dc * 256 + sub:
                                                    dc * 256 + 256:2],
                                            xT[:][:, dc * 2048 + tq * 512:
                                                  dc * 2048 + (tq + 1) * 512],
                                            start=(dc == 0), stop=(dc == 15))
                                        if dc == 15:
                                            mm.inc(ins, 1)

                    @block.scalar
                    def _(s):
                        def gl_scalar(b, sub):
                            for tq in range(2):
                                pass
                            for tq in range(4):
                                s.wait_ge(mm.h,
                                          b_mm + 16 * b + 4 * sub + tq + 1)
                                s.activation(g2c[tq % 2][:],
                                             p[4 * sub + tq][:], AF.Square,
                                             scale=SQC)
                                ins = s.copy(tsc[0:1, 0:2],
                                             g2c[tq % 2][0:1, 0:2])
                                s_a.inc(ins, 1)
                                s.wait_ge(v_a.h, b_v + 24 * b + 4 + 8 * sub +
                                          2 * tq + 1)
                                s.activation(tnh[tq % 2][:], arg[tq % 2][:],
                                             AF.Tanh)
                                ins = s.copy(tsc[0:1, 2:4],
                                             tnh[tq % 2][0:1, 0:2])
                                s_a.inc(ins, 1)

                        def st_inter(b, sub, vbase):
                            cc = 2 * b + sub
                            for tq in range(4):
                                s.wait_ge(v_a.h, vbase + tq + 1)
                                ins = s.dma_start(
                                    out=inter_s[tq, cc // 16, :, cc % 16, :],
                                    in_=stg[slot(cc, tq)][:])
                                q_st.inc(ins, 16)

                        for b in range(64):
                            if b >= 1:
                                st_inter(b - 1, 1, b_v + 24 * b)
                            gl_scalar(b, 0)
                            gl_scalar(b, 1)
                            st_inter(b, 0, b_v + 24 * b + 20)
                        st_inter(63, 1, b_v + 24 * 64)
                        s.wait_ge(q_st.h, b_qst + 16 * 512)

                    @block.vector
                    def _(v):
                        def inter_ops(b, sub, mmbase):
                            cc = 2 * b + sub
                            for tq in range(4):
                                v.wait_ge(mm.h, mmbase + tq + 1)
                                g2p = cc - 2
                                if g2p >= 0:
                                    v.wait_ge(q_st.h, b_qst +
                                              16 * (4 * g2p + tq + 1))
                                v.tensor_tensor(stg[slot(cc, tq)][:],
                                                gl[sub][tq][:],
                                                p[4 * sub + tq][:], ALU.mult)
                                ins = v.tensor_copy(
                                    tv[0:1, 0:2],
                                    stg[slot(cc, tq)][0:1, 0:2])
                                v_a.inc(ins, 1)

                        def gl_vector(b, sub):
                            for tq in range(4):
                                v.wait_ge(s_a.h, b_sa + 16 * b + 8 * sub +
                                          2 * tq + 1)
                                v.scalar_tensor_tensor(
                                    arg[tq % 2][:], g2c[tq % 2][:], C0,
                                    p[4 * sub + tq][:], ALU.add, ALU.mult)
                                ins = v.tensor_copy(tv[0:1, 2:4],
                                                    arg[tq % 2][0:1, 0:2])
                                v_a.inc(ins, 1)
                                v.wait_ge(s_a.h, b_sa + 16 * b + 8 * sub +
                                          2 * tq + 2)
                                v.scalar_tensor_tensor(
                                    gl[sub][tq][:], tnh[tq % 2][:], 1.0,
                                    p[4 * sub + tq][:], ALU.add, ALU.mult)
                                ins = v.tensor_copy(tv[0:1, 4:6],
                                                    gl[sub][tq][0:1, 0:2])
                                v_a.inc(ins, 1)

                        for b in range(64):
                            if b >= 1:
                                inter_ops(b - 1, 1,
                                          b_mm + 16 * (b - 1) + 12)
                            else:
                                for _d in range(4):
                                    ins = v.tensor_copy(tv[0:1, 6:8],
                                                        tv[0:1, 0:2])
                                    v_a.inc(ins, 1)
                            gl_vector(b, 0)
                            gl_vector(b, 1)
                            inter_ops(b, 0, b_mm + 16 * b + 8)
                        inter_ops(63, 1, b_mm + 16 * 63 + 12)

        def emit_ffn_b(x_src, dst, final=False, stats=False):
            with ExitStack() as es:
                uid_ctr[0] += 1
                _u = uid_ctr[0]
                sb = lambda n, s, d: es.enter_context(
                    nc.sbuf_tensor(f"{n}_{_u}", s, d))
                inter_a = sb("fb_inta", [128, 32768], BF16)
                inter_b = sb("fb_intb", [128, 32768], BF16)
                wdr = [sb(f"fb_wd{i}", [128, 8192], BF16) for i in range(2)]
                xres = [sb(f"fb_xr{i}", [128, 2048], F32) for i in range(2)]
                ystg = [sb(f"fb_ys{i}", [128, 2048], F32) for i in range(2)]
                tv = sb("fb_tv", [1, 8], F32)
                b_s, b_mm, b_va, b_qld, b_qst = (s_ld.v, mm.v, v_a.v, q_ld.v,
                                                 q_st.v)

                def xtile(t_ap, tb, hs):
                    return t_ap[tb * 512:(tb + 1) * 512,
                                hs * 512:(hs + 1) * 512].rearrange(
                                    "(a p) n -> p a n", p=128)

                ld_plan = []
                ipos = {}
                wpos = {}
                _np = [0, 0]

                def plan_eighth(tb, e):
                    gate = None
                    if tb >= 1:
                        gate = 8 * (4 * (tb - 1) + 3) + e + 1
                    ld_plan.append(("i", tb, e, gate))
                    _np[0] += 1
                    ipos[(tb, e)] = _np[0]

                def plan_wd(tb, hs, g):
                    gs = (tb * 4 + hs) * 8 + g
                    gate = gs - 1 if gs >= 2 else None
                    ld_plan.append(("w", tb, hs, g, gate))
                    _np[1] += 1
                    wpos[(tb, hs, g)] = _np[1]

                for k in range(8):
                    plan_eighth(0, k)
                    plan_wd(0, 0, k)
                for tb in range(4):
                    for hs in range(3):
                        if (tb, hs) != (0, 0):
                            for g in range(8):
                                plan_wd(tb, hs, g)
                    if tb < 3:
                        plan_wd(tb, 3, 0)
                        plan_wd(tb, 3, 1)
                        for k in range(6):
                            plan_eighth(tb + 1, k)
                            plan_wd(tb, 3, k + 2)
                        plan_eighth(tb + 1, 6)
                        plan_eighth(tb + 1, 7)
                    else:
                        for g in range(8):
                            plan_wd(tb, 3, g)

                b_il = i_ld.v
                with nc.Block() as block:
                    @block.gpsimd
                    def _(gp):
                        for item in ld_plan:
                            if item[0] != "i":
                                continue
                            _, tb, e, gate = item
                            if gate is not None:
                                gp.wait_ge(mm.h, b_mm + gate)
                            half = inter_a if e < 4 else inter_b
                            ee = e % 4
                            ins = gp.dma_start(
                                out=half[:][:, ee * 8192:(ee + 1) * 8192]
                                .rearrange("p (c n) -> p c n", n=512),
                                in_=inter_s[tb, e])
                            i_ld.inc(ins, 16)

                    @block.sync
                    def _(sync):
                        for item in ld_plan:
                            if item[0] != "w":
                                continue
                            _, tb, hs, g, gate = item
                            if gate is not None:
                                sync.wait_ge(mm.h, b_mm + gate)
                            gs = (tb * 4 + hs) * 8 + g
                            ins = sync.dma_start(
                                out=wdr[gs % 2][:].rearrange(
                                    "p (c n) -> p c n", n=512),
                                in_=wd_s[4 * g + hs])
                            s_ld.inc(ins, 16)

                    @block.tensor
                    def _(te):
                        for tb in range(4):
                            for hs in range(4):
                                idx = 4 * tb + hs
                                if idx >= 2:
                                    te.wait_ge(v_a.h, b_va + idx - 1)
                                pd = [p[(idx % 2) * 4 + tt] for tt in range(4)]
                                for c in range(128):
                                    g = c // 16
                                    gs = (tb * 4 + hs) * 8 + g
                                    if c % 16 == 0:
                                        te.wait_ge(i_ld.h,
                                                   b_il + 16 * ipos[(tb, g)])
                                        te.wait_ge(s_ld.h,
                                                   b_s + 16 * wpos[(tb, hs,
                                                                    g)])
                                    ibuf = inter_a if c < 64 else inter_b
                                    cc = c % 64
                                    for tt in range(4):
                                        ins = te.matmul(
                                            pd[tt][:],
                                            ibuf[:][:,
                                                    cc * 512 + tt * 128:
                                                    cc * 512 + (tt + 1) * 128],
                                            wdr[gs % 2][:][:,
                                                           (c % 16) * 512:
                                                           (c % 16 + 1) * 512],
                                            start=(c == 0), stop=(c == 127))
                                    if c % 16 == 15:
                                        mm.inc(ins, 1)

                    @block.scalar
                    def _(s):
                        for idx in range(16):
                            tb, hs = divmod(idx, 4)
                            if idx == 0:
                                ins = s.dma_start(
                                    out=xres[0][:].rearrange(
                                        "p (a n) -> p a n", n=512),
                                    in_=xtile(x_src, 0, 0))
                                q_ld.inc(ins, 16)
                            if idx + 1 < 16:
                                tb2, hs2 = divmod(idx + 1, 4)
                                if idx >= 1:
                                    s.wait_ge(v_a.h, b_va + idx)
                                ins = s.dma_start(
                                    out=xres[(idx + 1) % 2][:].rearrange(
                                        "p (a n) -> p a n", n=512),
                                    in_=xtile(x_src, tb2, hs2))
                                q_ld.inc(ins, 16)
                            s.wait_ge(v_a.h, b_va + idx + 1)
                            ins = s.dma_start(
                                out=xtile(dst, tb, hs),
                                in_=ystg[idx % 2][:].rearrange(
                                    "p (a n) -> p a n", n=512))
                            q_st.inc(ins, 16)
                        s.wait_ge(q_st.h, b_qst + 16 * 16)

                    @block.vector
                    def _(v):
                        for idx in range(16):
                            pd = [p[(idx % 2) * 4 + tt] for tt in range(4)]
                            v.wait_ge(mm.h, b_mm + 8 * (idx + 1))
                            v.wait_ge(q_ld.h, b_qld + 16 * (idx + 1))
                            if idx >= 2:
                                v.wait_ge(q_st.h, b_qst + 16 * (idx - 1))
                            for tt in range(4):
                                v.scalar_tensor_tensor(
                                    ystg[idx % 2][:][:,
                                                     tt * 512:(tt + 1) * 512],
                                    pd[tt][:], 0.25,
                                    xres[idx % 2][:][:,
                                                     tt * 512:(tt + 1) * 512],
                                    ALU.mult, ALU.add)
                            ins = v.tensor_copy(tv[0:1, 0:2],
                                                ystg[idx % 2][0:1, 0:2])
                            v_a.inc(ins, 1)
                            if stats:
                                tbs, hss = divmod(idx, 4)
                                for a in range(4):
                                    ysl = ystg[idx % 2][:][:,
                                                          a * 512:
                                                          (a + 1) * 512]
                                    col = (4 * tbs + a) * 4 + hss
                                    v.reduce_sum(
                                        sa_psum[:, col:col + 1], ysl,
                                        axis=AX.X)
                                    v.tensor_tensor(sa_scr[:], ysl, ysl,
                                                    ALU.mult)
                                    v.reduce_sum(
                                        sa_psq[:, col:col + 1], sa_scr[:],
                                        axis=AX.X)

        def emit_attn_proj(xT):
            with ExitStack() as es:
                uid_ctr[0] += 1
                _u = uid_ctr[0]
                sb = lambda n, s, d: es.enter_context(
                    nc.sbuf_tensor(f"{n}_{_u}", s, d))
                wqs = [sb(f"ap_wq{i}", [128, 8192], BF16) for i in range(2)]
                wks = [sb(f"ap_wk{i}", [128, 8192], BF16) for i in range(2)]
                sqq = sb("ap_sqq", [128, 512], F32)
                sqk = sb("ap_sqk", [128, 512], F32)
                qss = sb("ap_qss", [128, 4], F32)
                kss = sb("ap_kss", [128, 4], F32)
                sdq = sb("ap_sdq", [128, 4], F32)
                sdk = sb("ap_sdk", [128, 4], F32)
                rq_all = sb("ap_rqa", [128, 256], F32)
                rk_all = sb("ap_rka", [128, 256], F32)
                qraw_st = [sb(f"ap_qr{i}", [128, 512], BF16) for i in range(2)]
                kvraw_st = [sb(f"ap_kr{i}", [128, 512], BF16) for i in range(2)]
                qn_st = [sb(f"ap_qn{i}", [128, 512], BF16)
                         for i in range(2)]
                kvn_st = [sb(f"ap_kv{i}", [128, 512], BF16) for i in range(2)]
                qnT_sp = [sb(f"ap_qT{i}", [128, 8192], BF16) for i in range(2)]
                tv = sb("ap_tv", [1, 8], F32)
                tsc = sb("ap_ts", [1, 8], F32)
                rqv = rq_all[:].rearrange("p (u h) -> p u h", h=4)
                rkv = rk_all[:].rearrange("p (u h) -> p u h", h=4)

                b_g, b_mm, b_sa, b_vb, b_qst = (g_ld.v, mm.v, s_a.v, v_b.v,
                                                q_st.v)
                with nc.Block() as block:
                    @block.gpsimd
                    def _(gp):
                        for sp in range(4):
                            if sp >= 2:
                                gp.wait_ge(mm.h, b_mm + 32 * (sp - 1))
                            srcw = wqT[:, sp * 512:(sp + 1) * 512].rearrange(
                                "(c p) n -> p c n", p=128)
                            ins = gp.dma_start(
                                out=wqs[sp % 2][:].rearrange(
                                    "p (c n) -> p c n", n=512), in_=srcw)
                            g_ld.inc(ins, 16)
                            srcw = wkvT[:, sp * 512:(sp + 1) * 512].rearrange(
                                "(c p) n -> p c n", p=128)
                            ins = gp.dma_start(
                                out=wks[sp % 2][:].rearrange(
                                    "p (c n) -> p c n", n=512), in_=srcw)
                            g_ld.inc(ins, 16)

                    @block.tensor
                    def _(te):
                        for u in range(64):
                            sp, t = divmod(u, 16)
                            if t == 0:
                                te.wait_ge(g_ld.h, b_g + 32 * (sp + 1))
                            if u >= 2:
                                te.wait_ge(v_b.h, b_vb + 3 * (u - 2) + 3)
                                te.wait_ge(s_a.h, b_sa + 4 * (u - 2) + 4)
                            pq = p[2 * (u % 2)]
                            pkv = p[2 * (u % 2) + 1]
                            for ps, wsp in ((pq, wqs[sp % 2]),
                                            (pkv, wks[sp % 2])):
                                for dc in range(16):
                                    ins = te.matmul(
                                        ps[:],
                                        xT[:][:, dc * 2048 + t * 128:
                                              dc * 2048 + (t + 1) * 128],
                                        wsp[:][:, dc * 512:(dc + 1) * 512],
                                        start=(dc == 0), stop=(dc == 15))
                                mm.inc(ins, 1)

                    @block.scalar
                    def _(s):
                        for u in range(64):
                            s.wait_ge(mm.h, b_mm + 2 * u + 1)
                            if u >= 1:
                                s.wait_ge(v_b.h, b_vb + 3 * (u - 1) + 1)
                            pq = p[2 * (u % 2)]
                            pkv = p[2 * (u % 2) + 1]
                            s.activation(sqq[:], pq[:], AF.Square)
                            ins = s.copy(tsc[0:1, 0:2], sqq[0:1, 0:2])
                            s_a.inc(ins, 1)
                            s.wait_ge(v_b.h, b_vb + 3 * u + 1)
                            s.activation(sdq[:], qss[:], AF.Sqrt, scale=128.0)
                            ins = s.copy(tsc[0:1, 2:4], sdq[0:1, 0:2])
                            s_a.inc(ins, 1)
                            s.wait_ge(mm.h, b_mm + 2 * u + 2)
                            s.activation(sqk[:], pkv[:], AF.Square)
                            ins = s.copy(tsc[0:1, 4:6], sqk[0:1, 0:2])
                            s_a.inc(ins, 1)
                            s.wait_ge(v_b.h, b_vb + 3 * u + 2)
                            s.activation(sdk[:], kss[:], AF.Sqrt, scale=128.0)
                            ins = s.copy(tsc[0:1, 6:8], sdk[0:1, 0:2])
                            s_a.inc(ins, 1)
                            sp, t = divmod(u, 16)
                            s.wait_ge(v_b.h, b_vb + 3 * u + 3)
                            ins = s.dma_start(
                                out=q_raw_s[t * 128:(t + 1) * 128,
                                            sp * 512:(sp + 1) * 512],
                                in_=qraw_st[u % 2][:])
                            q_st.inc(ins, 16)
                            ins = s.dma_start(
                                out=kv_raw_s[t * 128:(t + 1) * 128,
                                             sp * 512:(sp + 1) * 512],
                                in_=kvraw_st[u % 2][:])
                            q_st.inc(ins, 16)
                        s.wait_ge(q_st.h, b_qst + 32 * 64)

                    @block.vector
                    def _(v):
                        for u in range(64):
                            pq = p[2 * (u % 2)]
                            pkv = p[2 * (u % 2) + 1]
                            v.wait_ge(s_a.h, b_sa + 4 * u + 1)
                            v.reduce_sum(qss[:].unsqueeze(2),
                                         sqq[:].rearrange(
                                             "p (a b) -> p a b", b=128),
                                         axis=AX.X)
                            ins = v.tensor_copy(tv[0:1, 0:2], qss[0:1, 0:2])
                            v_b.inc(ins, 1)
                            v.wait_ge(s_a.h, b_sa + 4 * u + 2)
                            v.reciprocal(rqv[:, u, :], sdq[:])
                            if u >= 2:
                                v.wait_ge(q_st.h, b_qst + 32 * (u - 1))
                            v.tensor_copy(qraw_st[u % 2][:], pq[:])
                            v.wait_ge(s_a.h, b_sa + 4 * u + 3)
                            v.reduce_sum(kss[:].unsqueeze(2),
                                         sqk[:].rearrange(
                                             "p (a b) -> p a b", b=128),
                                         axis=AX.X)
                            ins = v.tensor_copy(tv[0:1, 2:4], kss[0:1, 0:2])
                            v_b.inc(ins, 1)
                            v.wait_ge(s_a.h, b_sa + 4 * u + 4)
                            v.reciprocal(rkv[:, u, :], sdk[:])
                            v.tensor_copy(kvraw_st[u % 2][:], pkv[:])
                            ins = v.tensor_copy(tv[0:1, 4:6], tv[0:1, 0:2])
                            v_b.inc(ins, 1)

                b_s, b_mm, b_vb, b_tp, b_qst, b_s2 = (
                    s_ld.v, mm.v, v_b.v, t_tp.v, q_st.v, s2_st.v)
                with nc.Block() as block:
                    @block.sync
                    def _(sync):
                        for u in range(64):
                            sp, t = divmod(u, 16)
                            if u >= 2:
                                sync.wait_ge(v_b.h, b_vb + 3 * (u - 2) + 3)
                            ins = sync.dma_start(
                                out=qraw_st[u % 2][:],
                                in_=q_raw_s[t * 128:(t + 1) * 128,
                                            sp * 512:(sp + 1) * 512])
                            s_ld.inc(ins, 16)
                            ins = sync.dma_start(
                                out=kvraw_st[u % 2][:],
                                in_=kv_raw_s[t * 128:(t + 1) * 128,
                                             sp * 512:(sp + 1) * 512])
                            s_ld.inc(ins, 16)

                    @block.gpsimd
                    def _(gp):
                        for sp in range(4):
                            gp.wait_ge(v_b.h,
                                       b_vb + 3 * (16 * sp + 16) + 1)
                            qTv = qnT_sp[sp % 2][:].rearrange(
                                "p (h n) -> p h n", n=2048)
                            for h in range(4):
                                ins = gp.dma_start(
                                    out=qnT_s[(4 * sp + h) * 128:
                                              (4 * sp + h + 1) * 128, :],
                                    in_=qTv[:, h, :])
                                s2_st.inc(ins, 16)
                        gp.wait_ge(s2_st.h, b_s2 + 16 * 16)

                    @block.vector
                    def _(v):
                        def qcopy(uc):
                            spc, tc = divmod(uc, 16)
                            if tc == 0 and spc >= 2:
                                v.wait_ge(s2_st.h, b_s2 + 64 * (spc - 1))
                            v.wait_ge(t_tp.h, b_tp + uc + 1)
                            pub = p[6 + uc % 2][:].bitcast(BF16)
                            qTv = qnT_sp[spc % 2][:].rearrange(
                                "p (h n) -> p h n", n=2048)
                            ins = v.tensor_copy(
                                qTv[:, :, tc * 128:(tc + 1) * 128],
                                pub[:, 0:512].rearrange(
                                    "p (h n) -> p h n", n=128))
                            v_b.inc(ins, 1)

                        for u in range(64):
                            sp, t = divmod(u, 16)
                            if u >= 1:
                                qcopy(u - 1)
                            else:
                                ins = v.tensor_copy(tv[0:1, 0:2],
                                                    tv[0:1, 2:4])
                                v_b.inc(ins, 1)
                            v.wait_ge(s_ld.h, b_s + 32 * (u + 1))
                            if u >= 1:
                                v.wait_ge(t_tp.h, b_tp + u - 1)
                            v.tensor_tensor(
                                qn_st[u % 2][:].rearrange(
                                    "p (a b) -> p a b", b=128),
                                qraw_st[u % 2][:].rearrange(
                                    "p (a b) -> p a b", b=128),
                                rqv[:, u, :].unsqueeze(2).broadcast_to(
                                    [128, 4, 128]),
                                ALU.mult)
                            ins = v.tensor_copy(tv[0:1, 0:2], tv[0:1, 2:4])
                            v_b.inc(ins, 1)
                            if u >= 2:
                                v.wait_ge(q_st.h, b_qst + 16 * (u - 1))
                            v.tensor_tensor(
                                kvn_st[u % 2][:].rearrange(
                                    "p (a b) -> p a b", b=128),
                                kvraw_st[u % 2][:].rearrange(
                                    "p (a b) -> p a b", b=128),
                                rkv[:, u, :].unsqueeze(2).broadcast_to(
                                    [128, 4, 128]),
                                ALU.mult)
                            ins = v.tensor_copy(tv[0:1, 2:4], tv[0:1, 4:6])
                            v_b.inc(ins, 1)
                        qcopy(63)

                    @block.tensor
                    def _(te):
                        for u in range(64):
                            pub = p[6 + u % 2][:].bitcast(BF16)
                            te.wait_ge(v_b.h, b_vb + 3 * u + 2)
                            for h in range(4):
                                ins = te.transpose(
                                    pub[:, h * 128:(h + 1) * 128],
                                    qn_st[u % 2][:][:, h * 128:(h + 1) * 128],
                                    kc["ident"][:])
                            t_tp.inc(ins, 1)

                    @block.scalar
                    def _(s):
                        for u in range(64):
                            sp, t = divmod(u, 16)
                            s.wait_ge(v_b.h, b_vb + 3 * u + 3)
                            ins = s.dma_start(
                                out=kvn_s[sp, :, t, :],
                                in_=kvn_st[u % 2][:])
                            q_st.inc(ins, 16)
                        s.wait_ge(q_st.h, b_qst + 16 * 64)

        def emit_cumsum():
            with ExitStack() as es:
                uid_ctr[0] += 1
                _u = uid_ctr[0]
                sb = lambda n, s, d: es.enter_context(
                    nc.sbuf_tensor(f"{n}_{_u}", s, d))
                kvsp = [sb(f"cs_kv{i}", [128, 8192], BF16) for i in range(2)]
                kvc_st = [sb(f"cs_st{i}", [128, 512], BF16)
                          for i in range(2)]
                kvcT_sp = [sb(f"cs_T{i}", [128, 8192], BF16) for i in range(2)]
                b_s, b_mm, b_va, b_vb, b_tp, b_s2 = (s_ld.v, mm.v, v_a.v,
                                                     v_b.v, t_tp.v, s2_st.v)

                with nc.Block() as block:
                    @block.sync
                    def _(sync):
                        for sp in range(4):
                            if sp >= 2:
                                sync.wait_ge(mm.h, b_mm + 16 * (sp - 1))
                            src = kvn_s[sp]
                            ins = sync.dma_start(
                                out=kvsp[sp % 2][:].rearrange(
                                    "p (a n) -> p a n", n=512), in_=src)
                            s_ld.inc(ins, 16)
                            sync.wait_ge(v_b.h, b_vb + 16 * (sp + 1))
                            Tv = kvcT_sp[sp % 2][:].rearrange(
                                "p (h n) -> p h n", n=2048)
                            for h in range(4):
                                ins = sync.dma_start(
                                    out=kvcT_s[(4 * sp + h) * 128:
                                               (4 * sp + h + 1) * 128, :],
                                    in_=Tv[:, h, :])
                                s2_st.inc(ins, 16)
                        sync.wait_ge(s2_st.h, b_s2 + 16 * 16)

                    @block.tensor
                    def _(te):
                        p6b = p[6][:].bitcast(BF16)
                        for u in range(64):
                            sp, tout = divmod(u, 16)
                            if tout == 0:
                                te.wait_ge(s_ld.h, b_s + 16 * (sp + 1))
                            pc = p[u % 2]
                            if u >= 2:
                                te.wait_ge(v_a.h, b_va + u - 1)
                            kvv = kvsp[sp % 2][:].rearrange(
                                "p (a n) -> p a n", n=512)
                            for tp in range(tout + 1):
                                ins = te.matmul(
                                    pc[:],
                                    kc["tri"][:] if tp == tout
                                    else kc["ones"][:],
                                    kvv[:, tp, :],
                                    start=(tp == 0), stop=(tp == tout))
                            mm.inc(ins, 1)
                            te.wait_ge(v_a.h, b_va + u + 1)
                            if u >= 2:
                                te.wait_ge(v_b.h, b_vb + u - 1)
                            pub = p[6 + u % 2][:].bitcast(BF16)
                            for h in range(4):
                                ins = te.transpose(
                                    pub[:, h * 128:(h + 1) * 128],
                                    kvc_st[u % 2][:][:, h * 128:(h + 1) * 128],
                                    kc["ident"][:])
                            t_tp.inc(ins, 1)

                    @block.vector
                    def _(v):
                        for u in range(64):
                            sp, tout = divmod(u, 16)
                            pc = p[u % 2]
                            v.wait_ge(mm.h, b_mm + u + 1)
                            ins = v.tensor_copy(kvc_st[u % 2][:], pc[:])
                            v_a.inc(ins, 1)
                            if tout == 0 and sp >= 2:
                                v.wait_ge(s2_st.h, b_s2 + 64 * (sp - 1))
                            v.wait_ge(t_tp.h, b_tp + u + 1)
                            pub = p[6 + u % 2][:].bitcast(BF16)
                            Tv = kvcT_sp[sp % 2][:].rearrange(
                                "p (h n) -> p h n", n=2048)
                            ins = v.tensor_copy(
                                Tv[:, :, tout * 128:(tout + 1) * 128],
                                pub[:, 0:512].rearrange(
                                    "p (h n) -> p h n", n=128))
                            v_b.inc(ins, 1)

        def emit_heads():
            with ExitStack() as es:
                uid_ctr[0] += 1
                _u = uid_ctr[0]
                sb = lambda n, s, d: es.enter_context(
                    nc.sbuf_tensor(f"{n}_{_u}", s, d))
                qh = [sb(f"hd_q{i}", [128, 2048], BF16) for i in range(2)]
                kvh = [sb(f"hd_k{i}", [128, 2048], BF16) for i in range(2)]
                qfr = sb("hd_qfr", [128, 2048], F32)
                qfi = sb("hd_qfi", [128, 2048], F32)
                kr = sb("hd_kr", [128, 2048], F32)
                ki = sb("hd_ki", [128, 2048], F32)
                t1 = sb("hd_t1", [128, 2048], F32)
                u1 = sb("hd_u1", [128, 2048], F32)
                qvr = sb("hd_qvr", [128, 2048], BF16)
                qvi = sb("hd_qvi", [128, 2048], BF16)
                vals_st = [sb(f"hd_v{i}", [128, 2048], BF16) for i in range(2)]
                b_s, b_mm, b_sa, b_vb, b_qst = (s_ld.v, mm.v, s_a.v, v_b.v,
                                                q_st.v)

                with nc.Block() as block:
                    @block.sync
                    def _(sync):
                        for h in range(16):
                            if h >= 2:
                                sync.wait_ge(mm.h, b_mm + 12 * (h - 1))
                            ins = sync.dma_start(
                                out=qh[h % 2][:],
                                in_=qnT_s[h * 128:(h + 1) * 128, :])
                            s_ld.inc(ins, 16)
                            ins = sync.dma_start(
                                out=kvh[h % 2][:],
                                in_=kvcT_s[h * 128:(h + 1) * 128, :])
                            s_ld.inc(ins, 16)

                    @block.tensor
                    def _(te):
                        for h in range(16):
                            te.wait_ge(s_ld.h, b_s + 32 * (h + 1))
                            if h >= 1:
                                te.wait_ge(v_b.h, b_vb + 4 * h)
                                te.wait_ge(s_a.h, b_sa + 12 * (h - 1) + 8)
                            g = 0
                            for srcb in (qh[h % 2], kvh[h % 2]):
                                for sp4 in range(4):
                                    pa = p[2 * (g % 2)]
                                    pb = p[2 * (g % 2) + 1]
                                    if g >= 2:
                                        te.wait_ge(s_a.h,
                                                   b_sa + 12 * h + g - 1)
                                    sl = srcb[:][:, sp4 * 512:(sp4 + 1) * 512]
                                    te.matmul(pa[:], kc["cf"][:], sl,
                                              start=True, stop=True)
                                    ins = te.matmul(pb[:], kc["sf"][:], sl,
                                                    start=True, stop=True)
                                    mm.inc(ins, 1)
                                    g += 1
                            if h >= 1:
                                te.wait_ge(s_a.h, b_sa + 12 * h)
                            for sp4 in range(4):
                                pv = p[4 + sp4 % 2]
                                te.wait_ge(v_b.h, b_vb + 4 * h + sp4 + 1)
                                if sp4 >= 2:
                                    te.wait_ge(s_a.h,
                                               b_sa + 12 * h + 8 + sp4 - 1)
                                te.matmul(pv[:], kc["ci"][:],
                                          qvr[:][:, sp4 * 512:(sp4 + 1) * 512],
                                          start=True, stop=False)
                                ins = te.matmul(
                                    pv[:], kc["si"][:],
                                    qvi[:][:, sp4 * 512:(sp4 + 1) * 512],
                                    start=False, stop=True)
                                mm.inc(ins, 1)

                    @block.scalar
                    def _(s):
                        for h in range(16):
                            dsts = [(qfr, qfi), (qfr, qfi), (qfr, qfi),
                                    (qfr, qfi), (kr, ki), (kr, ki), (kr, ki),
                                    (kr, ki)]
                            for g in range(8):
                                sp4 = g % 4
                                pa = p[2 * (g % 2)]
                                pb = p[2 * (g % 2) + 1]
                                s.wait_ge(mm.h, b_mm + 12 * h + g + 1)
                                if h >= 1 and g == 0:
                                    s.wait_ge(v_b.h, b_vb + 4 * h)
                                da, db = dsts[g]
                                s.activation(
                                    da[:][:, sp4 * 512:(sp4 + 1) * 512],
                                    pa[:], AF.Copy)
                                ins = s.activation(
                                    db[:][:, sp4 * 512:(sp4 + 1) * 512],
                                    pb[:], AF.Copy)
                                s_a.inc(ins, 1)
                            for sp4 in range(4):
                                pv = p[4 + sp4 % 2]
                                s.wait_ge(mm.h, b_mm + 12 * h + 8 + sp4 + 1)
                                if sp4 == 0 and h >= 1:
                                    s.wait_ge(q_st.h, b_qst + 16 * (h - 1))
                                ins = s.activation(
                                    vals_st[h % 2][:][:,
                                                      sp4 * 512:(sp4 + 1) * 512],
                                    pv[:], AF.Copy)
                                s_a.inc(ins, 1)
                            s.wait_ge(s_a.h, b_sa + 12 * h + 12)
                            ins = s.dma_start(
                                out=vals_s[:, :, h, :]
                                .transpose([1, 0, 2]),
                                in_=vals_st[h % 2][:].rearrange(
                                    "p (a n) -> p a n", n=512))
                            q_st.inc(ins, 16)
                        s.wait_ge(q_st.h, b_qst + 16 * 16)

                    @block.vector
                    def _(v):
                        for h in range(16):
                            for sp4 in range(4):
                                sl = slice(sp4 * 512, (sp4 + 1) * 512)
                                v.wait_ge(s_a.h, b_sa + 12 * h + 5 + sp4)
                                v.tensor_tensor(t1[:][:, sl], qfr[:][:, sl],
                                                kr[:][:, sl], ALU.mult)
                                v.tensor_tensor(u1[:][:, sl], qfi[:][:, sl],
                                                ki[:][:, sl], ALU.mult)
                                v.tensor_tensor(qvr[:][:, sl], t1[:][:, sl],
                                                u1[:][:, sl], ALU.subtract)
                                v.tensor_tensor(t1[:][:, sl], qfr[:][:, sl],
                                                ki[:][:, sl], ALU.mult)
                                v.tensor_tensor(u1[:][:, sl], qfi[:][:, sl],
                                                kr[:][:, sl], ALU.mult)
                                ins = v.tensor_tensor(qvi[:][:, sl],
                                                      t1[:][:, sl],
                                                      u1[:][:, sl], ALU.add)
                                v_b.inc(ins, 1)

        def emit_wo():
            with ExitStack() as es:
                uid_ctr[0] += 1
                _u = uid_ctr[0]
                sb = lambda n, s, d: es.enter_context(
                    nc.sbuf_tensor(f"{n}_{_u}", s, d))
                wo_sb = sb("wo_w", [128, 32768], BF16)
                vals_sb = [sb(f"wo_v{i}", [128, 8192], BF16) for i in range(2)]
                xres = [sb(f"wo_xr{i}", [128, 2048], F32) for i in range(2)]
                ystg = [sb(f"wo_ys{i}", [128, 8192], F32) for i in range(2)]
                b_g, b_s, b_mm, b_va, b_qld, b_qst = (g_ld.v, s_ld.v, mm.v,
                                                      v_a.v, q_ld.v, q_st.v)

                def xtile(t_ap, tb, dsp):
                    return t_ap[tb * 512:(tb + 1) * 512,
                                dsp * 512:(dsp + 1) * 512].rearrange(
                                    "(a p) n -> p a n", p=128)

                with nc.Block() as block:
                    @block.gpsimd
                    def _(gp):
                        src = woT.rearrange("(c p) n -> p c n", p=128)
                        ins = gp.dma_start(
                            out=wo_sb[:].rearrange("p (c n) -> p c n", n=2048),
                            in_=src)
                        g_ld.inc(ins, 16)

                    @block.sync
                    def _(sync):
                        for tb in range(4):
                            if tb >= 2:
                                sync.wait_ge(mm.h, b_mm + 4 * (tb - 1))
                            src = vals_s[tb]
                            ins = sync.dma_start(
                                out=vals_sb[tb % 2][:].rearrange(
                                    "p (c n) -> p c n", n=512), in_=src)
                            s_ld.inc(ins, 16)

                    @block.tensor
                    def _(te):
                        te.wait_ge(g_ld.h, b_g + 16)
                        for idx in range(16):
                            tb, dsp = divmod(idx, 4)
                            if dsp == 0:
                                te.wait_ge(s_ld.h, b_s + 16 * (tb + 1))
                            if idx >= 2:
                                te.wait_ge(v_a.h, b_va + idx - 1)
                            pd = [p[(idx % 2) * 4 + tt] for tt in range(4)]
                            vv = vals_sb[tb % 2][:].rearrange(
                                "p (c n) -> p c n", n=512)
                            for oc in range(16):
                                for tt in range(4):
                                    ins = te.matmul(
                                        pd[tt][:],
                                        vv[:, oc, tt * 128:(tt + 1) * 128],
                                        wo_sb[:][:, oc * 2048 + dsp * 512:
                                                 oc * 2048 + (dsp + 1) * 512],
                                        start=(oc == 0), stop=(oc == 15))
                            mm.inc(ins, 1)

                    @block.scalar
                    def _(s):
                        for idx in range(16):
                            tb, dsp = divmod(idx, 4)
                            if idx == 0:
                                ins = s.dma_start(
                                    out=xres[0][:].rearrange(
                                        "p (a n) -> p a n", n=512),
                                    in_=xtile(x1_s, 0, 0))
                                q_ld.inc(ins, 16)
                            if idx + 1 < 16:
                                tb2, dsp2 = divmod(idx + 1, 4)
                                if idx >= 1:
                                    s.wait_ge(v_a.h, b_va + idx)
                                ins = s.dma_start(
                                    out=xres[(idx + 1) % 2][:].rearrange(
                                        "p (a n) -> p a n", n=512),
                                    in_=xtile(x1_s, tb2, dsp2))
                                q_ld.inc(ins, 16)
                            if dsp == 3:
                                s.wait_ge(v_a.h, b_va + idx + 1)
                                ins = s.dma_start(
                                    out=x2_s[tb * 512:(tb + 1) * 512, :]
                                    .rearrange("(a p) n -> p a n", p=128),
                                    in_=ystg[tb % 2][:].rearrange(
                                        "p (a n) -> p a n", n=2048))
                                q_st.inc(ins, 16)
                        s.wait_ge(q_st.h, b_qst + 16 * 4)

                    @block.vector
                    def _(v):
                        for idx in range(16):
                            tbs, dss = divmod(idx, 4)
                            pd = [p[(idx % 2) * 4 + tt] for tt in range(4)]
                            v.wait_ge(mm.h, b_mm + idx + 1)
                            v.wait_ge(q_ld.h, b_qld + 16 * (idx + 1))
                            if dss == 0 and tbs >= 2:
                                v.wait_ge(q_st.h, b_qst + 16 * (tbs - 1))
                            for tt in range(4):
                                ins = v.scalar_tensor_tensor(
                                    ystg[tbs % 2][:][:,
                                                    tt * 2048 + dss * 512:
                                                    tt * 2048 +
                                                    (dss + 1) * 512],
                                    pd[tt][:], 1.0,
                                    xres[idx % 2][:][:,
                                                     tt * 512:(tt + 1) * 512],
                                    ALU.mult, ALU.add)
                            v_a.inc(ins, 1)
                            for a in range(4):
                                ysl = ystg[tbs % 2][:][:,
                                                      a * 2048 + dss * 512:
                                                      a * 2048 +
                                                      (dss + 1) * 512]
                                col = (4 * tbs + a) * 4 + dss
                                v.reduce_sum(sa_psum[:, col:col + 1], ysl,
                                             axis=AX.X)
                                v.tensor_tensor(sa_scr[:], ysl, ysl,
                                                ALU.mult)
                                v.reduce_sum(sa_psq[:, col:col + 1],
                                             sa_scr[:], axis=AX.X)

        def emit_gather():
            b_cc = g_ld.v
            with nc.Block() as block:
                @block.gpsimd
                def _(gp):
                    pairs = [
                        (wg_loc.rearrange("a p c n -> a (p c n)"),
                         wg_s.rearrange("a p c n -> a (p c n)")),
                        (wu_loc.rearrange("a p c n -> a (p c n)"),
                         wu_s.rearrange("a p c n -> a (p c n)")),
                        (wd_loc.rearrange("a h p n -> a (h p n)"),
                         wd_s.rearrange("a h p n -> a (h p n)")),
                    ]
                    for i, (src_ap, dst_ap) in enumerate(pairs):
                        ins = gp.collective_compute(
                            "AllGather", ALU.bypass,
                            replica_groups=[list(range(8))],
                            ins=[src_ap], outs=[dst_ap])
                        g_ld.inc(ins, 1)
                    gp.wait_ge(g_ld.h, b_cc + 3)

        emit_dequant()
        with ExitStack() as g:
            xT1 = g.enter_context(nc.sbuf_tensor("xT1", [128, 32768], BF16))
            emit_ln_T(x_in, xT1, "ffn")
            if debug:
                with nc.Block() as block:
                    @block.sync
                    def _(sync):
                        ins = sync.dma_start(out=xt_dbg, in_=xT1[:])
                        s_ld.inc(ins, 16)
                        sync.wait_ge(s_ld.h, s_ld.v)
            emit_ffn_a(xT1)
        emit_ffn_b(x_in, x1_s, final=False, stats=True)
        with ExitStack() as g:
            xTa = g.enter_context(nc.sbuf_tensor("xTa", [128, 32768], BF16))
            emit_ln_T(x1_s, xTa, "nat", prestats=True)
            emit_attn_proj(xTa)
        emit_cumsum()
        emit_heads()
        emit_wo()
        with ExitStack() as g:
            xT2 = g.enter_context(nc.sbuf_tensor("xT2", [128, 32768], BF16))
            emit_ln_T(x2_s, xT2, "ffn", prestats=True)
            emit_ffn_a(xT2)
        emit_ffn_b(x2_s, y_out, final=True)

    return nc


def kernel(L0_attn_x, L0_attn_W_q, L0_attn_W_kv, L0_attn_W_o,
           L0_ffn_gate_W_packed, L0_ffn_gate_scales,
           L0_ffn_up_W_packed, L0_ffn_up_scales,
           L0_ffn_down_W_packed, L0_ffn_down_scales):
    x = np.ascontiguousarray(np.asarray(L0_attn_x, dtype=np.float32))

    def pk_T(a):
        a = np.ascontiguousarray(np.asarray(a, dtype=np.int32))
        return np.ascontiguousarray(a.view(np.uint8)[:, ::4].T)

    def wT(a):
        return np.ascontiguousarray(np.asarray(a, dtype=np.float32).T)

    common = {
        "wqT": wT(L0_attn_W_q),
        "wkvT": wT(L0_attn_W_kv),
        "woT": wT(L0_attn_W_o),
    }
    gpkT = pk_T(L0_ffn_gate_W_packed)
    upkT = pk_T(L0_ffn_up_W_packed)
    dpkT = pk_T(L0_ffn_down_W_packed)
    gscT = wT(L0_ffn_gate_scales)
    uscT = wT(L0_ffn_up_scales)
    dscT = wT(L0_ffn_down_scales)
    nc = build_nc()
    in_maps = [dict(
        common, x=np.ascontiguousarray(x[b]),
        gpk=np.ascontiguousarray(gpkT[:, b * 2048:(b + 1) * 2048]),
        upk=np.ascontiguousarray(upkT[:, b * 2048:(b + 1) * 2048]),
        dpk=np.ascontiguousarray(dpkT[b * 1024:(b + 1) * 1024, :]),
        gsc=np.ascontiguousarray(gscT[:, b * 2048:(b + 1) * 2048]),
        usc=np.ascontiguousarray(uscT[:, b * 2048:(b + 1) * 2048]),
        dsc=np.ascontiguousarray(dscT[b * 64:(b + 1) * 64, :]),
    ) for b in range(8)]
    res = run_bass_kernel_spmd(nc, in_maps, core_ids=list(range(8)))
    return np.stack([np.asarray(res.results[b]["y"]) for b in range(8)],
                    axis=0)


# version 59
# speedup vs baseline: 1.0046x; 1.0031x over previous
import math
import numpy as np
import ml_dtypes
from contextlib import ExitStack

import concourse.bass as bass
import concourse.mybir as mybir
from concourse.bass_utils import run_bass_kernel_spmd

F32 = mybir.dt.float32
BF16 = mybir.dt.bfloat16
I32 = mybir.dt.int32
U8 = mybir.dt.uint8
AF = mybir.ActivationFunctionType
ALU = mybir.AluOpType
AX = mybir.AxisListType

S = 2048
D = 2048
M = 16384
KBITS = 0x3F000000
C0 = 0.7978845608028654
SQC = math.sqrt(C0 * 0.044715)


class Sem:
    def __init__(self, nc, name):
        self.h = nc.alloc_semaphore(name)
        self.v = 0

    def inc(self, instr, n=1):
        instr.then_inc(self.h, n)
        self.v += n


def _consts():
    i = np.arange(128)
    ang = 2.0 * np.pi * (np.outer(i, i) % 128) / 128.0
    bf = ml_dtypes.bfloat16
    return {
        "ident": np.eye(128, dtype=np.float32).astype(bf),
        "cf": np.cos(ang).astype(np.float32).astype(bf),
        "sf": (-np.sin(ang)).astype(np.float32).astype(bf),
        "ci": (np.cos(ang) / 128.0).astype(np.float32).astype(bf),
        "si": (-np.sin(ang) / 128.0).astype(np.float32).astype(bf),
        "tri": np.triu(np.ones((128, 128), np.float32)).astype(bf),
        "ones": np.ones((128, 128), np.float32).astype(bf),
    }


def build_nc(debug=False, kv_raw=False):
    nc = bass.Bass("TRN2", target_bir_lowering=False)
    uid_ctr = [0]

    x_in = nc.dram_tensor("x", [S, D], F32, kind="ExternalInput").ap()
    wqT = nc.dram_tensor("wqT", [D, D], F32, kind="ExternalInput").ap()
    wkvT = nc.dram_tensor("wkvT", [D, D], F32, kind="ExternalInput").ap()
    woT = nc.dram_tensor("woT", [D, D], F32, kind="ExternalInput").ap()
    gpk = nc.dram_tensor("gpk", [1024, 2048], U8, kind="ExternalInput").ap()
    upk = nc.dram_tensor("upk", [1024, 2048], U8, kind="ExternalInput").ap()
    dpk = nc.dram_tensor("dpk", [1024, 2048], U8, kind="ExternalInput").ap()
    gsc = nc.dram_tensor("gsc", [64, 2048], F32, kind="ExternalInput").ap()
    usc = nc.dram_tensor("usc", [64, 2048], F32, kind="ExternalInput").ap()
    dsc = nc.dram_tensor("dsc", [64, 2048], F32, kind="ExternalInput").ap()
    y_out = nc.dram_tensor("y", [S, D], F32, kind="ExternalOutput").ap()

    def scratch(name, shape, dtype):
        kind = "ExternalOutput" if debug else "Internal"
        return nc.dram_tensor(name, shape, dtype, kind=kind).ap()

    wg_s = nc.dram_tensor("wg_s", [64, 128, 16, 256], BF16, kind="Internal",
                          addr_space="Shared").ap()
    wu_s = nc.dram_tensor("wu_s", [64, 128, 16, 256], BF16, kind="Internal",
                          addr_space="Shared").ap()
    wd_s = nc.dram_tensor("wd_s", [32, 128, 16, 512], BF16, kind="Internal",
                          addr_space="Shared").ap()
    wg_loc = nc.dram_tensor("wg_loc", [8, 128, 16, 256], BF16, kind="Internal").ap()
    wu_loc = nc.dram_tensor("wu_loc", [8, 128, 16, 256], BF16, kind="Internal").ap()
    wd_loc = nc.dram_tensor("wd_loc", [4, 128, 16, 512], BF16, kind="Internal").ap()
    wsac = nc.dram_tensor("wsac", [8, 128, 4, 256], BF16, kind="Internal").ap()
    inter_s = scratch("inter_s", [4, 8, 128, 16, 512], BF16)
    x1_s = scratch("x1_s", [S, D], F32)
    x2_s = scratch("x2_s", [S, D], F32)
    qnT_s = scratch("qnT_s", [D, S], BF16)
    xt_dbg = scratch("xt_dbg", [128, 32768], BF16) if debug else None
    kvn_s = scratch("kvn_s", [4, 128, 16, 512], BF16)
    q_raw_s = nc.dram_tensor("q_raw_s", [4, 16, 128, 512], BF16,
                             kind="Internal").ap()
    kv_raw_s = nc.dram_tensor("kv_raw_s", [4, 16, 128, 512], BF16,
                              kind="Internal").ap()
    kvcT_s = scratch("kvcT_s", [D, S], BF16)
    vals_s = scratch("vals_s", [4, 128, 16, 512], BF16)

    cns = _consts()
    c_aps = {k: nc.inline_tensor(v, f"c_{k}").ap() for k, v in cns.items()}

    with ExitStack() as top:
        sbt = lambda n, s, d: top.enter_context(nc.sbuf_tensor(n, s, d))
        kc = {k: sbt(f"k_{k}", [128, 128], BF16) for k in cns}
        k_eps = sbt("k_eps", [128, 1], F32)
        sa_psum = sbt("sa_psum", [128, 64], F32)
        sa_psq = sbt("sa_psq", [128, 64], F32)
        sa_scr = sbt("sa_scr", [128, 512], F32)
        nc.const_aps.register(F32, 1e-5, k_eps[:, 0:1])
        p = [top.enter_context(nc.psum_tensor(f"pb{i}", [128, 512], F32))
             for i in range(8)]

        s_ld = Sem(nc, "s_ld")
        s2_st = Sem(nc, "s2_st")
        g_ld = Sem(nc, "g_ld")
        q_ld = Sem(nc, "q_ld")
        q_st = Sem(nc, "q_st")
        v_a = Sem(nc, "v_a")
        v_b = Sem(nc, "v_b")
        s_a = Sem(nc, "s_a")
        mm = Sem(nc, "mm")
        t_tp = Sem(nc, "t_tp")
        st_g = Sem(nc, "st_g")
        st_u = Sem(nc, "st_u")
        i_ld = Sem(nc, "i_ld")

        all_sems = [s_ld, s2_st, g_ld, q_ld, q_st, v_a, v_b, s_a, mm, t_tp,
                    st_g, st_u, i_ld]
        with nc.Block() as block:
            @block.gpsimd
            def _(gp):
                for sm in all_sems:
                    gp.sem_clear(sm.h)

        with nc.Block() as block:
            @block.sync
            def _(sync):
                for k in cns:
                    ins = sync.dma_start(out=kc[k][:], in_=c_aps[k])
                    s_ld.inc(ins, 16)

            @block.vector
            def _(v):
                v.memset(k_eps[:], 1e-5)
                v.wait_ge(s_ld.h, s_ld.v)

        def emit_dequant():
            with ExitStack() as es:
                uid_ctr[0] += 1
                _u = uid_ctr[0]
                sb = lambda n, s, d: es.enter_context(
                    nc.sbuf_tensor(f"{n}_{_u}", s, d))
                wp = [sb(f"dq_wp{i}", [128, 2048], I32) for i in range(2)]
                sexp = [sb(f"dq_sc{i}", [128, 2048], F32) for i in range(2)]
                mag = sb("dq_mag", [128, 4096], F32)
                corr = sb("dq_corr", [128, 4096], F32)
                sbits = sb("dq_sb", [128, 4096], I32)
                ssgn = sb("dq_ssgn", [128, 4096], F32)
                val = [sb(f"dq_val{i}", [128, 4096], BF16) for i in range(2)]
                tv = sb("dq_tv", [1, 8], F32)
                tsc = sb("dq_ts", [1, 8], F32)

                jobs = []
                for jc in (0, 1):
                    src = gpk[jc * 128:(jc + 1) * 128, :]
                    scs = gsc[jc * 8:(jc + 1) * 8, :]
                    dlo = wsac[0:8, :, 2 * jc, :].transpose([1, 0, 2])
                    dhi = wsac[0:8, :, 2 * jc + 1, :].transpose([1, 0, 2])
                    jobs.append((src, scs, dlo, dhi))
                for w_pk, w_sc, w_dst in ((gpk, gsc, wg_loc),
                                          (upk, usc, wu_loc)):
                    for jc in range(8):
                        src = w_pk[jc * 128:(jc + 1) * 128, :]
                        scs = w_sc[jc * 8:(jc + 1) * 8, :]
                        dlo = w_dst[0:8, :, 2 * jc, :].transpose([1, 0, 2])
                        dhi = w_dst[0:8, :, 2 * jc + 1, :].transpose([1, 0, 2])
                        jobs.append((src, scs, dlo, dhi))
                for jl in range(8):
                    src = dpk[jl * 128:(jl + 1) * 128, :]
                    scs = dsc[jl * 8:(jl + 1) * 8, :]
                    dlo = wd_loc[:, :, 2 * jl, :].transpose([1, 0, 2])
                    dhi = wd_loc[:, :, 2 * jl + 1, :].transpose([1, 0, 2])
                    jobs.append((src, scs, dlo, dhi))
                NJ = len(jobs)
                b_g, b_s, b_va, b_vb, b_sa, b_qst = (g_ld.v, s_ld.v, v_a.v,
                                                     v_b.v, s_a.v, q_st.v)
                b_stg, b_stu = st_g.v, st_u.v

                def st_sem(j):
                    if 2 <= j <= 9:
                        return st_g
                    if 10 <= j <= 17:
                        return st_u
                    return q_st

                def st_waits(eng, j):
                    nq = 32 * (min(j + 1, 2) + max(0, j - 17))
                    ng = 32 * max(0, min(j, 9) - 1)
                    nu = 32 * max(0, min(j, 17) - 9)
                    if nq:
                        eng.wait_ge(q_st.h, b_qst + nq)
                    if ng:
                        eng.wait_ge(st_g.h, b_stg + ng)
                    if nu:
                        eng.wait_ge(st_u.h, b_stu + nu)

                with nc.Block() as block:
                    @block.gpsimd
                    def _(gp):
                        gathers = {
                            11: (10, wg_loc.rearrange("a p c n -> a (p c n)"),
                                 wg_s.rearrange("a p c n -> a (p c n)")),
                            21: (18, wu_loc.rearrange("a p c n -> a (p c n)"),
                                 wu_s.rearrange("a p c n -> a (p c n)")),
                            25: (26, wd_loc.rearrange("h p c n -> h (p c n)"),
                                 wd_s.rearrange("a p c n -> a (p c n)")),
                        }
                        for i in range(NJ):
                            if i >= 2:
                                gp.wait_ge(v_b.h, b_vb + i - 1)
                            ins = gp.dma_start(out=wp[i % 2][:], in_=jobs[i][0])
                            g_ld.inc(ins, 16)
                        for i in (11, 21, 25):
                            nj, src_ap, dst_ap = gathers[i]
                            sm, tgt = {
                                11: (st_g, b_stg + 256),
                                21: (st_u, b_stu + 256),
                                25: (q_st, b_qst + 320)}[i]
                            gp.wait_ge(sm.h, tgt)
                            ins = gp.collective_compute(
                                "AllGather", ALU.bypass,
                                replica_groups=[list(range(8))],
                                ins=[src_ap], outs=[dst_ap])
                            g_ld.inc(ins, 1)
                        gp.wait_ge(g_ld.h, g_ld.v)

                    @block.sync
                    def _(sync):
                        for i in range(NJ):
                            if i >= 2:
                                sync.wait_ge(v_b.h, b_vb + i - 1)
                            scs = jobs[i][1]
                            bcast = scs.unsqueeze(1).broadcast_to(
                                [8, 16, scs.shape[1]])
                            ins = sync.dma_start(out=sexp[i % 2][:], in_=bcast)
                            s_ld.inc(ins, 16)

                    @block.vector
                    def _(v):
                        magi = mag[:].bitcast(I32)
                        ssgni = ssgn[:].bitcast(I32)
                        for i in range(NJ):
                            w = wp[i % 2][:]
                            sci = sexp[i % 2][:].bitcast(I32)
                            v.wait_ge(g_ld.h, b_g + 16 * (i + 1))
                            v.wait_ge(s_ld.h, b_s + 16 * (i + 1))
                            v.tensor_scalar(magi[:, 0:2048], w, 7, 22,
                                            ALU.bitwise_and,
                                            ALU.logical_shift_left)
                            v.tensor_scalar(magi[:, 2048:4096], w, 0x70, 18,
                                            ALU.bitwise_and,
                                            ALU.logical_shift_left)
                            v.tensor_scalar(magi[:, 0:2048], magi[:, 0:2048],
                                            KBITS, None, ALU.add)
                            v.tensor_scalar(magi[:, 2048:4096],
                                            magi[:, 2048:4096],
                                            KBITS, None, ALU.add)
                            ins = v.tensor_copy(tv[0:1, 0:2], mag[0:1, 0:2])
                            v_a.inc(ins, 1)
                            v.wait_ge(s_a.h, b_sa + i + 1)
                            v.tensor_tensor(mag[:], mag[:], corr[:],
                                            ALU.subtract)
                            v.tensor_scalar(sbits[:, 0:2048], w, 8, 28,
                                            ALU.bitwise_and,
                                            ALU.logical_shift_left)
                            v.tensor_scalar(sbits[:, 2048:4096], w, 0x80, 24,
                                            ALU.bitwise_and,
                                            ALU.logical_shift_left)
                            v.tensor_tensor(ssgni[:, 0:2048], sci,
                                            sbits[:, 0:2048], ALU.bitwise_or)
                            v.tensor_tensor(ssgni[:, 2048:4096], sci,
                                            sbits[:, 2048:4096], ALU.bitwise_or)
                            if i >= 2:
                                st_waits(v, i - 2)
                            v.tensor_tensor(val[i % 2][:], mag[:],
                                            ssgn[:], ALU.mult)
                            ins = v.tensor_copy(tv[0:1, 2:4], mag[0:1, 0:2])
                            v_b.inc(ins, 1)

                    @block.scalar
                    def _(s):
                        for i in range(NJ):
                            s.wait_ge(v_a.h, b_va + i + 1)
                            s.activation(corr[:], mag[:], AF.Relu,
                                         bias=1.0, scale=-1.0)
                            ins = s.copy(tsc[0:1, 0:2], corr[0:1, 0:2])
                            s_a.inc(ins, 1)
                            s.wait_ge(v_b.h, b_vb + i + 1)
                            _, _, dlo, dhi = jobs[i]
                            ncol = dlo.shape[2]
                            vv = val[i % 2][:]
                            lo = vv[:, 0:2048].rearrange("p (a n) -> p a n",
                                                         n=ncol)
                            hi = vv[:, 2048:4096].rearrange("p (a n) -> p a n",
                                                            n=ncol)
                            ins = s.dma_start(out=dlo, in_=lo)
                            st_sem(i).inc(ins, 16)
                            ins = s.dma_start(out=dhi, in_=hi)
                            st_sem(i).inc(ins, 16)
                        st_waits(s, NJ - 1)

        def emit_ln_T(src, xT, perm, prestats=False):
            with ExitStack() as es:
                uid_ctr[0] += 1
                _u = uid_ctr[0]
                sb = lambda n, s, d: es.enter_context(
                    nc.sbuf_tensor(f"{n}_{_u}", s, d))
                xr = [sb(f"ln_x{i}", [128, 2048], F32) for i in range(2)]
                lnr = [sb(f"ln_o{i}", [128, 2048], BF16) for i in range(2)]
                sums = sb("ln_sum", [128, 16], F32)
                sqs = sb("ln_sqs", [128, 16], F32)
                sq_scr = sb("ln_scr", [128, 2048], F32)
                nm = sb("ln_nm", [128, 16], F32)
                m2 = sb("ln_m2", [128, 16], F32)
                var = sb("ln_var", [128, 16], F32)
                sd = sb("ln_sd", [128, 16], F32)
                rstd = sb("ln_rstd", [128, 16], F32)
                xc = sb("ln_xc", [128, 2048], F32)
                tv = sb("ln_tv", [1, 8], F32)
                tsc = sb("ln_ts", [1, 8], F32)

                def cslice(ln_ap, dc):
                    if perm == "ffn":
                        jc, par = dc >> 1, dc & 1
                        base = 256 * jc + par
                        return ln_ap[:, base:base + 255:2]
                    return ln_ap[:, dc * 128:(dc + 1) * 128]

                b_s, b_v, b_sa = s_ld.v, v_b.v, s_a.v
                if not prestats:
                    with nc.Block() as block:
                        @block.sync
                        def _(sync):
                            for t in range(16):
                                if t >= 2:
                                    sync.wait_ge(s_a.h, b_sa + (t - 2) + 1)
                                    sync.wait_ge(v_b.h, b_v + (t - 2) + 1)
                                ins = sync.dma_start(
                                    out=xr[t % 2][:],
                                    in_=src[t * 128:(t + 1) * 128, :])
                                s_ld.inc(ins, 16)

                        @block.scalar
                        def _(s):
                            for t in range(16):
                                s.wait_ge(s_ld.h, b_s + 16 * (t + 1))
                                s.activation(sq_scr[:], xr[t % 2][:],
                                             AF.Square,
                                             accum_out=sqs[:, t:t + 1])
                                ins = s.copy(tsc[0:1, 0:2], sqs[0:1, 0:2])
                                s_a.inc(ins, 1)

                        @block.vector
                        def _(v):
                            for t in range(16):
                                v.wait_ge(s_ld.h, b_s + 16 * (t + 1))
                                ins = v.reduce_sum(sums[:, t:t + 1],
                                                   xr[t % 2][:],
                                                   axis=AX.X)
                                v_b.inc(ins, 1)

                if prestats:
                    with nc.Block() as block:
                        @block.vector
                        def _(v):
                            v.tensor_tensor(nm[:], sa_psum[:, 0::4],
                                            sa_psum[:, 1::4], ALU.add)
                            v.tensor_tensor(m2[:], sa_psum[:, 2::4],
                                            sa_psum[:, 3::4], ALU.add)
                            v.tensor_tensor(var[:], sa_psq[:, 0::4],
                                            sa_psq[:, 1::4], ALU.add)
                            v.tensor_tensor(sd[:], sa_psq[:, 2::4],
                                            sa_psq[:, 3::4], ALU.add)
                            for _pad in range(8):
                                v.tensor_copy(lnr[0][:][:, 0:512],
                                              xc[:][:, 0:512])
                            v.tensor_tensor(sums[:], nm[:], m2[:], ALU.add)
                            v.tensor_tensor(sqs[:], var[:], sd[:], ALU.add)
                b_v, b_sa = v_b.v, s_a.v
                with nc.Block() as block:
                    @block.vector
                    def _(v):
                        ins = v.tensor_scalar(nm[:], sums[:], -1.0 / D, None,
                                              ALU.mult)
                        v_b.inc(ins, 1)
                        v.wait_ge(s_a.h, b_sa + 1)
                        ins = v.scalar_tensor_tensor(
                            var[:], sqs[:], 1.0 / D, m2[:],
                            ALU.mult, ALU.subtract)
                        v_b.inc(ins, 1)
                        v.wait_ge(s_a.h, b_sa + 2)
                        ins = v.reciprocal(rstd[:], sd[:])
                        v_b.inc(ins, 1)

                    @block.scalar
                    def _(s):
                        s.wait_ge(v_b.h, b_v + 1)
                        ins = s.activation(m2[:], nm[:], AF.Square)
                        s_a.inc(ins, 1)
                        s.wait_ge(v_b.h, b_v + 2)
                        ins = s.activation(sd[:], var[:], AF.Sqrt, bias=1e-5)
                        s_a.inc(ins, 1)

                b_s, b_v, b_va, b_tp = s_ld.v, v_b.v, v_a.v, t_tp.v
                with nc.Block() as block:
                    @block.sync
                    def _(sync):
                        for t in range(16):
                            if t >= 2:
                                sync.wait_ge(v_b.h, b_v + (t - 2) + 1)
                            ins = sync.dma_start(
                                out=xr[t % 2][:],
                                in_=src[t * 128:(t + 1) * 128, :])
                            s_ld.inc(ins, 16)

                    @block.vector
                    def _(v):
                        xTv = xT[:].rearrange("p (c n) -> p c n", n=2048)
                        p6b = p[6][:].bitcast(BF16)
                        p7b = p[7][:].bitcast(BF16)
                        for t in range(16):
                            c = t % 2
                            v.wait_ge(s_ld.h, b_s + 16 * (t + 1))
                            v.tensor_tensor(
                                xc[:], xr[c][:],
                                nm[:, t:t + 1].broadcast_to([128, 2048]),
                                ALU.add)
                            if t >= 2:
                                v.wait_ge(t_tp.h, b_tp + 2 * (t - 2) + 2)
                            v.tensor_tensor(
                                lnr[c][:], xc[:],
                                rstd[:, t:t + 1].broadcast_to([128, 2048]),
                                ALU.mult)
                            ins = v.tensor_copy(tv[0:1, 4:6], tv[0:1, 0:2])
                            v_b.inc(ins, 1)
                            v.wait_ge(t_tp.h, b_tp + 2 * t + 1)
                            v.tensor_copy(
                                xTv[:, 0:8, t * 128:(t + 1) * 128],
                                p6b.rearrange("p (c n) -> p c n", n=128))
                            v.wait_ge(t_tp.h, b_tp + 2 * t + 2)
                            ins = v.tensor_copy(
                                xTv[:, 8:16, t * 128:(t + 1) * 128],
                                p7b.rearrange("p (c n) -> p c n", n=128))
                            v_a.inc(ins, 1)

                    @block.tensor
                    def _(te):
                        p6b = p[6][:].bitcast(BF16)
                        p7b = p[7][:].bitcast(BF16)
                        for t in range(16):
                            c = t % 2
                            te.wait_ge(v_b.h, b_v + t + 1)
                            if t >= 1:
                                te.wait_ge(v_a.h, b_va + t)
                            for dc in range(8):
                                ins = te.transpose(
                                    p6b[:, dc * 128:(dc + 1) * 128],
                                    cslice(lnr[c][:], dc), kc["ident"][:])
                            t_tp.inc(ins, 1)
                            for dc in range(8, 16):
                                ins = te.transpose(
                                    p7b[:, (dc - 8) * 128:(dc - 7) * 128],
                                    cslice(lnr[c][:], dc), kc["ident"][:])
                            t_tp.inc(ins, 1)

        def emit_ffn_a(xT):
            with ExitStack() as es:
                uid_ctr[0] += 1
                _u = uid_ctr[0]
                sb = lambda n, s, d: es.enter_context(
                    nc.sbuf_tensor(f"{n}_{_u}", s, d))
                wgb = [sb(f"fa_wg{i}", [128, 4096], BF16) for i in range(2)]
                wub = [sb(f"fa_wu{i}", [128, 4096], BF16) for i in range(2)]
                g2c = [sb(f"fa_g2c{i}", [128, 512], F32) for i in range(2)]
                arg = [sb(f"fa_arg{i}", [128, 512], F32) for i in range(2)]
                tnh = [sb(f"fa_tnh{i}", [128, 512], F32) for i in range(2)]
                gl = [[sb(f"fa_gl{s_}_{i}", [128, 512], F32)
                       for i in range(4)] for s_ in range(2)]
                stg = [sb(f"fa_st{i}", [128, 512], BF16) for i in range(8)]
                tv = sb("fa_tv", [1, 8], F32)
                tsc = sb("fa_ts", [1, 8], F32)
                b_s, b_mm, b_v, b_sa, b_qst = (s_ld.v, mm.v, v_a.v, s_a.v,
                                               q_st.v)

                def slot(g2, tq):
                    return (4 * g2 + tq) % 8

                with nc.Block() as block:
                    @block.sync
                    def _(sync):
                        for b in range(64):
                            if b >= 2:
                                sync.wait_ge(mm.h, b_mm + 16 * (b - 2) + 8)
                            ins = sync.dma_start(
                                out=wgb[b % 2][:].rearrange(
                                    "p (c n) -> p c n", n=256),
                                in_=wg_s[b])
                            s_ld.inc(ins, 16)
                            if b >= 2:
                                sync.wait_ge(mm.h, b_mm + 16 * (b - 2) + 16)
                            ins = sync.dma_start(
                                out=wub[b % 2][:].rearrange(
                                    "p (c n) -> p c n", n=256),
                                in_=wu_s[b])
                            s_ld.inc(ins, 16)

                    @block.tensor
                    def _(te):
                        for b in range(64):
                            for ph in range(4):
                                sub = ph % 2
                                off = 4 * sub
                                wblk = wgb[b % 2] if ph < 2 else wub[b % 2]
                                if ph == 0:
                                    te.wait_ge(s_ld.h, b_s + 32 * (b + 1))
                                    if b >= 1:
                                        te.wait_ge(v_a.h, b_v + 24 * b)
                                elif ph == 1:
                                    if b >= 1:
                                        te.wait_ge(v_a.h, b_v + 24 * b + 4)
                                elif ph == 2:
                                    te.wait_ge(v_a.h, b_v + 24 * b + 12)
                                    te.wait_ge(s_a.h, b_sa + 16 * b + 8)
                                else:
                                    te.wait_ge(v_a.h, b_v + 24 * b + 20)
                                    te.wait_ge(s_a.h, b_sa + 16 * b + 16)
                                for dc in range(16):
                                    for tq in range(4):
                                        ins = te.matmul(
                                            p[off + tq][:],
                                            wblk[:][:, dc * 256 + sub:
                                                    dc * 256 + 256:2],
                                            xT[:][:, dc * 2048 + tq * 512:
                                                  dc * 2048 + (tq + 1) * 512],
                                            start=(dc == 0), stop=(dc == 15))
                                        if dc == 15:
                                            mm.inc(ins, 1)

                    @block.scalar
                    def _(s):
                        def gl_scalar(b, sub):
                            for tq in range(2):
                                pass
                            for tq in range(4):
                                s.wait_ge(mm.h,
                                          b_mm + 16 * b + 4 * sub + tq + 1)
                                s.activation(g2c[tq % 2][:],
                                             p[4 * sub + tq][:], AF.Square,
                                             scale=SQC)
                                ins = s.copy(tsc[0:1, 0:2],
                                             g2c[tq % 2][0:1, 0:2])
                                s_a.inc(ins, 1)
                                s.wait_ge(v_a.h, b_v + 24 * b + 4 + 8 * sub +
                                          2 * tq + 1)
                                s.activation(tnh[tq % 2][:], arg[tq % 2][:],
                                             AF.Tanh)
                                ins = s.copy(tsc[0:1, 2:4],
                                             tnh[tq % 2][0:1, 0:2])
                                s_a.inc(ins, 1)

                        def st_inter(b, sub, vbase):
                            cc = 2 * b + sub
                            for tq in range(4):
                                s.wait_ge(v_a.h, vbase + tq + 1)
                                ins = s.dma_start(
                                    out=inter_s[tq, cc // 16, :, cc % 16, :],
                                    in_=stg[slot(cc, tq)][:])
                                q_st.inc(ins, 16)

                        for b in range(64):
                            if b >= 1:
                                st_inter(b - 1, 1, b_v + 24 * b)
                            gl_scalar(b, 0)
                            gl_scalar(b, 1)
                            st_inter(b, 0, b_v + 24 * b + 20)
                        st_inter(63, 1, b_v + 24 * 64)
                        s.wait_ge(q_st.h, b_qst + 16 * 512)

                    @block.vector
                    def _(v):
                        def inter_ops(b, sub, mmbase):
                            cc = 2 * b + sub
                            for tq in range(4):
                                v.wait_ge(mm.h, mmbase + tq + 1)
                                g2p = cc - 2
                                if g2p >= 0:
                                    v.wait_ge(q_st.h, b_qst +
                                              16 * (4 * g2p + tq + 1))
                                v.tensor_tensor(stg[slot(cc, tq)][:],
                                                gl[sub][tq][:],
                                                p[4 * sub + tq][:], ALU.mult)
                                ins = v.tensor_copy(
                                    tv[0:1, 0:2],
                                    stg[slot(cc, tq)][0:1, 0:2])
                                v_a.inc(ins, 1)

                        def gl_vector(b, sub):
                            for tq in range(4):
                                v.wait_ge(s_a.h, b_sa + 16 * b + 8 * sub +
                                          2 * tq + 1)
                                v.scalar_tensor_tensor(
                                    arg[tq % 2][:], g2c[tq % 2][:], C0,
                                    p[4 * sub + tq][:], ALU.add, ALU.mult)
                                ins = v.tensor_copy(tv[0:1, 2:4],
                                                    arg[tq % 2][0:1, 0:2])
                                v_a.inc(ins, 1)
                                v.wait_ge(s_a.h, b_sa + 16 * b + 8 * sub +
                                          2 * tq + 2)
                                v.scalar_tensor_tensor(
                                    gl[sub][tq][:], tnh[tq % 2][:], 1.0,
                                    p[4 * sub + tq][:], ALU.add, ALU.mult)
                                ins = v.tensor_copy(tv[0:1, 4:6],
                                                    gl[sub][tq][0:1, 0:2])
                                v_a.inc(ins, 1)

                        for b in range(64):
                            if b >= 1:
                                inter_ops(b - 1, 1,
                                          b_mm + 16 * (b - 1) + 12)
                            else:
                                for _d in range(4):
                                    ins = v.tensor_copy(tv[0:1, 6:8],
                                                        tv[0:1, 0:2])
                                    v_a.inc(ins, 1)
                            gl_vector(b, 0)
                            gl_vector(b, 1)
                            inter_ops(b, 0, b_mm + 16 * b + 8)
                        inter_ops(63, 1, b_mm + 16 * 63 + 12)

        def emit_ffn_b(x_src, dst, final=False, stats=False):
            with ExitStack() as es:
                uid_ctr[0] += 1
                _u = uid_ctr[0]
                sb = lambda n, s, d: es.enter_context(
                    nc.sbuf_tensor(f"{n}_{_u}", s, d))
                inter_a = sb("fb_inta", [128, 32768], BF16)
                inter_b = sb("fb_intb", [128, 32768], BF16)
                wdr = [sb(f"fb_wd{i}", [128, 8192], BF16) for i in range(2)]
                xres = [sb(f"fb_xr{i}", [128, 2048], F32) for i in range(2)]
                ystg = [sb(f"fb_ys{i}", [128, 2048], F32) for i in range(2)]
                tv = sb("fb_tv", [1, 8], F32)
                b_s, b_mm, b_va, b_qld, b_qst = (s_ld.v, mm.v, v_a.v, q_ld.v,
                                                 q_st.v)

                def xtile(t_ap, tb, hs):
                    return t_ap[tb * 512:(tb + 1) * 512,
                                hs * 512:(hs + 1) * 512].rearrange(
                                    "(a p) n -> p a n", p=128)

                ld_plan = []
                ipos = {}
                wpos = {}
                _np = [0, 0]

                def plan_eighth(tb, e):
                    gate = None
                    if tb >= 1:
                        gate = 8 * (4 * (tb - 1) + 3) + e + 1
                    ld_plan.append(("i", tb, e, gate))
                    _np[0] += 1
                    ipos[(tb, e)] = _np[0]

                def plan_wd(tb, hs, g):
                    gs = (tb * 4 + hs) * 8 + g
                    gate = gs - 1 if gs >= 2 else None
                    ld_plan.append(("w", tb, hs, g, gate))
                    _np[1] += 1
                    wpos[(tb, hs, g)] = _np[1]

                for k in range(8):
                    plan_eighth(0, k)
                    plan_wd(0, 0, k)
                for tb in range(4):
                    for hs in range(3):
                        if (tb, hs) != (0, 0):
                            for g in range(8):
                                plan_wd(tb, hs, g)
                    if tb < 3:
                        plan_wd(tb, 3, 0)
                        plan_wd(tb, 3, 1)
                        for k in range(6):
                            plan_eighth(tb + 1, k)
                            plan_wd(tb, 3, k + 2)
                        plan_eighth(tb + 1, 6)
                        plan_eighth(tb + 1, 7)
                    else:
                        for g in range(8):
                            plan_wd(tb, 3, g)

                b_il = i_ld.v
                with nc.Block() as block:
                    @block.gpsimd
                    def _(gp):
                        for item in ld_plan:
                            if item[0] != "i":
                                continue
                            _, tb, e, gate = item
                            if gate is not None:
                                gp.wait_ge(mm.h, b_mm + gate)
                            half = inter_a if e < 4 else inter_b
                            ee = e % 4
                            ins = gp.dma_start(
                                out=half[:][:, ee * 8192:(ee + 1) * 8192]
                                .rearrange("p (c n) -> p c n", n=512),
                                in_=inter_s[tb, e])
                            i_ld.inc(ins, 16)

                    @block.sync
                    def _(sync):
                        for item in ld_plan:
                            if item[0] != "w":
                                continue
                            _, tb, hs, g, gate = item
                            if gate is not None:
                                sync.wait_ge(mm.h, b_mm + gate)
                            gs = (tb * 4 + hs) * 8 + g
                            ins = sync.dma_start(
                                out=wdr[gs % 2][:].rearrange(
                                    "p (c n) -> p c n", n=512),
                                in_=wd_s[4 * g + hs])
                            s_ld.inc(ins, 16)

                    @block.tensor
                    def _(te):
                        for tb in range(4):
                            for hs in range(4):
                                idx = 4 * tb + hs
                                if idx >= 2:
                                    te.wait_ge(v_a.h, b_va + idx - 1)
                                pd = [p[(idx % 2) * 4 + tt] for tt in range(4)]
                                for c in range(128):
                                    g = c // 16
                                    gs = (tb * 4 + hs) * 8 + g
                                    if c % 16 == 0:
                                        te.wait_ge(i_ld.h,
                                                   b_il + 16 * ipos[(tb, g)])
                                        te.wait_ge(s_ld.h,
                                                   b_s + 16 * wpos[(tb, hs,
                                                                    g)])
                                    ibuf = inter_a if c < 64 else inter_b
                                    cc = c % 64
                                    for tt in range(4):
                                        ins = te.matmul(
                                            pd[tt][:],
                                            ibuf[:][:,
                                                    cc * 512 + tt * 128:
                                                    cc * 512 + (tt + 1) * 128],
                                            wdr[gs % 2][:][:,
                                                           (c % 16) * 512:
                                                           (c % 16 + 1) * 512],
                                            start=(c == 0), stop=(c == 127))
                                    if c % 16 == 15:
                                        mm.inc(ins, 1)

                    @block.scalar
                    def _(s):
                        for idx in range(16):
                            tb, hs = divmod(idx, 4)
                            if idx == 0:
                                ins = s.dma_start(
                                    out=xres[0][:].rearrange(
                                        "p (a n) -> p a n", n=512),
                                    in_=xtile(x_src, 0, 0))
                                q_ld.inc(ins, 16)
                            if idx + 1 < 16:
                                tb2, hs2 = divmod(idx + 1, 4)
                                if idx >= 1:
                                    s.wait_ge(v_a.h, b_va + idx)
                                ins = s.dma_start(
                                    out=xres[(idx + 1) % 2][:].rearrange(
                                        "p (a n) -> p a n", n=512),
                                    in_=xtile(x_src, tb2, hs2))
                                q_ld.inc(ins, 16)
                            s.wait_ge(v_a.h, b_va + idx + 1)
                            ins = s.dma_start(
                                out=xtile(dst, tb, hs),
                                in_=ystg[idx % 2][:].rearrange(
                                    "p (a n) -> p a n", n=512))
                            q_st.inc(ins, 16)
                        s.wait_ge(q_st.h, b_qst + 16 * 16)

                    @block.vector
                    def _(v):
                        for idx in range(16):
                            pd = [p[(idx % 2) * 4 + tt] for tt in range(4)]
                            v.wait_ge(mm.h, b_mm + 8 * (idx + 1))
                            v.wait_ge(q_ld.h, b_qld + 16 * (idx + 1))
                            if idx >= 2:
                                v.wait_ge(q_st.h, b_qst + 16 * (idx - 1))
                            for tt in range(4):
                                v.scalar_tensor_tensor(
                                    ystg[idx % 2][:][:,
                                                     tt * 512:(tt + 1) * 512],
                                    pd[tt][:], 0.25,
                                    xres[idx % 2][:][:,
                                                     tt * 512:(tt + 1) * 512],
                                    ALU.mult, ALU.add)
                            ins = v.tensor_copy(tv[0:1, 0:2],
                                                ystg[idx % 2][0:1, 0:2])
                            v_a.inc(ins, 1)
                            if stats:
                                tbs, hss = divmod(idx, 4)
                                for a in range(4):
                                    ysl = ystg[idx % 2][:][:,
                                                          a * 512:
                                                          (a + 1) * 512]
                                    col = (4 * tbs + a) * 4 + hss
                                    v.reduce_sum(
                                        sa_psum[:, col:col + 1], ysl,
                                        axis=AX.X)
                                    v.tensor_tensor(sa_scr[:], ysl, ysl,
                                                    ALU.mult)
                                    v.reduce_sum(
                                        sa_psq[:, col:col + 1], sa_scr[:],
                                        axis=AX.X)

        def emit_attn_proj(xT):
            with ExitStack() as es:
                uid_ctr[0] += 1
                _u = uid_ctr[0]
                sb = lambda n, s, d: es.enter_context(
                    nc.sbuf_tensor(f"{n}_{_u}", s, d))
                wqs = [sb(f"ap_wq{i}", [128, 8192], BF16) for i in range(2)]
                wks = [sb(f"ap_wk{i}", [128, 8192], BF16) for i in range(2)]
                sqq = sb("ap_sqq", [128, 512], F32)
                sqk = sb("ap_sqk", [128, 512], F32)
                qss = sb("ap_qss", [128, 4], F32)
                kss = sb("ap_kss", [128, 4], F32)
                sdq = sb("ap_sdq", [128, 4], F32)
                sdk = sb("ap_sdk", [128, 4], F32)
                rq_all = sb("ap_rqa", [128, 256], F32)
                rk_all = sb("ap_rka", [128, 256], F32)
                qraw_st = [sb(f"ap_qr{i}", [128, 512], BF16) for i in range(2)]
                kvraw_st = [sb(f"ap_kr{i}", [128, 512], BF16) for i in range(2)]
                qn_st = [sb(f"ap_qn{i}", [128, 512], BF16)
                         for i in range(2)]
                kvn_st = [sb(f"ap_kv{i}", [128, 512], BF16) for i in range(2)]
                qnT_sp = [sb(f"ap_qT{i}", [128, 8192], BF16) for i in range(2)]
                tv = sb("ap_tv", [1, 8], F32)
                tsc = sb("ap_ts", [1, 8], F32)
                rqv = rq_all[:].rearrange("p (u h) -> p u h", h=4)
                rkv = rk_all[:].rearrange("p (u h) -> p u h", h=4)

                b_g, b_mm, b_sa, b_vb, b_qst = (g_ld.v, mm.v, s_a.v, v_b.v,
                                                q_st.v)
                with nc.Block() as block:
                    @block.gpsimd
                    def _(gp):
                        for sp in range(4):
                            if sp >= 2:
                                gp.wait_ge(mm.h, b_mm + 32 * (sp - 1))
                            srcw = wqT[:, sp * 512:(sp + 1) * 512].rearrange(
                                "(c p) n -> p c n", p=128)
                            ins = gp.dma_start(
                                out=wqs[sp % 2][:].rearrange(
                                    "p (c n) -> p c n", n=512), in_=srcw)
                            g_ld.inc(ins, 16)
                            srcw = wkvT[:, sp * 512:(sp + 1) * 512].rearrange(
                                "(c p) n -> p c n", p=128)
                            ins = gp.dma_start(
                                out=wks[sp % 2][:].rearrange(
                                    "p (c n) -> p c n", n=512), in_=srcw)
                            g_ld.inc(ins, 16)

                    @block.tensor
                    def _(te):
                        for u in range(64):
                            sp, t = divmod(u, 16)
                            if t == 0:
                                te.wait_ge(g_ld.h, b_g + 32 * (sp + 1))
                            if u >= 2:
                                te.wait_ge(v_b.h, b_vb + 3 * (u - 2) + 3)
                                te.wait_ge(s_a.h, b_sa + 4 * (u - 2) + 4)
                            pq = p[2 * (u % 2)]
                            pkv = p[2 * (u % 2) + 1]
                            for ps, wsp in ((pq, wqs[sp % 2]),
                                            (pkv, wks[sp % 2])):
                                for dc in range(16):
                                    ins = te.matmul(
                                        ps[:],
                                        xT[:][:, dc * 2048 + t * 128:
                                              dc * 2048 + (t + 1) * 128],
                                        wsp[:][:, dc * 512:(dc + 1) * 512],
                                        start=(dc == 0), stop=(dc == 15))
                                mm.inc(ins, 1)

                    @block.scalar
                    def _(s):
                        for u in range(64):
                            s.wait_ge(mm.h, b_mm + 2 * u + 1)
                            if u >= 1:
                                s.wait_ge(v_b.h, b_vb + 3 * (u - 1) + 1)
                            pq = p[2 * (u % 2)]
                            pkv = p[2 * (u % 2) + 1]
                            s.activation(sqq[:], pq[:], AF.Square)
                            ins = s.copy(tsc[0:1, 0:2], sqq[0:1, 0:2])
                            s_a.inc(ins, 1)
                            s.wait_ge(v_b.h, b_vb + 3 * u + 1)
                            s.activation(sdq[:], qss[:], AF.Sqrt, scale=128.0)
                            ins = s.copy(tsc[0:1, 2:4], sdq[0:1, 0:2])
                            s_a.inc(ins, 1)
                            s.wait_ge(mm.h, b_mm + 2 * u + 2)
                            s.activation(sqk[:], pkv[:], AF.Square)
                            ins = s.copy(tsc[0:1, 4:6], sqk[0:1, 0:2])
                            s_a.inc(ins, 1)
                            s.wait_ge(v_b.h, b_vb + 3 * u + 2)
                            s.activation(sdk[:], kss[:], AF.Sqrt, scale=128.0)
                            ins = s.copy(tsc[0:1, 6:8], sdk[0:1, 0:2])
                            s_a.inc(ins, 1)
                            sp, t = divmod(u, 16)
                            s.wait_ge(v_b.h, b_vb + 3 * u + 3)
                            ins = s.dma_start(
                                out=q_raw_s[sp, t],
                                in_=qraw_st[u % 2][:])
                            q_st.inc(ins, 16)
                            ins = s.dma_start(
                                out=kv_raw_s[sp, t],
                                in_=kvraw_st[u % 2][:])
                            q_st.inc(ins, 16)
                        s.wait_ge(q_st.h, b_qst + 32 * 64)

                    @block.vector
                    def _(v):
                        for u in range(64):
                            pq = p[2 * (u % 2)]
                            pkv = p[2 * (u % 2) + 1]
                            v.wait_ge(s_a.h, b_sa + 4 * u + 1)
                            v.reduce_sum(qss[:].unsqueeze(2),
                                         sqq[:].rearrange(
                                             "p (a b) -> p a b", b=128),
                                         axis=AX.X)
                            ins = v.tensor_copy(tv[0:1, 0:2], qss[0:1, 0:2])
                            v_b.inc(ins, 1)
                            v.wait_ge(s_a.h, b_sa + 4 * u + 2)
                            v.reciprocal(rqv[:, u, :], sdq[:])
                            if u >= 2:
                                v.wait_ge(q_st.h, b_qst + 32 * (u - 1))
                            v.tensor_copy(qraw_st[u % 2][:], pq[:])
                            v.wait_ge(s_a.h, b_sa + 4 * u + 3)
                            v.reduce_sum(kss[:].unsqueeze(2),
                                         sqk[:].rearrange(
                                             "p (a b) -> p a b", b=128),
                                         axis=AX.X)
                            ins = v.tensor_copy(tv[0:1, 2:4], kss[0:1, 0:2])
                            v_b.inc(ins, 1)
                            v.wait_ge(s_a.h, b_sa + 4 * u + 4)
                            v.reciprocal(rkv[:, u, :], sdk[:])
                            v.tensor_copy(kvraw_st[u % 2][:], pkv[:])
                            ins = v.tensor_copy(tv[0:1, 4:6], tv[0:1, 0:2])
                            v_b.inc(ins, 1)

                b_s, b_mm, b_vb, b_tp, b_qst, b_s2 = (
                    s_ld.v, mm.v, v_b.v, t_tp.v, q_st.v, s2_st.v)
                with nc.Block() as block:
                    @block.sync
                    def _(sync):
                        for u in range(64):
                            sp, t = divmod(u, 16)
                            if u >= 2:
                                sync.wait_ge(v_b.h, b_vb + 3 * (u - 2) + 3)
                            ins = sync.dma_start(
                                out=qraw_st[u % 2][:],
                                in_=q_raw_s[sp, t])
                            s_ld.inc(ins, 16)
                            ins = sync.dma_start(
                                out=kvraw_st[u % 2][:],
                                in_=kv_raw_s[sp, t])
                            s_ld.inc(ins, 16)

                    @block.gpsimd
                    def _(gp):
                        for sp in range(4):
                            gp.wait_ge(v_b.h,
                                       b_vb + 3 * (16 * sp + 16) + 1)
                            qTv = qnT_sp[sp % 2][:].rearrange(
                                "p (h n) -> p h n", n=2048)
                            for h in range(4):
                                ins = gp.dma_start(
                                    out=qnT_s[(4 * sp + h) * 128:
                                              (4 * sp + h + 1) * 128, :],
                                    in_=qTv[:, h, :])
                                s2_st.inc(ins, 16)
                        gp.wait_ge(s2_st.h, b_s2 + 16 * 16)

                    @block.vector
                    def _(v):
                        def qcopy(uc):
                            spc, tc = divmod(uc, 16)
                            if tc == 0 and spc >= 2:
                                v.wait_ge(s2_st.h, b_s2 + 64 * (spc - 1))
                            v.wait_ge(t_tp.h, b_tp + uc + 1)
                            pub = p[6 + uc % 2][:].bitcast(BF16)
                            qTv = qnT_sp[spc % 2][:].rearrange(
                                "p (h n) -> p h n", n=2048)
                            ins = v.tensor_copy(
                                qTv[:, :, tc * 128:(tc + 1) * 128],
                                pub[:, 0:512].rearrange(
                                    "p (h n) -> p h n", n=128))
                            v_b.inc(ins, 1)

                        for u in range(64):
                            sp, t = divmod(u, 16)
                            if u >= 1:
                                qcopy(u - 1)
                            else:
                                ins = v.tensor_copy(tv[0:1, 0:2],
                                                    tv[0:1, 2:4])
                                v_b.inc(ins, 1)
                            v.wait_ge(s_ld.h, b_s + 32 * (u + 1))
                            if u >= 1:
                                v.wait_ge(t_tp.h, b_tp + u - 1)
                            v.tensor_tensor(
                                qn_st[u % 2][:].rearrange(
                                    "p (a b) -> p a b", b=128),
                                qraw_st[u % 2][:].rearrange(
                                    "p (a b) -> p a b", b=128),
                                rqv[:, u, :].unsqueeze(2).broadcast_to(
                                    [128, 4, 128]),
                                ALU.mult)
                            ins = v.tensor_copy(tv[0:1, 0:2], tv[0:1, 2:4])
                            v_b.inc(ins, 1)
                            if u >= 2:
                                v.wait_ge(q_st.h, b_qst + 16 * (u - 1))
                            v.tensor_tensor(
                                kvn_st[u % 2][:].rearrange(
                                    "p (a b) -> p a b", b=128),
                                kvraw_st[u % 2][:].rearrange(
                                    "p (a b) -> p a b", b=128),
                                rkv[:, u, :].unsqueeze(2).broadcast_to(
                                    [128, 4, 128]),
                                ALU.mult)
                            ins = v.tensor_copy(tv[0:1, 2:4], tv[0:1, 4:6])
                            v_b.inc(ins, 1)
                        qcopy(63)

                    @block.tensor
                    def _(te):
                        for u in range(64):
                            pub = p[6 + u % 2][:].bitcast(BF16)
                            te.wait_ge(v_b.h, b_vb + 3 * u + 2)
                            for h in range(4):
                                ins = te.transpose(
                                    pub[:, h * 128:(h + 1) * 128],
                                    qn_st[u % 2][:][:, h * 128:(h + 1) * 128],
                                    kc["ident"][:])
                            t_tp.inc(ins, 1)

                    @block.scalar
                    def _(s):
                        for u in range(64):
                            sp, t = divmod(u, 16)
                            s.wait_ge(v_b.h, b_vb + 3 * u + 3)
                            ins = s.dma_start(
                                out=kvn_s[sp, :, t, :],
                                in_=kvn_st[u % 2][:])
                            q_st.inc(ins, 16)
                        s.wait_ge(q_st.h, b_qst + 16 * 64)

        def emit_cumsum():
            with ExitStack() as es:
                uid_ctr[0] += 1
                _u = uid_ctr[0]
                sb = lambda n, s, d: es.enter_context(
                    nc.sbuf_tensor(f"{n}_{_u}", s, d))
                kvsp = [sb(f"cs_kv{i}", [128, 8192], BF16) for i in range(2)]
                kvc_st = [sb(f"cs_st{i}", [128, 512], BF16)
                          for i in range(2)]
                kvcT_sp = [sb(f"cs_T{i}", [128, 8192], BF16) for i in range(2)]
                b_s, b_mm, b_va, b_vb, b_tp, b_s2 = (s_ld.v, mm.v, v_a.v,
                                                     v_b.v, t_tp.v, s2_st.v)

                with nc.Block() as block:
                    @block.sync
                    def _(sync):
                        for sp in range(4):
                            if sp >= 2:
                                sync.wait_ge(mm.h, b_mm + 16 * (sp - 1))
                            src = kvn_s[sp]
                            ins = sync.dma_start(
                                out=kvsp[sp % 2][:].rearrange(
                                    "p (a n) -> p a n", n=512), in_=src)
                            s_ld.inc(ins, 16)
                            sync.wait_ge(v_b.h, b_vb + 16 * (sp + 1))
                            Tv = kvcT_sp[sp % 2][:].rearrange(
                                "p (h n) -> p h n", n=2048)
                            for h in range(4):
                                ins = sync.dma_start(
                                    out=kvcT_s[(4 * sp + h) * 128:
                                               (4 * sp + h + 1) * 128, :],
                                    in_=Tv[:, h, :])
                                s2_st.inc(ins, 16)
                        sync.wait_ge(s2_st.h, b_s2 + 16 * 16)

                    @block.tensor
                    def _(te):
                        p6b = p[6][:].bitcast(BF16)
                        for u in range(64):
                            sp, tout = divmod(u, 16)
                            if tout == 0:
                                te.wait_ge(s_ld.h, b_s + 16 * (sp + 1))
                            pc = p[u % 2]
                            if u >= 2:
                                te.wait_ge(v_a.h, b_va + u - 1)
                            kvv = kvsp[sp % 2][:].rearrange(
                                "p (a n) -> p a n", n=512)
                            for tp in range(tout + 1):
                                ins = te.matmul(
                                    pc[:],
                                    kc["tri"][:] if tp == tout
                                    else kc["ones"][:],
                                    kvv[:, tp, :],
                                    start=(tp == 0), stop=(tp == tout))
                            mm.inc(ins, 1)
                            te.wait_ge(v_a.h, b_va + u + 1)
                            if u >= 2:
                                te.wait_ge(v_b.h, b_vb + u - 1)
                            pub = p[6 + u % 2][:].bitcast(BF16)
                            for h in range(4):
                                ins = te.transpose(
                                    pub[:, h * 128:(h + 1) * 128],
                                    kvc_st[u % 2][:][:, h * 128:(h + 1) * 128],
                                    kc["ident"][:])
                            t_tp.inc(ins, 1)

                    @block.vector
                    def _(v):
                        for u in range(64):
                            sp, tout = divmod(u, 16)
                            pc = p[u % 2]
                            v.wait_ge(mm.h, b_mm + u + 1)
                            ins = v.tensor_copy(kvc_st[u % 2][:], pc[:])
                            v_a.inc(ins, 1)
                            if tout == 0 and sp >= 2:
                                v.wait_ge(s2_st.h, b_s2 + 64 * (sp - 1))
                            v.wait_ge(t_tp.h, b_tp + u + 1)
                            pub = p[6 + u % 2][:].bitcast(BF16)
                            Tv = kvcT_sp[sp % 2][:].rearrange(
                                "p (h n) -> p h n", n=2048)
                            ins = v.tensor_copy(
                                Tv[:, :, tout * 128:(tout + 1) * 128],
                                pub[:, 0:512].rearrange(
                                    "p (h n) -> p h n", n=128))
                            v_b.inc(ins, 1)

        def emit_heads():
            with ExitStack() as es:
                uid_ctr[0] += 1
                _u = uid_ctr[0]
                sb = lambda n, s, d: es.enter_context(
                    nc.sbuf_tensor(f"{n}_{_u}", s, d))
                qh = [sb(f"hd_q{i}", [128, 2048], BF16) for i in range(2)]
                kvh = [sb(f"hd_k{i}", [128, 2048], BF16) for i in range(2)]
                qfr = sb("hd_qfr", [128, 2048], F32)
                qfi = sb("hd_qfi", [128, 2048], F32)
                kr = sb("hd_kr", [128, 2048], F32)
                ki = sb("hd_ki", [128, 2048], F32)
                t1 = sb("hd_t1", [128, 2048], F32)
                u1 = sb("hd_u1", [128, 2048], F32)
                qvr = sb("hd_qvr", [128, 2048], BF16)
                qvi = sb("hd_qvi", [128, 2048], BF16)
                vals_st = [sb(f"hd_v{i}", [128, 2048], BF16) for i in range(2)]
                b_s, b_mm, b_sa, b_vb, b_qst = (s_ld.v, mm.v, s_a.v, v_b.v,
                                                q_st.v)

                with nc.Block() as block:
                    @block.sync
                    def _(sync):
                        for h in range(16):
                            if h >= 2:
                                sync.wait_ge(mm.h, b_mm + 12 * (h - 1))
                            ins = sync.dma_start(
                                out=qh[h % 2][:],
                                in_=qnT_s[h * 128:(h + 1) * 128, :])
                            s_ld.inc(ins, 16)
                            ins = sync.dma_start(
                                out=kvh[h % 2][:],
                                in_=kvcT_s[h * 128:(h + 1) * 128, :])
                            s_ld.inc(ins, 16)

                    @block.tensor
                    def _(te):
                        for h in range(16):
                            te.wait_ge(s_ld.h, b_s + 32 * (h + 1))
                            if h >= 1:
                                te.wait_ge(v_b.h, b_vb + 4 * h)
                                te.wait_ge(s_a.h, b_sa + 12 * (h - 1) + 8)
                            g = 0
                            for srcb in (qh[h % 2], kvh[h % 2]):
                                for sp4 in range(4):
                                    pa = p[2 * (g % 2)]
                                    pb = p[2 * (g % 2) + 1]
                                    if g >= 2:
                                        te.wait_ge(s_a.h,
                                                   b_sa + 12 * h + g - 1)
                                    sl = srcb[:][:, sp4 * 512:(sp4 + 1) * 512]
                                    te.matmul(pa[:], kc["cf"][:], sl,
                                              start=True, stop=True)
                                    ins = te.matmul(pb[:], kc["sf"][:], sl,
                                                    start=True, stop=True)
                                    mm.inc(ins, 1)
                                    g += 1
                            if h >= 1:
                                te.wait_ge(s_a.h, b_sa + 12 * h)
                            for sp4 in range(4):
                                pv = p[4 + sp4 % 2]
                                te.wait_ge(v_b.h, b_vb + 4 * h + sp4 + 1)
                                if sp4 >= 2:
                                    te.wait_ge(s_a.h,
                                               b_sa + 12 * h + 8 + sp4 - 1)
                                te.matmul(pv[:], kc["ci"][:],
                                          qvr[:][:, sp4 * 512:(sp4 + 1) * 512],
                                          start=True, stop=False)
                                ins = te.matmul(
                                    pv[:], kc["si"][:],
                                    qvi[:][:, sp4 * 512:(sp4 + 1) * 512],
                                    start=False, stop=True)
                                mm.inc(ins, 1)

                    @block.scalar
                    def _(s):
                        for h in range(16):
                            dsts = [(qfr, qfi), (qfr, qfi), (qfr, qfi),
                                    (qfr, qfi), (kr, ki), (kr, ki), (kr, ki),
                                    (kr, ki)]
                            for g in range(8):
                                sp4 = g % 4
                                pa = p[2 * (g % 2)]
                                pb = p[2 * (g % 2) + 1]
                                s.wait_ge(mm.h, b_mm + 12 * h + g + 1)
                                if h >= 1 and g == 0:
                                    s.wait_ge(v_b.h, b_vb + 4 * h)
                                da, db = dsts[g]
                                s.activation(
                                    da[:][:, sp4 * 512:(sp4 + 1) * 512],
                                    pa[:], AF.Copy)
                                ins = s.activation(
                                    db[:][:, sp4 * 512:(sp4 + 1) * 512],
                                    pb[:], AF.Copy)
                                s_a.inc(ins, 1)
                            for sp4 in range(4):
                                pv = p[4 + sp4 % 2]
                                s.wait_ge(mm.h, b_mm + 12 * h + 8 + sp4 + 1)
                                if sp4 == 0 and h >= 1:
                                    s.wait_ge(q_st.h, b_qst + 16 * (h - 1))
                                ins = s.activation(
                                    vals_st[h % 2][:][:,
                                                      sp4 * 512:(sp4 + 1) * 512],
                                    pv[:], AF.Copy)
                                s_a.inc(ins, 1)
                            s.wait_ge(s_a.h, b_sa + 12 * h + 12)
                            ins = s.dma_start(
                                out=vals_s[:, :, h, :]
                                .transpose([1, 0, 2]),
                                in_=vals_st[h % 2][:].rearrange(
                                    "p (a n) -> p a n", n=512))
                            q_st.inc(ins, 16)
                        s.wait_ge(q_st.h, b_qst + 16 * 16)

                    @block.vector
                    def _(v):
                        for h in range(16):
                            for sp4 in range(4):
                                sl = slice(sp4 * 512, (sp4 + 1) * 512)
                                v.wait_ge(s_a.h, b_sa + 12 * h + 5 + sp4)
                                v.tensor_tensor(t1[:][:, sl], qfr[:][:, sl],
                                                kr[:][:, sl], ALU.mult)
                                v.tensor_tensor(u1[:][:, sl], qfi[:][:, sl],
                                                ki[:][:, sl], ALU.mult)
                                v.tensor_tensor(qvr[:][:, sl], t1[:][:, sl],
                                                u1[:][:, sl], ALU.subtract)
                                v.tensor_tensor(t1[:][:, sl], qfr[:][:, sl],
                                                ki[:][:, sl], ALU.mult)
                                v.tensor_tensor(u1[:][:, sl], qfi[:][:, sl],
                                                kr[:][:, sl], ALU.mult)
                                ins = v.tensor_tensor(qvi[:][:, sl],
                                                      t1[:][:, sl],
                                                      u1[:][:, sl], ALU.add)
                                v_b.inc(ins, 1)

        def emit_wo():
            with ExitStack() as es:
                uid_ctr[0] += 1
                _u = uid_ctr[0]
                sb = lambda n, s, d: es.enter_context(
                    nc.sbuf_tensor(f"{n}_{_u}", s, d))
                wo_sb = sb("wo_w", [128, 32768], BF16)
                vals_sb = [sb(f"wo_v{i}", [128, 8192], BF16) for i in range(2)]
                xres = [sb(f"wo_xr{i}", [128, 2048], F32) for i in range(2)]
                ystg = [sb(f"wo_ys{i}", [128, 8192], F32) for i in range(2)]
                b_g, b_s, b_mm, b_va, b_qld, b_qst = (g_ld.v, s_ld.v, mm.v,
                                                      v_a.v, q_ld.v, q_st.v)

                def xtile(t_ap, tb, dsp):
                    return t_ap[tb * 512:(tb + 1) * 512,
                                dsp * 512:(dsp + 1) * 512].rearrange(
                                    "(a p) n -> p a n", p=128)

                with nc.Block() as block:
                    @block.gpsimd
                    def _(gp):
                        src = woT.rearrange("(c p) n -> p c n", p=128)
                        ins = gp.dma_start(
                            out=wo_sb[:].rearrange("p (c n) -> p c n", n=2048),
                            in_=src)
                        g_ld.inc(ins, 16)

                    @block.sync
                    def _(sync):
                        for tb in range(4):
                            if tb >= 2:
                                sync.wait_ge(mm.h, b_mm + 4 * (tb - 1))
                            src = vals_s[tb]
                            ins = sync.dma_start(
                                out=vals_sb[tb % 2][:].rearrange(
                                    "p (c n) -> p c n", n=512), in_=src)
                            s_ld.inc(ins, 16)

                    @block.tensor
                    def _(te):
                        te.wait_ge(g_ld.h, b_g + 16)
                        for idx in range(16):
                            tb, dsp = divmod(idx, 4)
                            if dsp == 0:
                                te.wait_ge(s_ld.h, b_s + 16 * (tb + 1))
                            if idx >= 2:
                                te.wait_ge(v_a.h, b_va + idx - 1)
                            pd = [p[(idx % 2) * 4 + tt] for tt in range(4)]
                            vv = vals_sb[tb % 2][:].rearrange(
                                "p (c n) -> p c n", n=512)
                            for oc in range(16):
                                for tt in range(4):
                                    ins = te.matmul(
                                        pd[tt][:],
                                        vv[:, oc, tt * 128:(tt + 1) * 128],
                                        wo_sb[:][:, oc * 2048 + dsp * 512:
                                                 oc * 2048 + (dsp + 1) * 512],
                                        start=(oc == 0), stop=(oc == 15))
                            mm.inc(ins, 1)

                    @block.scalar
                    def _(s):
                        for idx in range(16):
                            tb, dsp = divmod(idx, 4)
                            if idx == 0:
                                ins = s.dma_start(
                                    out=xres[0][:].rearrange(
                                        "p (a n) -> p a n", n=512),
                                    in_=xtile(x1_s, 0, 0))
                                q_ld.inc(ins, 16)
                            if idx + 1 < 16:
                                tb2, dsp2 = divmod(idx + 1, 4)
                                if idx >= 1:
                                    s.wait_ge(v_a.h, b_va + idx)
                                ins = s.dma_start(
                                    out=xres[(idx + 1) % 2][:].rearrange(
                                        "p (a n) -> p a n", n=512),
                                    in_=xtile(x1_s, tb2, dsp2))
                                q_ld.inc(ins, 16)
                            if dsp == 3:
                                s.wait_ge(v_a.h, b_va + idx + 1)
                                ins = s.dma_start(
                                    out=x2_s[tb * 512:(tb + 1) * 512, :]
                                    .rearrange("(a p) n -> p a n", p=128),
                                    in_=ystg[tb % 2][:].rearrange(
                                        "p (a n) -> p a n", n=2048))
                                q_st.inc(ins, 16)
                        s.wait_ge(q_st.h, b_qst + 16 * 4)

                    @block.vector
                    def _(v):
                        for idx in range(16):
                            tbs, dss = divmod(idx, 4)
                            pd = [p[(idx % 2) * 4 + tt] for tt in range(4)]
                            v.wait_ge(mm.h, b_mm + idx + 1)
                            v.wait_ge(q_ld.h, b_qld + 16 * (idx + 1))
                            if dss == 0 and tbs >= 2:
                                v.wait_ge(q_st.h, b_qst + 16 * (tbs - 1))
                            for tt in range(4):
                                ins = v.scalar_tensor_tensor(
                                    ystg[tbs % 2][:][:,
                                                    tt * 2048 + dss * 512:
                                                    tt * 2048 +
                                                    (dss + 1) * 512],
                                    pd[tt][:], 1.0,
                                    xres[idx % 2][:][:,
                                                     tt * 512:(tt + 1) * 512],
                                    ALU.mult, ALU.add)
                            v_a.inc(ins, 1)
                            for a in range(4):
                                ysl = ystg[tbs % 2][:][:,
                                                      a * 2048 + dss * 512:
                                                      a * 2048 +
                                                      (dss + 1) * 512]
                                col = (4 * tbs + a) * 4 + dss
                                v.reduce_sum(sa_psum[:, col:col + 1], ysl,
                                             axis=AX.X)
                                v.tensor_tensor(sa_scr[:], ysl, ysl,
                                                ALU.mult)
                                v.reduce_sum(sa_psq[:, col:col + 1],
                                             sa_scr[:], axis=AX.X)

        def emit_gather():
            b_cc = g_ld.v
            with nc.Block() as block:
                @block.gpsimd
                def _(gp):
                    pairs = [
                        (wg_loc.rearrange("a p c n -> a (p c n)"),
                         wg_s.rearrange("a p c n -> a (p c n)")),
                        (wu_loc.rearrange("a p c n -> a (p c n)"),
                         wu_s.rearrange("a p c n -> a (p c n)")),
                        (wd_loc.rearrange("a h p n -> a (h p n)"),
                         wd_s.rearrange("a h p n -> a (h p n)")),
                    ]
                    for i, (src_ap, dst_ap) in enumerate(pairs):
                        ins = gp.collective_compute(
                            "AllGather", ALU.bypass,
                            replica_groups=[list(range(8))],
                            ins=[src_ap], outs=[dst_ap])
                        g_ld.inc(ins, 1)
                    gp.wait_ge(g_ld.h, b_cc + 3)

        emit_dequant()
        with ExitStack() as g:
            xT1 = g.enter_context(nc.sbuf_tensor("xT1", [128, 32768], BF16))
            emit_ln_T(x_in, xT1, "ffn")
            if debug:
                with nc.Block() as block:
                    @block.sync
                    def _(sync):
                        ins = sync.dma_start(out=xt_dbg, in_=xT1[:])
                        s_ld.inc(ins, 16)
                        sync.wait_ge(s_ld.h, s_ld.v)
            emit_ffn_a(xT1)
        emit_ffn_b(x_in, x1_s, final=False, stats=True)
        with ExitStack() as g:
            xTa = g.enter_context(nc.sbuf_tensor("xTa", [128, 32768], BF16))
            emit_ln_T(x1_s, xTa, "nat", prestats=True)
            emit_attn_proj(xTa)
        emit_cumsum()
        emit_heads()
        emit_wo()
        with ExitStack() as g:
            xT2 = g.enter_context(nc.sbuf_tensor("xT2", [128, 32768], BF16))
            emit_ln_T(x2_s, xT2, "ffn", prestats=True)
            emit_ffn_a(xT2)
        emit_ffn_b(x2_s, y_out, final=True)

    return nc


def kernel(L0_attn_x, L0_attn_W_q, L0_attn_W_kv, L0_attn_W_o,
           L0_ffn_gate_W_packed, L0_ffn_gate_scales,
           L0_ffn_up_W_packed, L0_ffn_up_scales,
           L0_ffn_down_W_packed, L0_ffn_down_scales):
    x = np.ascontiguousarray(np.asarray(L0_attn_x, dtype=np.float32))

    def pk_T(a):
        a = np.ascontiguousarray(np.asarray(a, dtype=np.int32))
        return np.ascontiguousarray(a.view(np.uint8)[:, ::4].T)

    def wT(a):
        return np.ascontiguousarray(np.asarray(a, dtype=np.float32).T)

    common = {
        "wqT": wT(L0_attn_W_q),
        "wkvT": wT(L0_attn_W_kv),
        "woT": wT(L0_attn_W_o),
    }
    gpkT = pk_T(L0_ffn_gate_W_packed)
    upkT = pk_T(L0_ffn_up_W_packed)
    dpkT = pk_T(L0_ffn_down_W_packed)
    gscT = wT(L0_ffn_gate_scales)
    uscT = wT(L0_ffn_up_scales)
    dscT = wT(L0_ffn_down_scales)
    nc = build_nc()
    in_maps = [dict(
        common, x=np.ascontiguousarray(x[b]),
        gpk=np.ascontiguousarray(gpkT[:, b * 2048:(b + 1) * 2048]),
        upk=np.ascontiguousarray(upkT[:, b * 2048:(b + 1) * 2048]),
        dpk=np.ascontiguousarray(dpkT[b * 1024:(b + 1) * 1024, :]),
        gsc=np.ascontiguousarray(gscT[:, b * 2048:(b + 1) * 2048]),
        usc=np.ascontiguousarray(uscT[:, b * 2048:(b + 1) * 2048]),
        dsc=np.ascontiguousarray(dscT[b * 64:(b + 1) * 64, :]),
    ) for b in range(8)]
    res = run_bass_kernel_spmd(nc, in_maps, core_ids=list(range(8)))
    return np.stack([np.asarray(res.results[b]["y"]) for b in range(8)],
                    axis=0)


# version 60 (final)
# speedup vs baseline: 1.0047x; 1.0001x over previous
import math
import numpy as np
import ml_dtypes
from contextlib import ExitStack

import concourse.bass as bass
import concourse.mybir as mybir
from concourse.bass_utils import run_bass_kernel_spmd

F32 = mybir.dt.float32
BF16 = mybir.dt.bfloat16
I32 = mybir.dt.int32
U8 = mybir.dt.uint8
AF = mybir.ActivationFunctionType
ALU = mybir.AluOpType
AX = mybir.AxisListType

S = 2048
D = 2048
M = 16384
KBITS = 0x3F000000
C0 = 0.7978845608028654
SQC = math.sqrt(C0 * 0.044715)


class Sem:
    def __init__(self, nc, name):
        self.h = nc.alloc_semaphore(name)
        self.v = 0

    def inc(self, instr, n=1):
        instr.then_inc(self.h, n)
        self.v += n


def _consts():
    i = np.arange(128)
    ang = 2.0 * np.pi * (np.outer(i, i) % 128) / 128.0
    bf = ml_dtypes.bfloat16
    return {
        "ident": np.eye(128, dtype=np.float32).astype(bf),
        "cf": np.cos(ang).astype(np.float32).astype(bf),
        "sf": (-np.sin(ang)).astype(np.float32).astype(bf),
        "ci": (np.cos(ang) / 128.0).astype(np.float32).astype(bf),
        "si": (-np.sin(ang) / 128.0).astype(np.float32).astype(bf),
        "tri": np.triu(np.ones((128, 128), np.float32)).astype(bf),
        "ones": np.ones((128, 128), np.float32).astype(bf),
    }


def build_nc(debug=False, kv_raw=False):
    nc = bass.Bass("TRN2", target_bir_lowering=False)
    uid_ctr = [0]

    x_in = nc.dram_tensor("x", [S, D], F32, kind="ExternalInput").ap()
    wqT = nc.dram_tensor("wqT", [4, 128, 16, 512], F32,
                         kind="ExternalInput").ap()
    wkvT = nc.dram_tensor("wkvT", [4, 128, 16, 512], F32,
                          kind="ExternalInput").ap()
    woT = nc.dram_tensor("woT", [D, D], F32, kind="ExternalInput").ap()
    gpk = nc.dram_tensor("gpk", [1024, 2048], U8, kind="ExternalInput").ap()
    upk = nc.dram_tensor("upk", [1024, 2048], U8, kind="ExternalInput").ap()
    dpk = nc.dram_tensor("dpk", [1024, 2048], U8, kind="ExternalInput").ap()
    gsc = nc.dram_tensor("gsc", [64, 2048], F32, kind="ExternalInput").ap()
    usc = nc.dram_tensor("usc", [64, 2048], F32, kind="ExternalInput").ap()
    dsc = nc.dram_tensor("dsc", [64, 2048], F32, kind="ExternalInput").ap()
    y_out = nc.dram_tensor("y", [S, D], F32, kind="ExternalOutput").ap()

    def scratch(name, shape, dtype):
        kind = "ExternalOutput" if debug else "Internal"
        return nc.dram_tensor(name, shape, dtype, kind=kind).ap()

    wg_s = nc.dram_tensor("wg_s", [64, 128, 16, 256], BF16, kind="Internal",
                          addr_space="Shared").ap()
    wu_s = nc.dram_tensor("wu_s", [64, 128, 16, 256], BF16, kind="Internal",
                          addr_space="Shared").ap()
    wd_s = nc.dram_tensor("wd_s", [32, 128, 16, 512], BF16, kind="Internal",
                          addr_space="Shared").ap()
    wg_loc = nc.dram_tensor("wg_loc", [8, 128, 16, 256], BF16, kind="Internal").ap()
    wu_loc = nc.dram_tensor("wu_loc", [8, 128, 16, 256], BF16, kind="Internal").ap()
    wd_loc = nc.dram_tensor("wd_loc", [4, 128, 16, 512], BF16, kind="Internal").ap()
    wsac = nc.dram_tensor("wsac", [8, 128, 4, 256], BF16, kind="Internal").ap()
    inter_s = scratch("inter_s", [4, 8, 128, 16, 512], BF16)
    x1_s = scratch("x1_s", [S, D], F32)
    x2_s = scratch("x2_s", [S, D], F32)
    qnT_s = scratch("qnT_s", [D, S], BF16)
    xt_dbg = scratch("xt_dbg", [128, 32768], BF16) if debug else None
    kvn_s = scratch("kvn_s", [4, 128, 16, 512], BF16)
    q_raw_s = nc.dram_tensor("q_raw_s", [4, 16, 128, 512], BF16,
                             kind="Internal").ap()
    kv_raw_s = nc.dram_tensor("kv_raw_s", [4, 16, 128, 512], BF16,
                              kind="Internal").ap()
    kvcT_s = scratch("kvcT_s", [D, S], BF16)
    vals_s = scratch("vals_s", [4, 128, 16, 512], BF16)

    cns = _consts()
    c_aps = {k: nc.inline_tensor(v, f"c_{k}").ap() for k, v in cns.items()}

    with ExitStack() as top:
        sbt = lambda n, s, d: top.enter_context(nc.sbuf_tensor(n, s, d))
        kc = {k: sbt(f"k_{k}", [128, 128], BF16) for k in cns}
        k_eps = sbt("k_eps", [128, 1], F32)
        sa_psum = sbt("sa_psum", [128, 64], F32)
        sa_psq = sbt("sa_psq", [128, 64], F32)
        sa_scr = sbt("sa_scr", [128, 512], F32)
        nc.const_aps.register(F32, 1e-5, k_eps[:, 0:1])
        p = [top.enter_context(nc.psum_tensor(f"pb{i}", [128, 512], F32))
             for i in range(8)]

        s_ld = Sem(nc, "s_ld")
        s2_st = Sem(nc, "s2_st")
        g_ld = Sem(nc, "g_ld")
        q_ld = Sem(nc, "q_ld")
        q_st = Sem(nc, "q_st")
        v_a = Sem(nc, "v_a")
        v_b = Sem(nc, "v_b")
        s_a = Sem(nc, "s_a")
        mm = Sem(nc, "mm")
        t_tp = Sem(nc, "t_tp")
        st_g = Sem(nc, "st_g")
        st_u = Sem(nc, "st_u")
        i_ld = Sem(nc, "i_ld")

        all_sems = [s_ld, s2_st, g_ld, q_ld, q_st, v_a, v_b, s_a, mm, t_tp,
                    st_g, st_u, i_ld]
        with nc.Block() as block:
            @block.gpsimd
            def _(gp):
                for sm in all_sems:
                    gp.sem_clear(sm.h)

        with nc.Block() as block:
            @block.sync
            def _(sync):
                for k in cns:
                    ins = sync.dma_start(out=kc[k][:], in_=c_aps[k])
                    s_ld.inc(ins, 16)

            @block.vector
            def _(v):
                v.memset(k_eps[:], 1e-5)
                v.wait_ge(s_ld.h, s_ld.v)

        def emit_dequant():
            with ExitStack() as es:
                uid_ctr[0] += 1
                _u = uid_ctr[0]
                sb = lambda n, s, d: es.enter_context(
                    nc.sbuf_tensor(f"{n}_{_u}", s, d))
                wp = [sb(f"dq_wp{i}", [128, 2048], I32) for i in range(2)]
                sexp = [sb(f"dq_sc{i}", [128, 2048], F32) for i in range(2)]
                mag = sb("dq_mag", [128, 4096], F32)
                corr = sb("dq_corr", [128, 4096], F32)
                sbits = sb("dq_sb", [128, 4096], I32)
                ssgn = sb("dq_ssgn", [128, 4096], F32)
                val = [sb(f"dq_val{i}", [128, 4096], BF16) for i in range(2)]
                tv = sb("dq_tv", [1, 8], F32)
                tsc = sb("dq_ts", [1, 8], F32)

                jobs = []
                for jc in (0, 1):
                    src = gpk[jc * 128:(jc + 1) * 128, :]
                    scs = gsc[jc * 8:(jc + 1) * 8, :]
                    dlo = wsac[0:8, :, 2 * jc, :].transpose([1, 0, 2])
                    dhi = wsac[0:8, :, 2 * jc + 1, :].transpose([1, 0, 2])
                    jobs.append((src, scs, dlo, dhi))
                for w_pk, w_sc, w_dst in ((gpk, gsc, wg_loc),
                                          (upk, usc, wu_loc)):
                    for jc in range(8):
                        src = w_pk[jc * 128:(jc + 1) * 128, :]
                        scs = w_sc[jc * 8:(jc + 1) * 8, :]
                        dlo = w_dst[0:8, :, 2 * jc, :].transpose([1, 0, 2])
                        dhi = w_dst[0:8, :, 2 * jc + 1, :].transpose([1, 0, 2])
                        jobs.append((src, scs, dlo, dhi))
                for jl in range(8):
                    src = dpk[jl * 128:(jl + 1) * 128, :]
                    scs = dsc[jl * 8:(jl + 1) * 8, :]
                    dlo = wd_loc[:, :, 2 * jl, :].transpose([1, 0, 2])
                    dhi = wd_loc[:, :, 2 * jl + 1, :].transpose([1, 0, 2])
                    jobs.append((src, scs, dlo, dhi))
                NJ = len(jobs)
                b_g, b_s, b_va, b_vb, b_sa, b_qst = (g_ld.v, s_ld.v, v_a.v,
                                                     v_b.v, s_a.v, q_st.v)
                b_stg, b_stu = st_g.v, st_u.v

                def st_sem(j):
                    if 2 <= j <= 9:
                        return st_g
                    if 10 <= j <= 17:
                        return st_u
                    return q_st

                def st_waits(eng, j):
                    nq = 32 * (min(j + 1, 2) + max(0, j - 17))
                    ng = 32 * max(0, min(j, 9) - 1)
                    nu = 32 * max(0, min(j, 17) - 9)
                    if nq:
                        eng.wait_ge(q_st.h, b_qst + nq)
                    if ng:
                        eng.wait_ge(st_g.h, b_stg + ng)
                    if nu:
                        eng.wait_ge(st_u.h, b_stu + nu)

                with nc.Block() as block:
                    @block.gpsimd
                    def _(gp):
                        gathers = {
                            11: (10, wg_loc.rearrange("a p c n -> a (p c n)"),
                                 wg_s.rearrange("a p c n -> a (p c n)")),
                            21: (18, wu_loc.rearrange("a p c n -> a (p c n)"),
                                 wu_s.rearrange("a p c n -> a (p c n)")),
                            25: (26, wd_loc.rearrange("h p c n -> h (p c n)"),
                                 wd_s.rearrange("a p c n -> a (p c n)")),
                        }
                        for i in range(NJ):
                            if i >= 2:
                                gp.wait_ge(v_b.h, b_vb + i - 1)
                            ins = gp.dma_start(out=wp[i % 2][:], in_=jobs[i][0])
                            g_ld.inc(ins, 16)
                        for i in (11, 21, 25):
                            nj, src_ap, dst_ap = gathers[i]
                            sm, tgt = {
                                11: (st_g, b_stg + 256),
                                21: (st_u, b_stu + 256),
                                25: (q_st, b_qst + 320)}[i]
                            gp.wait_ge(sm.h, tgt)
                            ins = gp.collective_compute(
                                "AllGather", ALU.bypass,
                                replica_groups=[list(range(8))],
                                ins=[src_ap], outs=[dst_ap])
                            g_ld.inc(ins, 1)
                        gp.wait_ge(g_ld.h, g_ld.v)

                    @block.sync
                    def _(sync):
                        for i in range(NJ):
                            if i >= 2:
                                sync.wait_ge(v_b.h, b_vb + i - 1)
                            scs = jobs[i][1]
                            bcast = scs.unsqueeze(1).broadcast_to(
                                [8, 16, scs.shape[1]])
                            ins = sync.dma_start(out=sexp[i % 2][:], in_=bcast)
                            s_ld.inc(ins, 16)

                    @block.vector
                    def _(v):
                        magi = mag[:].bitcast(I32)
                        ssgni = ssgn[:].bitcast(I32)
                        for i in range(NJ):
                            w = wp[i % 2][:]
                            sci = sexp[i % 2][:].bitcast(I32)
                            v.wait_ge(g_ld.h, b_g + 16 * (i + 1))
                            v.wait_ge(s_ld.h, b_s + 16 * (i + 1))
                            v.tensor_scalar(magi[:, 0:2048], w, 7, 22,
                                            ALU.bitwise_and,
                                            ALU.logical_shift_left)
                            v.tensor_scalar(magi[:, 2048:4096], w, 0x70, 18,
                                            ALU.bitwise_and,
                                            ALU.logical_shift_left)
                            v.tensor_scalar(magi[:, 0:2048], magi[:, 0:2048],
                                            KBITS, None, ALU.add)
                            v.tensor_scalar(magi[:, 2048:4096],
                                            magi[:, 2048:4096],
                                            KBITS, None, ALU.add)
                            ins = v.tensor_copy(tv[0:1, 0:2], mag[0:1, 0:2])
                            v_a.inc(ins, 1)
                            v.wait_ge(s_a.h, b_sa + i + 1)
                            v.tensor_tensor(mag[:], mag[:], corr[:],
                                            ALU.subtract)
                            v.tensor_scalar(sbits[:, 0:2048], w, 8, 28,
                                            ALU.bitwise_and,
                                            ALU.logical_shift_left)
                            v.tensor_scalar(sbits[:, 2048:4096], w, 0x80, 24,
                                            ALU.bitwise_and,
                                            ALU.logical_shift_left)
                            v.tensor_tensor(ssgni[:, 0:2048], sci,
                                            sbits[:, 0:2048], ALU.bitwise_or)
                            v.tensor_tensor(ssgni[:, 2048:4096], sci,
                                            sbits[:, 2048:4096], ALU.bitwise_or)
                            if i >= 2:
                                st_waits(v, i - 2)
                            v.tensor_tensor(val[i % 2][:], mag[:],
                                            ssgn[:], ALU.mult)
                            ins = v.tensor_copy(tv[0:1, 2:4], mag[0:1, 0:2])
                            v_b.inc(ins, 1)

                    @block.scalar
                    def _(s):
                        for i in range(NJ):
                            s.wait_ge(v_a.h, b_va + i + 1)
                            s.activation(corr[:], mag[:], AF.Relu,
                                         bias=1.0, scale=-1.0)
                            ins = s.copy(tsc[0:1, 0:2], corr[0:1, 0:2])
                            s_a.inc(ins, 1)
                            s.wait_ge(v_b.h, b_vb + i + 1)
                            _, _, dlo, dhi = jobs[i]
                            ncol = dlo.shape[2]
                            vv = val[i % 2][:]
                            lo = vv[:, 0:2048].rearrange("p (a n) -> p a n",
                                                         n=ncol)
                            hi = vv[:, 2048:4096].rearrange("p (a n) -> p a n",
                                                            n=ncol)
                            ins = s.dma_start(out=dlo, in_=lo)
                            st_sem(i).inc(ins, 16)
                            ins = s.dma_start(out=dhi, in_=hi)
                            st_sem(i).inc(ins, 16)
                        st_waits(s, NJ - 1)

        def emit_ln_T(src, xT, perm, prestats=False):
            with ExitStack() as es:
                uid_ctr[0] += 1
                _u = uid_ctr[0]
                sb = lambda n, s, d: es.enter_context(
                    nc.sbuf_tensor(f"{n}_{_u}", s, d))
                xr = [sb(f"ln_x{i}", [128, 2048], F32) for i in range(2)]
                lnr = [sb(f"ln_o{i}", [128, 2048], BF16) for i in range(2)]
                sums = sb("ln_sum", [128, 16], F32)
                sqs = sb("ln_sqs", [128, 16], F32)
                sq_scr = sb("ln_scr", [128, 2048], F32)
                nm = sb("ln_nm", [128, 16], F32)
                m2 = sb("ln_m2", [128, 16], F32)
                var = sb("ln_var", [128, 16], F32)
                sd = sb("ln_sd", [128, 16], F32)
                rstd = sb("ln_rstd", [128, 16], F32)
                xc = sb("ln_xc", [128, 2048], F32)
                tv = sb("ln_tv", [1, 8], F32)
                tsc = sb("ln_ts", [1, 8], F32)

                def cslice(ln_ap, dc):
                    if perm == "ffn":
                        jc, par = dc >> 1, dc & 1
                        base = 256 * jc + par
                        return ln_ap[:, base:base + 255:2]
                    return ln_ap[:, dc * 128:(dc + 1) * 128]

                b_s, b_v, b_sa = s_ld.v, v_b.v, s_a.v
                if not prestats:
                    with nc.Block() as block:
                        @block.sync
                        def _(sync):
                            for t in range(16):
                                if t >= 2:
                                    sync.wait_ge(s_a.h, b_sa + (t - 2) + 1)
                                    sync.wait_ge(v_b.h, b_v + (t - 2) + 1)
                                ins = sync.dma_start(
                                    out=xr[t % 2][:],
                                    in_=src[t * 128:(t + 1) * 128, :])
                                s_ld.inc(ins, 16)

                        @block.scalar
                        def _(s):
                            for t in range(16):
                                s.wait_ge(s_ld.h, b_s + 16 * (t + 1))
                                s.activation(sq_scr[:], xr[t % 2][:],
                                             AF.Square,
                                             accum_out=sqs[:, t:t + 1])
                                ins = s.copy(tsc[0:1, 0:2], sqs[0:1, 0:2])
                                s_a.inc(ins, 1)

                        @block.vector
                        def _(v):
                            for t in range(16):
                                v.wait_ge(s_ld.h, b_s + 16 * (t + 1))
                                ins = v.reduce_sum(sums[:, t:t + 1],
                                                   xr[t % 2][:],
                                                   axis=AX.X)
                                v_b.inc(ins, 1)

                if prestats:
                    with nc.Block() as block:
                        @block.vector
                        def _(v):
                            v.tensor_tensor(nm[:], sa_psum[:, 0::4],
                                            sa_psum[:, 1::4], ALU.add)
                            v.tensor_tensor(m2[:], sa_psum[:, 2::4],
                                            sa_psum[:, 3::4], ALU.add)
                            v.tensor_tensor(var[:], sa_psq[:, 0::4],
                                            sa_psq[:, 1::4], ALU.add)
                            v.tensor_tensor(sd[:], sa_psq[:, 2::4],
                                            sa_psq[:, 3::4], ALU.add)
                            for _pad in range(8):
                                v.tensor_copy(lnr[0][:][:, 0:512],
                                              xc[:][:, 0:512])
                            v.tensor_tensor(sums[:], nm[:], m2[:], ALU.add)
                            v.tensor_tensor(sqs[:], var[:], sd[:], ALU.add)
                b_v, b_sa = v_b.v, s_a.v
                with nc.Block() as block:
                    @block.vector
                    def _(v):
                        ins = v.tensor_scalar(nm[:], sums[:], -1.0 / D, None,
                                              ALU.mult)
                        v_b.inc(ins, 1)
                        v.wait_ge(s_a.h, b_sa + 1)
                        ins = v.scalar_tensor_tensor(
                            var[:], sqs[:], 1.0 / D, m2[:],
                            ALU.mult, ALU.subtract)
                        v_b.inc(ins, 1)
                        v.wait_ge(s_a.h, b_sa + 2)
                        ins = v.reciprocal(rstd[:], sd[:])
                        v_b.inc(ins, 1)

                    @block.scalar
                    def _(s):
                        s.wait_ge(v_b.h, b_v + 1)
                        ins = s.activation(m2[:], nm[:], AF.Square)
                        s_a.inc(ins, 1)
                        s.wait_ge(v_b.h, b_v + 2)
                        ins = s.activation(sd[:], var[:], AF.Sqrt, bias=1e-5)
                        s_a.inc(ins, 1)

                b_s, b_v, b_va, b_tp = s_ld.v, v_b.v, v_a.v, t_tp.v
                with nc.Block() as block:
                    @block.sync
                    def _(sync):
                        for t in range(16):
                            if t >= 2:
                                sync.wait_ge(v_b.h, b_v + (t - 2) + 1)
                            ins = sync.dma_start(
                                out=xr[t % 2][:],
                                in_=src[t * 128:(t + 1) * 128, :])
                            s_ld.inc(ins, 16)

                    @block.vector
                    def _(v):
                        xTv = xT[:].rearrange("p (c n) -> p c n", n=2048)
                        p6b = p[6][:].bitcast(BF16)
                        p7b = p[7][:].bitcast(BF16)
                        for t in range(16):
                            c = t % 2
                            v.wait_ge(s_ld.h, b_s + 16 * (t + 1))
                            v.tensor_tensor(
                                xc[:], xr[c][:],
                                nm[:, t:t + 1].broadcast_to([128, 2048]),
                                ALU.add)
                            if t >= 2:
                                v.wait_ge(t_tp.h, b_tp + 2 * (t - 2) + 2)
                            v.tensor_tensor(
                                lnr[c][:], xc[:],
                                rstd[:, t:t + 1].broadcast_to([128, 2048]),
                                ALU.mult)
                            ins = v.tensor_copy(tv[0:1, 4:6], tv[0:1, 0:2])
                            v_b.inc(ins, 1)
                            v.wait_ge(t_tp.h, b_tp + 2 * t + 1)
                            v.tensor_copy(
                                xTv[:, 0:8, t * 128:(t + 1) * 128],
                                p6b.rearrange("p (c n) -> p c n", n=128))
                            v.wait_ge(t_tp.h, b_tp + 2 * t + 2)
                            ins = v.tensor_copy(
                                xTv[:, 8:16, t * 128:(t + 1) * 128],
                                p7b.rearrange("p (c n) -> p c n", n=128))
                            v_a.inc(ins, 1)

                    @block.tensor
                    def _(te):
                        p6b = p[6][:].bitcast(BF16)
                        p7b = p[7][:].bitcast(BF16)
                        for t in range(16):
                            c = t % 2
                            te.wait_ge(v_b.h, b_v + t + 1)
                            if t >= 1:
                                te.wait_ge(v_a.h, b_va + t)
                            for dc in range(8):
                                ins = te.transpose(
                                    p6b[:, dc * 128:(dc + 1) * 128],
                                    cslice(lnr[c][:], dc), kc["ident"][:])
                            t_tp.inc(ins, 1)
                            for dc in range(8, 16):
                                ins = te.transpose(
                                    p7b[:, (dc - 8) * 128:(dc - 7) * 128],
                                    cslice(lnr[c][:], dc), kc["ident"][:])
                            t_tp.inc(ins, 1)

        def emit_ffn_a(xT):
            with ExitStack() as es:
                uid_ctr[0] += 1
                _u = uid_ctr[0]
                sb = lambda n, s, d: es.enter_context(
                    nc.sbuf_tensor(f"{n}_{_u}", s, d))
                wgb = [sb(f"fa_wg{i}", [128, 4096], BF16) for i in range(2)]
                wub = [sb(f"fa_wu{i}", [128, 4096], BF16) for i in range(2)]
                g2c = [sb(f"fa_g2c{i}", [128, 512], F32) for i in range(2)]
                arg = [sb(f"fa_arg{i}", [128, 512], F32) for i in range(2)]
                tnh = [sb(f"fa_tnh{i}", [128, 512], F32) for i in range(2)]
                gl = [[sb(f"fa_gl{s_}_{i}", [128, 512], F32)
                       for i in range(4)] for s_ in range(2)]
                stg = [sb(f"fa_st{i}", [128, 512], BF16) for i in range(8)]
                tv = sb("fa_tv", [1, 8], F32)
                tsc = sb("fa_ts", [1, 8], F32)
                b_s, b_mm, b_v, b_sa, b_qst = (s_ld.v, mm.v, v_a.v, s_a.v,
                                               q_st.v)

                def slot(g2, tq):
                    return (4 * g2 + tq) % 8

                with nc.Block() as block:
                    @block.sync
                    def _(sync):
                        for b in range(64):
                            if b >= 2:
                                sync.wait_ge(mm.h, b_mm + 16 * (b - 2) + 8)
                            ins = sync.dma_start(
                                out=wgb[b % 2][:].rearrange(
                                    "p (c n) -> p c n", n=256),
                                in_=wg_s[b])
                            s_ld.inc(ins, 16)
                            if b >= 2:
                                sync.wait_ge(mm.h, b_mm + 16 * (b - 2) + 16)
                            ins = sync.dma_start(
                                out=wub[b % 2][:].rearrange(
                                    "p (c n) -> p c n", n=256),
                                in_=wu_s[b])
                            s_ld.inc(ins, 16)

                    @block.tensor
                    def _(te):
                        for b in range(64):
                            for ph in range(4):
                                sub = ph % 2
                                off = 4 * sub
                                wblk = wgb[b % 2] if ph < 2 else wub[b % 2]
                                if ph == 0:
                                    te.wait_ge(s_ld.h, b_s + 32 * (b + 1))
                                    if b >= 1:
                                        te.wait_ge(v_a.h, b_v + 24 * b)
                                elif ph == 1:
                                    if b >= 1:
                                        te.wait_ge(v_a.h, b_v + 24 * b + 4)
                                elif ph == 2:
                                    te.wait_ge(v_a.h, b_v + 24 * b + 12)
                                    te.wait_ge(s_a.h, b_sa + 16 * b + 8)
                                else:
                                    te.wait_ge(v_a.h, b_v + 24 * b + 20)
                                    te.wait_ge(s_a.h, b_sa + 16 * b + 16)
                                for dc in range(16):
                                    for tq in range(4):
                                        ins = te.matmul(
                                            p[off + tq][:],
                                            wblk[:][:, dc * 256 + sub:
                                                    dc * 256 + 256:2],
                                            xT[:][:, dc * 2048 + tq * 512:
                                                  dc * 2048 + (tq + 1) * 512],
                                            start=(dc == 0), stop=(dc == 15))
                                        if dc == 15:
                                            mm.inc(ins, 1)

                    @block.scalar
                    def _(s):
                        def gl_scalar(b, sub):
                            for tq in range(2):
                                pass
                            for tq in range(4):
                                s.wait_ge(mm.h,
                                          b_mm + 16 * b + 4 * sub + tq + 1)
                                s.activation(g2c[tq % 2][:],
                                             p[4 * sub + tq][:], AF.Square,
                                             scale=SQC)
                                ins = s.copy(tsc[0:1, 0:2],
                                             g2c[tq % 2][0:1, 0:2])
                                s_a.inc(ins, 1)
                                s.wait_ge(v_a.h, b_v + 24 * b + 4 + 8 * sub +
                                          2 * tq + 1)
                                s.activation(tnh[tq % 2][:], arg[tq % 2][:],
                                             AF.Tanh)
                                ins = s.copy(tsc[0:1, 2:4],
                                             tnh[tq % 2][0:1, 0:2])
                                s_a.inc(ins, 1)

                        def st_inter(b, sub, vbase):
                            cc = 2 * b + sub
                            for tq in range(4):
                                s.wait_ge(v_a.h, vbase + tq + 1)
                                ins = s.dma_start(
                                    out=inter_s[tq, cc // 16, :, cc % 16, :],
                                    in_=stg[slot(cc, tq)][:])
                                q_st.inc(ins, 16)

                        for b in range(64):
                            if b >= 1:
                                st_inter(b - 1, 1, b_v + 24 * b)
                            gl_scalar(b, 0)
                            gl_scalar(b, 1)
                            st_inter(b, 0, b_v + 24 * b + 20)
                        st_inter(63, 1, b_v + 24 * 64)
                        s.wait_ge(q_st.h, b_qst + 16 * 512)

                    @block.vector
                    def _(v):
                        def inter_ops(b, sub, mmbase):
                            cc = 2 * b + sub
                            for tq in range(4):
                                v.wait_ge(mm.h, mmbase + tq + 1)
                                g2p = cc - 2
                                if g2p >= 0:
                                    v.wait_ge(q_st.h, b_qst +
                                              16 * (4 * g2p + tq + 1))
                                v.tensor_tensor(stg[slot(cc, tq)][:],
                                                gl[sub][tq][:],
                                                p[4 * sub + tq][:], ALU.mult)
                                ins = v.tensor_copy(
                                    tv[0:1, 0:2],
                                    stg[slot(cc, tq)][0:1, 0:2])
                                v_a.inc(ins, 1)

                        def gl_vector(b, sub):
                            for tq in range(4):
                                v.wait_ge(s_a.h, b_sa + 16 * b + 8 * sub +
                                          2 * tq + 1)
                                v.scalar_tensor_tensor(
                                    arg[tq % 2][:], g2c[tq % 2][:], C0,
                                    p[4 * sub + tq][:], ALU.add, ALU.mult)
                                ins = v.tensor_copy(tv[0:1, 2:4],
                                                    arg[tq % 2][0:1, 0:2])
                                v_a.inc(ins, 1)
                                v.wait_ge(s_a.h, b_sa + 16 * b + 8 * sub +
                                          2 * tq + 2)
                                v.scalar_tensor_tensor(
                                    gl[sub][tq][:], tnh[tq % 2][:], 1.0,
                                    p[4 * sub + tq][:], ALU.add, ALU.mult)
                                ins = v.tensor_copy(tv[0:1, 4:6],
                                                    gl[sub][tq][0:1, 0:2])
                                v_a.inc(ins, 1)

                        for b in range(64):
                            if b >= 1:
                                inter_ops(b - 1, 1,
                                          b_mm + 16 * (b - 1) + 12)
                            else:
                                for _d in range(4):
                                    ins = v.tensor_copy(tv[0:1, 6:8],
                                                        tv[0:1, 0:2])
                                    v_a.inc(ins, 1)
                            gl_vector(b, 0)
                            gl_vector(b, 1)
                            inter_ops(b, 0, b_mm + 16 * b + 8)
                        inter_ops(63, 1, b_mm + 16 * 63 + 12)

        def emit_ffn_b(x_src, dst, final=False, stats=False):
            with ExitStack() as es:
                uid_ctr[0] += 1
                _u = uid_ctr[0]
                sb = lambda n, s, d: es.enter_context(
                    nc.sbuf_tensor(f"{n}_{_u}", s, d))
                inter_a = sb("fb_inta", [128, 32768], BF16)
                inter_b = sb("fb_intb", [128, 32768], BF16)
                wdr = [sb(f"fb_wd{i}", [128, 8192], BF16) for i in range(2)]
                xres = [sb(f"fb_xr{i}", [128, 2048], F32) for i in range(2)]
                ystg = [sb(f"fb_ys{i}", [128, 2048], F32) for i in range(2)]
                tv = sb("fb_tv", [1, 8], F32)
                b_s, b_mm, b_va, b_qld, b_qst = (s_ld.v, mm.v, v_a.v, q_ld.v,
                                                 q_st.v)

                def xtile(t_ap, tb, hs):
                    return t_ap[tb * 512:(tb + 1) * 512,
                                hs * 512:(hs + 1) * 512].rearrange(
                                    "(a p) n -> p a n", p=128)

                ld_plan = []
                ipos = {}
                wpos = {}
                _np = [0, 0]

                def plan_eighth(tb, e):
                    gate = None
                    if tb >= 1:
                        gate = 8 * (4 * (tb - 1) + 3) + e + 1
                    ld_plan.append(("i", tb, e, gate))
                    _np[0] += 1
                    ipos[(tb, e)] = _np[0]

                def plan_wd(tb, hs, g):
                    gs = (tb * 4 + hs) * 8 + g
                    gate = gs - 1 if gs >= 2 else None
                    ld_plan.append(("w", tb, hs, g, gate))
                    _np[1] += 1
                    wpos[(tb, hs, g)] = _np[1]

                for k in range(8):
                    plan_eighth(0, k)
                    plan_wd(0, 0, k)
                for tb in range(4):
                    for hs in range(3):
                        if (tb, hs) != (0, 0):
                            for g in range(8):
                                plan_wd(tb, hs, g)
                    if tb < 3:
                        plan_wd(tb, 3, 0)
                        plan_wd(tb, 3, 1)
                        for k in range(6):
                            plan_eighth(tb + 1, k)
                            plan_wd(tb, 3, k + 2)
                        plan_eighth(tb + 1, 6)
                        plan_eighth(tb + 1, 7)
                    else:
                        for g in range(8):
                            plan_wd(tb, 3, g)

                b_il = i_ld.v
                with nc.Block() as block:
                    @block.gpsimd
                    def _(gp):
                        for item in ld_plan:
                            if item[0] != "i":
                                continue
                            _, tb, e, gate = item
                            if gate is not None:
                                gp.wait_ge(mm.h, b_mm + gate)
                            half = inter_a if e < 4 else inter_b
                            ee = e % 4
                            ins = gp.dma_start(
                                out=half[:][:, ee * 8192:(ee + 1) * 8192]
                                .rearrange("p (c n) -> p c n", n=512),
                                in_=inter_s[tb, e])
                            i_ld.inc(ins, 16)

                    @block.sync
                    def _(sync):
                        for item in ld_plan:
                            if item[0] != "w":
                                continue
                            _, tb, hs, g, gate = item
                            if gate is not None:
                                sync.wait_ge(mm.h, b_mm + gate)
                            gs = (tb * 4 + hs) * 8 + g
                            ins = sync.dma_start(
                                out=wdr[gs % 2][:].rearrange(
                                    "p (c n) -> p c n", n=512),
                                in_=wd_s[4 * g + hs])
                            s_ld.inc(ins, 16)

                    @block.tensor
                    def _(te):
                        for tb in range(4):
                            for hs in range(4):
                                idx = 4 * tb + hs
                                if idx >= 2:
                                    te.wait_ge(v_a.h, b_va + idx - 1)
                                pd = [p[(idx % 2) * 4 + tt] for tt in range(4)]
                                for c in range(128):
                                    g = c // 16
                                    gs = (tb * 4 + hs) * 8 + g
                                    if c % 16 == 0:
                                        te.wait_ge(i_ld.h,
                                                   b_il + 16 * ipos[(tb, g)])
                                        te.wait_ge(s_ld.h,
                                                   b_s + 16 * wpos[(tb, hs,
                                                                    g)])
                                    ibuf = inter_a if c < 64 else inter_b
                                    cc = c % 64
                                    for tt in range(4):
                                        ins = te.matmul(
                                            pd[tt][:],
                                            ibuf[:][:,
                                                    cc * 512 + tt * 128:
                                                    cc * 512 + (tt + 1) * 128],
                                            wdr[gs % 2][:][:,
                                                           (c % 16) * 512:
                                                           (c % 16 + 1) * 512],
                                            start=(c == 0), stop=(c == 127))
                                    if c % 16 == 15:
                                        mm.inc(ins, 1)

                    @block.scalar
                    def _(s):
                        for idx in range(16):
                            tb, hs = divmod(idx, 4)
                            if idx == 0:
                                ins = s.dma_start(
                                    out=xres[0][:].rearrange(
                                        "p (a n) -> p a n", n=512),
                                    in_=xtile(x_src, 0, 0))
                                q_ld.inc(ins, 16)
                            if idx + 1 < 16:
                                tb2, hs2 = divmod(idx + 1, 4)
                                if idx >= 1:
                                    s.wait_ge(v_a.h, b_va + idx)
                                ins = s.dma_start(
                                    out=xres[(idx + 1) % 2][:].rearrange(
                                        "p (a n) -> p a n", n=512),
                                    in_=xtile(x_src, tb2, hs2))
                                q_ld.inc(ins, 16)
                            s.wait_ge(v_a.h, b_va + idx + 1)
                            ins = s.dma_start(
                                out=xtile(dst, tb, hs),
                                in_=ystg[idx % 2][:].rearrange(
                                    "p (a n) -> p a n", n=512))
                            q_st.inc(ins, 16)
                        s.wait_ge(q_st.h, b_qst + 16 * 16)

                    @block.vector
                    def _(v):
                        for idx in range(16):
                            pd = [p[(idx % 2) * 4 + tt] for tt in range(4)]
                            v.wait_ge(mm.h, b_mm + 8 * (idx + 1))
                            v.wait_ge(q_ld.h, b_qld + 16 * (idx + 1))
                            if idx >= 2:
                                v.wait_ge(q_st.h, b_qst + 16 * (idx - 1))
                            for tt in range(4):
                                v.scalar_tensor_tensor(
                                    ystg[idx % 2][:][:,
                                                     tt * 512:(tt + 1) * 512],
                                    pd[tt][:], 0.25,
                                    xres[idx % 2][:][:,
                                                     tt * 512:(tt + 1) * 512],
                                    ALU.mult, ALU.add)
                            ins = v.tensor_copy(tv[0:1, 0:2],
                                                ystg[idx % 2][0:1, 0:2])
                            v_a.inc(ins, 1)
                            if stats:
                                tbs, hss = divmod(idx, 4)
                                for a in range(4):
                                    ysl = ystg[idx % 2][:][:,
                                                          a * 512:
                                                          (a + 1) * 512]
                                    col = (4 * tbs + a) * 4 + hss
                                    v.reduce_sum(
                                        sa_psum[:, col:col + 1], ysl,
                                        axis=AX.X)
                                    v.tensor_tensor(sa_scr[:], ysl, ysl,
                                                    ALU.mult)
                                    v.reduce_sum(
                                        sa_psq[:, col:col + 1], sa_scr[:],
                                        axis=AX.X)

        def emit_attn_proj(xT):
            with ExitStack() as es:
                uid_ctr[0] += 1
                _u = uid_ctr[0]
                sb = lambda n, s, d: es.enter_context(
                    nc.sbuf_tensor(f"{n}_{_u}", s, d))
                wqs = [sb(f"ap_wq{i}", [128, 8192], BF16) for i in range(2)]
                wks = [sb(f"ap_wk{i}", [128, 8192], BF16) for i in range(2)]
                sqq = sb("ap_sqq", [128, 512], F32)
                sqk = sb("ap_sqk", [128, 512], F32)
                qss = sb("ap_qss", [128, 4], F32)
                kss = sb("ap_kss", [128, 4], F32)
                sdq = sb("ap_sdq", [128, 4], F32)
                sdk = sb("ap_sdk", [128, 4], F32)
                rq_all = sb("ap_rqa", [128, 256], F32)
                rk_all = sb("ap_rka", [128, 256], F32)
                qraw_st = [sb(f"ap_qr{i}", [128, 512], BF16) for i in range(2)]
                kvraw_st = [sb(f"ap_kr{i}", [128, 512], BF16) for i in range(2)]
                qn_st = [sb(f"ap_qn{i}", [128, 512], BF16)
                         for i in range(2)]
                kvn_st = [sb(f"ap_kv{i}", [128, 512], BF16) for i in range(2)]
                qnT_sp = [sb(f"ap_qT{i}", [128, 8192], BF16) for i in range(2)]
                tv = sb("ap_tv", [1, 8], F32)
                tsc = sb("ap_ts", [1, 8], F32)
                rqv = rq_all[:].rearrange("p (u h) -> p u h", h=4)
                rkv = rk_all[:].rearrange("p (u h) -> p u h", h=4)

                b_g, b_mm, b_sa, b_vb, b_qst = (g_ld.v, mm.v, s_a.v, v_b.v,
                                                q_st.v)
                with nc.Block() as block:
                    @block.gpsimd
                    def _(gp):
                        for sp in range(4):
                            if sp >= 2:
                                gp.wait_ge(mm.h, b_mm + 32 * (sp - 1))
                            ins = gp.dma_start(
                                out=wqs[sp % 2][:].rearrange(
                                    "p (c n) -> p c n", n=512),
                                in_=wqT[sp])
                            g_ld.inc(ins, 16)
                            ins = gp.dma_start(
                                out=wks[sp % 2][:].rearrange(
                                    "p (c n) -> p c n", n=512),
                                in_=wkvT[sp])
                            g_ld.inc(ins, 16)

                    @block.tensor
                    def _(te):
                        for u in range(64):
                            sp, t = divmod(u, 16)
                            if t == 0:
                                te.wait_ge(g_ld.h, b_g + 32 * (sp + 1))
                            if u >= 2:
                                te.wait_ge(v_b.h, b_vb + 3 * (u - 2) + 3)
                                te.wait_ge(s_a.h, b_sa + 4 * (u - 2) + 4)
                            pq = p[2 * (u % 2)]
                            pkv = p[2 * (u % 2) + 1]
                            for ps, wsp in ((pq, wqs[sp % 2]),
                                            (pkv, wks[sp % 2])):
                                for dc in range(16):
                                    ins = te.matmul(
                                        ps[:],
                                        xT[:][:, dc * 2048 + t * 128:
                                              dc * 2048 + (t + 1) * 128],
                                        wsp[:][:, dc * 512:(dc + 1) * 512],
                                        start=(dc == 0), stop=(dc == 15))
                                mm.inc(ins, 1)

                    @block.scalar
                    def _(s):
                        for u in range(64):
                            s.wait_ge(mm.h, b_mm + 2 * u + 1)
                            if u >= 1:
                                s.wait_ge(v_b.h, b_vb + 3 * (u - 1) + 1)
                            pq = p[2 * (u % 2)]
                            pkv = p[2 * (u % 2) + 1]
                            s.activation(sqq[:], pq[:], AF.Square)
                            ins = s.copy(tsc[0:1, 0:2], sqq[0:1, 0:2])
                            s_a.inc(ins, 1)
                            s.wait_ge(v_b.h, b_vb + 3 * u + 1)
                            s.activation(sdq[:], qss[:], AF.Sqrt, scale=128.0)
                            ins = s.copy(tsc[0:1, 2:4], sdq[0:1, 0:2])
                            s_a.inc(ins, 1)
                            s.wait_ge(mm.h, b_mm + 2 * u + 2)
                            s.activation(sqk[:], pkv[:], AF.Square)
                            ins = s.copy(tsc[0:1, 4:6], sqk[0:1, 0:2])
                            s_a.inc(ins, 1)
                            s.wait_ge(v_b.h, b_vb + 3 * u + 2)
                            s.activation(sdk[:], kss[:], AF.Sqrt, scale=128.0)
                            ins = s.copy(tsc[0:1, 6:8], sdk[0:1, 0:2])
                            s_a.inc(ins, 1)
                            sp, t = divmod(u, 16)
                            s.wait_ge(v_b.h, b_vb + 3 * u + 3)
                            ins = s.dma_start(
                                out=q_raw_s[sp, t],
                                in_=qraw_st[u % 2][:])
                            q_st.inc(ins, 16)
                            ins = s.dma_start(
                                out=kv_raw_s[sp, t],
                                in_=kvraw_st[u % 2][:])
                            q_st.inc(ins, 16)
                        s.wait_ge(q_st.h, b_qst + 32 * 64)

                    @block.vector
                    def _(v):
                        for u in range(64):
                            pq = p[2 * (u % 2)]
                            pkv = p[2 * (u % 2) + 1]
                            v.wait_ge(s_a.h, b_sa + 4 * u + 1)
                            v.reduce_sum(qss[:].unsqueeze(2),
                                         sqq[:].rearrange(
                                             "p (a b) -> p a b", b=128),
                                         axis=AX.X)
                            ins = v.tensor_copy(tv[0:1, 0:2], qss[0:1, 0:2])
                            v_b.inc(ins, 1)
                            v.wait_ge(s_a.h, b_sa + 4 * u + 2)
                            v.reciprocal(rqv[:, u, :], sdq[:])
                            if u >= 2:
                                v.wait_ge(q_st.h, b_qst + 32 * (u - 1))
                            v.tensor_copy(qraw_st[u % 2][:], pq[:])
                            v.wait_ge(s_a.h, b_sa + 4 * u + 3)
                            v.reduce_sum(kss[:].unsqueeze(2),
                                         sqk[:].rearrange(
                                             "p (a b) -> p a b", b=128),
                                         axis=AX.X)
                            ins = v.tensor_copy(tv[0:1, 2:4], kss[0:1, 0:2])
                            v_b.inc(ins, 1)
                            v.wait_ge(s_a.h, b_sa + 4 * u + 4)
                            v.reciprocal(rkv[:, u, :], sdk[:])
                            v.tensor_copy(kvraw_st[u % 2][:], pkv[:])
                            ins = v.tensor_copy(tv[0:1, 4:6], tv[0:1, 0:2])
                            v_b.inc(ins, 1)

                b_s, b_mm, b_vb, b_tp, b_qst, b_s2 = (
                    s_ld.v, mm.v, v_b.v, t_tp.v, q_st.v, s2_st.v)
                with nc.Block() as block:
                    @block.sync
                    def _(sync):
                        for u in range(64):
                            sp, t = divmod(u, 16)
                            if u >= 2:
                                sync.wait_ge(v_b.h, b_vb + 3 * (u - 2) + 3)
                            ins = sync.dma_start(
                                out=qraw_st[u % 2][:],
                                in_=q_raw_s[sp, t])
                            s_ld.inc(ins, 16)
                            ins = sync.dma_start(
                                out=kvraw_st[u % 2][:],
                                in_=kv_raw_s[sp, t])
                            s_ld.inc(ins, 16)

                    @block.gpsimd
                    def _(gp):
                        for sp in range(4):
                            gp.wait_ge(v_b.h,
                                       b_vb + 3 * (16 * sp + 16) + 1)
                            qTv = qnT_sp[sp % 2][:].rearrange(
                                "p (h n) -> p h n", n=2048)
                            for h in range(4):
                                ins = gp.dma_start(
                                    out=qnT_s[(4 * sp + h) * 128:
                                              (4 * sp + h + 1) * 128, :],
                                    in_=qTv[:, h, :])
                                s2_st.inc(ins, 16)
                        gp.wait_ge(s2_st.h, b_s2 + 16 * 16)

                    @block.vector
                    def _(v):
                        def qcopy(uc):
                            spc, tc = divmod(uc, 16)
                            if tc == 0 and spc >= 2:
                                v.wait_ge(s2_st.h, b_s2 + 64 * (spc - 1))
                            v.wait_ge(t_tp.h, b_tp + uc + 1)
                            pub = p[6 + uc % 2][:].bitcast(BF16)
                            qTv = qnT_sp[spc % 2][:].rearrange(
                                "p (h n) -> p h n", n=2048)
                            ins = v.tensor_copy(
                                qTv[:, :, tc * 128:(tc + 1) * 128],
                                pub[:, 0:512].rearrange(
                                    "p (h n) -> p h n", n=128))
                            v_b.inc(ins, 1)

                        for u in range(64):
                            sp, t = divmod(u, 16)
                            if u >= 1:
                                qcopy(u - 1)
                            else:
                                ins = v.tensor_copy(tv[0:1, 0:2],
                                                    tv[0:1, 2:4])
                                v_b.inc(ins, 1)
                            v.wait_ge(s_ld.h, b_s + 32 * (u + 1))
                            if u >= 1:
                                v.wait_ge(t_tp.h, b_tp + u - 1)
                            v.tensor_tensor(
                                qn_st[u % 2][:].rearrange(
                                    "p (a b) -> p a b", b=128),
                                qraw_st[u % 2][:].rearrange(
                                    "p (a b) -> p a b", b=128),
                                rqv[:, u, :].unsqueeze(2).broadcast_to(
                                    [128, 4, 128]),
                                ALU.mult)
                            ins = v.tensor_copy(tv[0:1, 0:2], tv[0:1, 2:4])
                            v_b.inc(ins, 1)
                            if u >= 2:
                                v.wait_ge(q_st.h, b_qst + 16 * (u - 1))
                            v.tensor_tensor(
                                kvn_st[u % 2][:].rearrange(
                                    "p (a b) -> p a b", b=128),
                                kvraw_st[u % 2][:].rearrange(
                                    "p (a b) -> p a b", b=128),
                                rkv[:, u, :].unsqueeze(2).broadcast_to(
                                    [128, 4, 128]),
                                ALU.mult)
                            ins = v.tensor_copy(tv[0:1, 2:4], tv[0:1, 4:6])
                            v_b.inc(ins, 1)
                        qcopy(63)

                    @block.tensor
                    def _(te):
                        for u in range(64):
                            pub = p[6 + u % 2][:].bitcast(BF16)
                            te.wait_ge(v_b.h, b_vb + 3 * u + 2)
                            for h in range(4):
                                ins = te.transpose(
                                    pub[:, h * 128:(h + 1) * 128],
                                    qn_st[u % 2][:][:, h * 128:(h + 1) * 128],
                                    kc["ident"][:])
                            t_tp.inc(ins, 1)

                    @block.scalar
                    def _(s):
                        for u in range(64):
                            sp, t = divmod(u, 16)
                            s.wait_ge(v_b.h, b_vb + 3 * u + 3)
                            ins = s.dma_start(
                                out=kvn_s[sp, :, t, :],
                                in_=kvn_st[u % 2][:])
                            q_st.inc(ins, 16)
                        s.wait_ge(q_st.h, b_qst + 16 * 64)

        def emit_cumsum():
            with ExitStack() as es:
                uid_ctr[0] += 1
                _u = uid_ctr[0]
                sb = lambda n, s, d: es.enter_context(
                    nc.sbuf_tensor(f"{n}_{_u}", s, d))
                kvsp = [sb(f"cs_kv{i}", [128, 8192], BF16) for i in range(2)]
                kvc_st = [sb(f"cs_st{i}", [128, 512], BF16)
                          for i in range(2)]
                kvcT_sp = [sb(f"cs_T{i}", [128, 8192], BF16) for i in range(2)]
                b_s, b_mm, b_va, b_vb, b_tp, b_s2 = (s_ld.v, mm.v, v_a.v,
                                                     v_b.v, t_tp.v, s2_st.v)

                with nc.Block() as block:
                    @block.sync
                    def _(sync):
                        for sp in range(4):
                            if sp >= 2:
                                sync.wait_ge(mm.h, b_mm + 16 * (sp - 1))
                            src = kvn_s[sp]
                            ins = sync.dma_start(
                                out=kvsp[sp % 2][:].rearrange(
                                    "p (a n) -> p a n", n=512), in_=src)
                            s_ld.inc(ins, 16)
                            sync.wait_ge(v_b.h, b_vb + 16 * (sp + 1))
                            Tv = kvcT_sp[sp % 2][:].rearrange(
                                "p (h n) -> p h n", n=2048)
                            for h in range(4):
                                ins = sync.dma_start(
                                    out=kvcT_s[(4 * sp + h) * 128:
                                               (4 * sp + h + 1) * 128, :],
                                    in_=Tv[:, h, :])
                                s2_st.inc(ins, 16)
                        sync.wait_ge(s2_st.h, b_s2 + 16 * 16)

                    @block.tensor
                    def _(te):
                        p6b = p[6][:].bitcast(BF16)
                        for u in range(64):
                            sp, tout = divmod(u, 16)
                            if tout == 0:
                                te.wait_ge(s_ld.h, b_s + 16 * (sp + 1))
                            pc = p[u % 2]
                            if u >= 2:
                                te.wait_ge(v_a.h, b_va + u - 1)
                            kvv = kvsp[sp % 2][:].rearrange(
                                "p (a n) -> p a n", n=512)
                            for tp in range(tout + 1):
                                ins = te.matmul(
                                    pc[:],
                                    kc["tri"][:] if tp == tout
                                    else kc["ones"][:],
                                    kvv[:, tp, :],
                                    start=(tp == 0), stop=(tp == tout))
                            mm.inc(ins, 1)
                            te.wait_ge(v_a.h, b_va + u + 1)
                            if u >= 2:
                                te.wait_ge(v_b.h, b_vb + u - 1)
                            pub = p[6 + u % 2][:].bitcast(BF16)
                            for h in range(4):
                                ins = te.transpose(
                                    pub[:, h * 128:(h + 1) * 128],
                                    kvc_st[u % 2][:][:, h * 128:(h + 1) * 128],
                                    kc["ident"][:])
                            t_tp.inc(ins, 1)

                    @block.vector
                    def _(v):
                        for u in range(64):
                            sp, tout = divmod(u, 16)
                            pc = p[u % 2]
                            v.wait_ge(mm.h, b_mm + u + 1)
                            ins = v.tensor_copy(kvc_st[u % 2][:], pc[:])
                            v_a.inc(ins, 1)
                            if tout == 0 and sp >= 2:
                                v.wait_ge(s2_st.h, b_s2 + 64 * (sp - 1))
                            v.wait_ge(t_tp.h, b_tp + u + 1)
                            pub = p[6 + u % 2][:].bitcast(BF16)
                            Tv = kvcT_sp[sp % 2][:].rearrange(
                                "p (h n) -> p h n", n=2048)
                            ins = v.tensor_copy(
                                Tv[:, :, tout * 128:(tout + 1) * 128],
                                pub[:, 0:512].rearrange(
                                    "p (h n) -> p h n", n=128))
                            v_b.inc(ins, 1)

        def emit_heads():
            with ExitStack() as es:
                uid_ctr[0] += 1
                _u = uid_ctr[0]
                sb = lambda n, s, d: es.enter_context(
                    nc.sbuf_tensor(f"{n}_{_u}", s, d))
                qh = [sb(f"hd_q{i}", [128, 2048], BF16) for i in range(2)]
                kvh = [sb(f"hd_k{i}", [128, 2048], BF16) for i in range(2)]
                qfr = sb("hd_qfr", [128, 2048], F32)
                qfi = sb("hd_qfi", [128, 2048], F32)
                kr = sb("hd_kr", [128, 2048], F32)
                ki = sb("hd_ki", [128, 2048], F32)
                t1 = sb("hd_t1", [128, 2048], F32)
                u1 = sb("hd_u1", [128, 2048], F32)
                qvr = sb("hd_qvr", [128, 2048], BF16)
                qvi = sb("hd_qvi", [128, 2048], BF16)
                vals_st = [sb(f"hd_v{i}", [128, 2048], BF16) for i in range(2)]
                b_s, b_mm, b_sa, b_vb, b_qst = (s_ld.v, mm.v, s_a.v, v_b.v,
                                                q_st.v)

                with nc.Block() as block:
                    @block.sync
                    def _(sync):
                        for h in range(16):
                            if h >= 2:
                                sync.wait_ge(mm.h, b_mm + 12 * (h - 1))
                            ins = sync.dma_start(
                                out=qh[h % 2][:],
                                in_=qnT_s[h * 128:(h + 1) * 128, :])
                            s_ld.inc(ins, 16)
                            ins = sync.dma_start(
                                out=kvh[h % 2][:],
                                in_=kvcT_s[h * 128:(h + 1) * 128, :])
                            s_ld.inc(ins, 16)

                    @block.tensor
                    def _(te):
                        for h in range(16):
                            te.wait_ge(s_ld.h, b_s + 32 * (h + 1))
                            if h >= 1:
                                te.wait_ge(v_b.h, b_vb + 4 * h)
                                te.wait_ge(s_a.h, b_sa + 12 * (h - 1) + 8)
                            g = 0
                            for srcb in (qh[h % 2], kvh[h % 2]):
                                for sp4 in range(4):
                                    pa = p[2 * (g % 2)]
                                    pb = p[2 * (g % 2) + 1]
                                    if g >= 2:
                                        te.wait_ge(s_a.h,
                                                   b_sa + 12 * h + g - 1)
                                    sl = srcb[:][:, sp4 * 512:(sp4 + 1) * 512]
                                    te.matmul(pa[:], kc["cf"][:], sl,
                                              start=True, stop=True)
                                    ins = te.matmul(pb[:], kc["sf"][:], sl,
                                                    start=True, stop=True)
                                    mm.inc(ins, 1)
                                    g += 1
                            if h >= 1:
                                te.wait_ge(s_a.h, b_sa + 12 * h)
                            for sp4 in range(4):
                                pv = p[4 + sp4 % 2]
                                te.wait_ge(v_b.h, b_vb + 4 * h + sp4 + 1)
                                if sp4 >= 2:
                                    te.wait_ge(s_a.h,
                                               b_sa + 12 * h + 8 + sp4 - 1)
                                te.matmul(pv[:], kc["ci"][:],
                                          qvr[:][:, sp4 * 512:(sp4 + 1) * 512],
                                          start=True, stop=False)
                                ins = te.matmul(
                                    pv[:], kc["si"][:],
                                    qvi[:][:, sp4 * 512:(sp4 + 1) * 512],
                                    start=False, stop=True)
                                mm.inc(ins, 1)

                    @block.scalar
                    def _(s):
                        for h in range(16):
                            dsts = [(qfr, qfi), (qfr, qfi), (qfr, qfi),
                                    (qfr, qfi), (kr, ki), (kr, ki), (kr, ki),
                                    (kr, ki)]
                            for g in range(8):
                                sp4 = g % 4
                                pa = p[2 * (g % 2)]
                                pb = p[2 * (g % 2) + 1]
                                s.wait_ge(mm.h, b_mm + 12 * h + g + 1)
                                if h >= 1 and g == 0:
                                    s.wait_ge(v_b.h, b_vb + 4 * h)
                                da, db = dsts[g]
                                s.activation(
                                    da[:][:, sp4 * 512:(sp4 + 1) * 512],
                                    pa[:], AF.Copy)
                                ins = s.activation(
                                    db[:][:, sp4 * 512:(sp4 + 1) * 512],
                                    pb[:], AF.Copy)
                                s_a.inc(ins, 1)
                            for sp4 in range(4):
                                pv = p[4 + sp4 % 2]
                                s.wait_ge(mm.h, b_mm + 12 * h + 8 + sp4 + 1)
                                if sp4 == 0 and h >= 1:
                                    s.wait_ge(q_st.h, b_qst + 16 * (h - 1))
                                ins = s.activation(
                                    vals_st[h % 2][:][:,
                                                      sp4 * 512:(sp4 + 1) * 512],
                                    pv[:], AF.Copy)
                                s_a.inc(ins, 1)
                            s.wait_ge(s_a.h, b_sa + 12 * h + 12)
                            ins = s.dma_start(
                                out=vals_s[:, :, h, :]
                                .transpose([1, 0, 2]),
                                in_=vals_st[h % 2][:].rearrange(
                                    "p (a n) -> p a n", n=512))
                            q_st.inc(ins, 16)
                        s.wait_ge(q_st.h, b_qst + 16 * 16)

                    @block.vector
                    def _(v):
                        for h in range(16):
                            for sp4 in range(4):
                                sl = slice(sp4 * 512, (sp4 + 1) * 512)
                                v.wait_ge(s_a.h, b_sa + 12 * h + 5 + sp4)
                                v.tensor_tensor(t1[:][:, sl], qfr[:][:, sl],
                                                kr[:][:, sl], ALU.mult)
                                v.tensor_tensor(u1[:][:, sl], qfi[:][:, sl],
                                                ki[:][:, sl], ALU.mult)
                                v.tensor_tensor(qvr[:][:, sl], t1[:][:, sl],
                                                u1[:][:, sl], ALU.subtract)
                                v.tensor_tensor(t1[:][:, sl], qfr[:][:, sl],
                                                ki[:][:, sl], ALU.mult)
                                v.tensor_tensor(u1[:][:, sl], qfi[:][:, sl],
                                                kr[:][:, sl], ALU.mult)
                                ins = v.tensor_tensor(qvi[:][:, sl],
                                                      t1[:][:, sl],
                                                      u1[:][:, sl], ALU.add)
                                v_b.inc(ins, 1)

        def emit_wo():
            with ExitStack() as es:
                uid_ctr[0] += 1
                _u = uid_ctr[0]
                sb = lambda n, s, d: es.enter_context(
                    nc.sbuf_tensor(f"{n}_{_u}", s, d))
                wo_sb = sb("wo_w", [128, 32768], BF16)
                vals_sb = [sb(f"wo_v{i}", [128, 8192], BF16) for i in range(2)]
                xres = [sb(f"wo_xr{i}", [128, 2048], F32) for i in range(2)]
                ystg = [sb(f"wo_ys{i}", [128, 8192], F32) for i in range(2)]
                b_g, b_s, b_mm, b_va, b_qld, b_qst = (g_ld.v, s_ld.v, mm.v,
                                                      v_a.v, q_ld.v, q_st.v)

                def xtile(t_ap, tb, dsp):
                    return t_ap[tb * 512:(tb + 1) * 512,
                                dsp * 512:(dsp + 1) * 512].rearrange(
                                    "(a p) n -> p a n", p=128)

                with nc.Block() as block:
                    @block.gpsimd
                    def _(gp):
                        src = woT.rearrange("(c p) n -> p c n", p=128)
                        ins = gp.dma_start(
                            out=wo_sb[:].rearrange("p (c n) -> p c n", n=2048),
                            in_=src)
                        g_ld.inc(ins, 16)

                    @block.sync
                    def _(sync):
                        for tb in range(4):
                            if tb >= 2:
                                sync.wait_ge(mm.h, b_mm + 4 * (tb - 1))
                            src = vals_s[tb]
                            ins = sync.dma_start(
                                out=vals_sb[tb % 2][:].rearrange(
                                    "p (c n) -> p c n", n=512), in_=src)
                            s_ld.inc(ins, 16)

                    @block.tensor
                    def _(te):
                        te.wait_ge(g_ld.h, b_g + 16)
                        for idx in range(16):
                            tb, dsp = divmod(idx, 4)
                            if dsp == 0:
                                te.wait_ge(s_ld.h, b_s + 16 * (tb + 1))
                            if idx >= 2:
                                te.wait_ge(v_a.h, b_va + idx - 1)
                            pd = [p[(idx % 2) * 4 + tt] for tt in range(4)]
                            vv = vals_sb[tb % 2][:].rearrange(
                                "p (c n) -> p c n", n=512)
                            for oc in range(16):
                                for tt in range(4):
                                    ins = te.matmul(
                                        pd[tt][:],
                                        vv[:, oc, tt * 128:(tt + 1) * 128],
                                        wo_sb[:][:, oc * 2048 + dsp * 512:
                                                 oc * 2048 + (dsp + 1) * 512],
                                        start=(oc == 0), stop=(oc == 15))
                            mm.inc(ins, 1)

                    @block.scalar
                    def _(s):
                        for idx in range(16):
                            tb, dsp = divmod(idx, 4)
                            if idx == 0:
                                ins = s.dma_start(
                                    out=xres[0][:].rearrange(
                                        "p (a n) -> p a n", n=512),
                                    in_=xtile(x1_s, 0, 0))
                                q_ld.inc(ins, 16)
                            if idx + 1 < 16:
                                tb2, dsp2 = divmod(idx + 1, 4)
                                if idx >= 1:
                                    s.wait_ge(v_a.h, b_va + idx)
                                ins = s.dma_start(
                                    out=xres[(idx + 1) % 2][:].rearrange(
                                        "p (a n) -> p a n", n=512),
                                    in_=xtile(x1_s, tb2, dsp2))
                                q_ld.inc(ins, 16)
                            if dsp == 3:
                                s.wait_ge(v_a.h, b_va + idx + 1)
                                ins = s.dma_start(
                                    out=x2_s[tb * 512:(tb + 1) * 512, :]
                                    .rearrange("(a p) n -> p a n", p=128),
                                    in_=ystg[tb % 2][:].rearrange(
                                        "p (a n) -> p a n", n=2048))
                                q_st.inc(ins, 16)
                        s.wait_ge(q_st.h, b_qst + 16 * 4)

                    @block.vector
                    def _(v):
                        for idx in range(16):
                            tbs, dss = divmod(idx, 4)
                            pd = [p[(idx % 2) * 4 + tt] for tt in range(4)]
                            v.wait_ge(mm.h, b_mm + idx + 1)
                            v.wait_ge(q_ld.h, b_qld + 16 * (idx + 1))
                            if dss == 0 and tbs >= 2:
                                v.wait_ge(q_st.h, b_qst + 16 * (tbs - 1))
                            for tt in range(4):
                                ins = v.scalar_tensor_tensor(
                                    ystg[tbs % 2][:][:,
                                                    tt * 2048 + dss * 512:
                                                    tt * 2048 +
                                                    (dss + 1) * 512],
                                    pd[tt][:], 1.0,
                                    xres[idx % 2][:][:,
                                                     tt * 512:(tt + 1) * 512],
                                    ALU.mult, ALU.add)
                            v_a.inc(ins, 1)
                            for a in range(4):
                                ysl = ystg[tbs % 2][:][:,
                                                      a * 2048 + dss * 512:
                                                      a * 2048 +
                                                      (dss + 1) * 512]
                                col = (4 * tbs + a) * 4 + dss
                                v.reduce_sum(sa_psum[:, col:col + 1], ysl,
                                             axis=AX.X)
                                v.tensor_tensor(sa_scr[:], ysl, ysl,
                                                ALU.mult)
                                v.reduce_sum(sa_psq[:, col:col + 1],
                                             sa_scr[:], axis=AX.X)

        def emit_gather():
            b_cc = g_ld.v
            with nc.Block() as block:
                @block.gpsimd
                def _(gp):
                    pairs = [
                        (wg_loc.rearrange("a p c n -> a (p c n)"),
                         wg_s.rearrange("a p c n -> a (p c n)")),
                        (wu_loc.rearrange("a p c n -> a (p c n)"),
                         wu_s.rearrange("a p c n -> a (p c n)")),
                        (wd_loc.rearrange("a h p n -> a (h p n)"),
                         wd_s.rearrange("a h p n -> a (h p n)")),
                    ]
                    for i, (src_ap, dst_ap) in enumerate(pairs):
                        ins = gp.collective_compute(
                            "AllGather", ALU.bypass,
                            replica_groups=[list(range(8))],
                            ins=[src_ap], outs=[dst_ap])
                        g_ld.inc(ins, 1)
                    gp.wait_ge(g_ld.h, b_cc + 3)

        emit_dequant()
        with ExitStack() as g:
            xT1 = g.enter_context(nc.sbuf_tensor("xT1", [128, 32768], BF16))
            emit_ln_T(x_in, xT1, "ffn")
            if debug:
                with nc.Block() as block:
                    @block.sync
                    def _(sync):
                        ins = sync.dma_start(out=xt_dbg, in_=xT1[:])
                        s_ld.inc(ins, 16)
                        sync.wait_ge(s_ld.h, s_ld.v)
            emit_ffn_a(xT1)
        emit_ffn_b(x_in, x1_s, final=False, stats=True)
        with ExitStack() as g:
            xTa = g.enter_context(nc.sbuf_tensor("xTa", [128, 32768], BF16))
            emit_ln_T(x1_s, xTa, "nat", prestats=True)
            emit_attn_proj(xTa)
        emit_cumsum()
        emit_heads()
        emit_wo()
        with ExitStack() as g:
            xT2 = g.enter_context(nc.sbuf_tensor("xT2", [128, 32768], BF16))
            emit_ln_T(x2_s, xT2, "ffn", prestats=True)
            emit_ffn_a(xT2)
        emit_ffn_b(x2_s, y_out, final=True)

    return nc


def kernel(L0_attn_x, L0_attn_W_q, L0_attn_W_kv, L0_attn_W_o,
           L0_ffn_gate_W_packed, L0_ffn_gate_scales,
           L0_ffn_up_W_packed, L0_ffn_up_scales,
           L0_ffn_down_W_packed, L0_ffn_down_scales):
    x = np.ascontiguousarray(np.asarray(L0_attn_x, dtype=np.float32))

    def pk_T(a):
        a = np.ascontiguousarray(np.asarray(a, dtype=np.int32))
        return np.ascontiguousarray(a.view(np.uint8)[:, ::4].T)

    def wT(a):
        return np.ascontiguousarray(np.asarray(a, dtype=np.float32).T)

    def w_tiled(a):
        t = wT(a).reshape(16, 128, 4, 512)
        return np.ascontiguousarray(t.transpose(2, 1, 0, 3))

    common = {
        "wqT": w_tiled(L0_attn_W_q),
        "wkvT": w_tiled(L0_attn_W_kv),
        "woT": wT(L0_attn_W_o),
    }
    gpkT = pk_T(L0_ffn_gate_W_packed)
    upkT = pk_T(L0_ffn_up_W_packed)
    dpkT = pk_T(L0_ffn_down_W_packed)
    gscT = wT(L0_ffn_gate_scales)
    uscT = wT(L0_ffn_up_scales)
    dscT = wT(L0_ffn_down_scales)
    nc = build_nc()
    in_maps = [dict(
        common, x=np.ascontiguousarray(x[b]),
        gpk=np.ascontiguousarray(gpkT[:, b * 2048:(b + 1) * 2048]),
        upk=np.ascontiguousarray(upkT[:, b * 2048:(b + 1) * 2048]),
        dpk=np.ascontiguousarray(dpkT[b * 1024:(b + 1) * 1024, :]),
        gsc=np.ascontiguousarray(gscT[:, b * 2048:(b + 1) * 2048]),
        usc=np.ascontiguousarray(uscT[:, b * 2048:(b + 1) * 2048]),
        dsc=np.ascontiguousarray(dscT[b * 64:(b + 1) * 64, :]),
    ) for b in range(8)]
    res = run_bass_kernel_spmd(nc, in_maps, core_ids=list(range(8)))
    return np.stack([np.asarray(res.results[b]["y"]) for b in range(8)],
                    axis=0)
